# version 07
# baseline (speedup 1.0000x reference)
import numpy as np
import ml_dtypes
import concourse.bass as bass
import concourse.mybir as mybir
from concourse.bass_utils import run_bass_kernel_spmd
from concourse.bass_core_rust import IndirectOffsetOnAxis
from contextlib import ExitStack

F32 = mybir.dt.float32
BF16 = mybir.dt.bfloat16
I32 = mybir.dt.int32
ALU = mybir.AluOpType
AF = mybir.ActivationFunctionType
AX = mybir.AxisListType.X
BF = ml_dtypes.bfloat16

T, DM, DI, DS, DTR = 4096, 1024, 2048, 16, 64
NE, FF, CAP = 8, 3584, 1280
NJ = DI // 128
NKD = DM // 128
NF = FF // 128
TC = 1024
NCH = T // TC
NCT = 32
NCAP = CAP // 128
EPS = 1e-5

NCORES = 8
DEBUG_TAPS = False


def trunc_bf16(a):
    a = np.ascontiguousarray(a, dtype=np.float32)
    u = a.view(np.uint16)
    hi = np.ascontiguousarray(u.reshape(-1, 2)[:, 1])
    return hi.reshape(a.shape).view(BF)


ENGINES = ("pe", "dve", "act", "pool", "sync")
SEM_OF = {"pe": "sP", "dve": "sV", "act": "sA", "pool": "sGc", "sync": "sS"}
DMA_POOL = {"sync": ["sSd%d" % i for i in range(12)],
            "pool": ["sGd%d" % i for i in range(8)],
            "act": ["sAd0"]}
ALL_SEMS = sorted(set(SEM_OF.values()) | {s for v in DMA_POOL.values() for s in v})


class Buf:
    __slots__ = ("name", "w", "r")

    def __init__(self, name):
        self.name = name
        self.w = None
        self.r = []


class Sched:
    def __init__(self):
        self.streams = {e: [] for e in ENGINES}
        self.cnt = {s: 0 for s in ALL_SEMS}
        self.waited = {e: {} for e in ENGINES}
        self.rr = {q: 0 for q in DMA_POOL}

    def _wait(self, eng, waits, sem, val):
        seen = self.waited[eng]
        if seen.get(sem, 0) >= val:
            return
        seen[sem] = val
        waits.append((sem, val))

    def emit(self, eng, body, reads=(), writes=(), dma=False, selfwait=False, ndma=1):
        waits = []
        for b in reads:
            if b.w is not None and b.w[0] != eng:
                self._wait(eng, waits, b.w[1], b.w[2])
        for b in writes:
            if b.w is not None and b.w[0] != eng:
                self._wait(eng, waits, b.w[1], b.w[2])
            for (reng, rsem, rval) in b.r:
                if reng != eng:
                    self._wait(eng, waits, rsem, rval)
        if dma:
            pool = DMA_POOL[eng]
            sem = pool[self.rr[eng] % len(pool)]
            self.rr[eng] += 1
            if self.cnt[sem] > 0:
                self._wait(eng, waits, sem, self.cnt[sem])
            inc = 16 * ndma
        else:
            sem = SEM_OF[eng]
            inc = 1
        self.cnt[sem] += inc
        val = self.cnt[sem]
        self.streams[eng].append((waits, body, sem, inc, selfwait, val))
        ev = (eng, sem, val)
        for b in reads:
            b.r.append(ev)
        for b in writes:
            b.w = ev
            b.r = []
        return ev

    def barrier(self):
        snapshot = dict(self.cnt)
        for e in ENGINES:
            waits = []
            for sem, val in snapshot.items():
                if val > 0:
                    self._wait(e, waits, sem, val)
            if waits:
                self.streams[e].append((waits, None, None, 0, False, 0))


def build_program():
    nc = bass.Bass("TRN2", target_bir_lowering=False, debug=False,
                   detect_race_conditions=False)
    S = Sched()
    es = ExitStack()

    def din(name, shape, dt):
        return nc.dram_tensor(name, list(shape), dt, kind="ExternalInput").ap()

    def dscratch(name, shape, dt):
        kind = "ExternalOutput" if DEBUG_TAPS else "Internal"
        return nc.dram_tensor(name, list(shape), dt, kind=kind).ap()

    x_hbm = din("x", (T, DM), F32)
    w_inproj = din("w_inproj", (32, 128, 8 * 128), BF16)
    b_inproj = din("b_inproj", (128, 32), F32)
    w_xproj1 = din("w_xproj1", (128, 16 * 80), BF16)
    w_xprojC = din("w_xprojC", (128, 16 * 16), BF16)
    w_dtproj = din("w_dtproj", (64, DI), BF16)
    b_dtproj = din("b_dtproj", (128, NJ), F32)
    conv_w4 = din("conv_w4", (128, NJ * 4), F32)
    conv_bc = din("conv_bc", (128, NJ), F32)
    A_logr = din("A_logr", (128, NJ * DS), F32)
    D_col = din("D_col", (128, NJ), F32)
    w_outproj = din("w_outproj", (8, 128, 16 * 128), BF16)
    b_outproj = din("b_outproj", (128, 8), F32)
    gate_wT = din("gate_wT", (128, NKD * NE), F32)
    w1_dev = din("w1_dev", (NE, NF, 128, 8 * 128), BF16)
    w3_dev = din("w3_dev", (NE, NF, 128, 8 * 128), BF16)
    w2_dev = din("w2_dev", (NE, NF, 128, DM), BF16)
    ident_f_d = din("ident_f", (128, 128), F32)
    ident_b_d = din("ident_b", (128, 128), BF16)
    LT_d = din("LT", (128, 128), F32)
    BSEL_d = din("BSEL", (16, DS * 128), BF16)

    xn_tok_mir = dscratch("xn_tok_mir", (T, DM), BF16)
    xn_dmaj_mir = dscratch("xn_dmaj_mir", (DM, T), BF16)
    yg_mir = dscratch("yg_mir", (NCH, NJ, 128, TC), BF16)
    disp_dram = dscratch("disp_dram", (NE * CAP, DM), BF16)
    g_dram = dscratch("g_dram", (NE, NF, 128, CAP), BF16)
    eout_dram = dscratch("eout_dram", (NE * CAP, DM), BF16)
    out_hbm = nc.dram_tensor("out", [T, DM], F32, kind="ExternalOutput").ap()

    B_xn_tok_mir = Buf("xn_tok_mir")
    B_xn_dmaj_mir = Buf("xn_dmaj_mir")
    B_yg_mir = Buf("yg_mir")
    B_disp_dram = Buf("disp_dram")
    B_g_dram = Buf("g_dram")
    B_eout_dram = Buf("eout_dram")
    B_out_hbm = Buf("out_hbm")

    sems = {}
    for sname in ALL_SEMS:
        sems[sname] = es.enter_context(nc.semaphore(sname))

    def sb(name, shape, dt, stack=None):
        t = (stack or es).enter_context(nc.sbuf_tensor(name, list(shape), dt))
        return t, Buf(name)

    def psum(name, shape, dt, stack=None):
        t = (stack or es).enter_context(nc.psum_tensor(name, list(shape), dt))
        return t, Buf(name)

    identf, B_identf = sb("identf", (128, 128), F32)
    identb, B_identb = sb("identb", (128, 128), BF16)

    BSEL_sb, B_BSEL = sb("BSEL_sb", (16, DS * 128), BF16)
    A_sb, B_A = sb("A_sb", (128, NJ * DS), F32)
    invA_sb, B_invA = sb("invA_sb", (128, NJ * DS), F32)
    bin_sb, B_bin = sb("bin_sb", (128, 32), F32)
    bdt_sb, B_bdt = sb("bdt_sb", (128, NJ), F32)
    cw_sb, B_cw = sb("cw_sb", (128, NJ * 4), F32)
    cb_sb, B_cb = sb("cb_sb", (128, NJ), F32)
    Dc_sb, B_Dc = sb("Dc_sb", (128, NJ), F32)
    bout_sb, B_bout = sb("bout_sb", (128, 8), F32)
    wxp1_sb, B_wxp1 = sb("wxp1_sb", (128, 16 * 80), BF16)
    wxpC_sb, B_wxpC = sb("wxpC_sb", (128, 16 * 16), BF16)
    wdt_sb, B_wdt = sb("wdt_sb", (64, DI), BF16)
    dsl1_sb, B_dsl1 = sb("dsl1", (128, 32), I32)
    dsl2_sb, B_dsl2 = sb("dsl2", (128, 32), I32)
    pm1_sb, B_pm1 = sb("pm1", (128, 32), F32)
    pm2_sb, B_pm2 = sb("pm2", (128, 32), F32)
    gstate, B_gstate = sb("gstate", (128, NJ * DS), F32)
    xinst, B_xinst = sb("xinst", (128, NJ * 4), BF16)

    cst_sb, B_cst = sb("cst_sb", (128, 8), F32)
    CONSTS = {EPS: 0, 1.0: 1, -0.5: 2, -1.0: 3, 0.0: 4}
    for v, i in CONSTS.items():
        nc.const_aps.register(F32, v, cst_sb[:, i:i + 1])

    def _fillconsts(eng):
        last = None
        for v, i in CONSTS.items():
            last = eng.memset(cst_sb[:, i:i + 1], float(v))
        return last
    S.emit("dve", _fillconsts, writes=[B_cst])

    BCREG = {}

    def _mkbc(eng):
        r = eng.alloc_register("bcreg")
        BCREG["r"] = r
        return eng.reg_mov(r, NE * CAP - 1)
    S.emit("pool", _mkbc)
    S.barrier()

    PS = []
    B_PS = []
    for i in range(7):
        t, b = psum("ps%d" % i, (128, 512), F32)
        PS.append(t)
        B_PS.append(b)
    PSB, B_PSB = psum("psb", (128, 512), BF16)

    p0 = ExitStack()
    LT_sb, B_LT = sb("LT_sb", (128, 128), F32, p0)
    gw_sb, B_gw = sb("gw_sb", (128, NKD * NE), F32, p0)
    scr0, B_scr0 = sb("scr0", (128, NJ * DS), F32, p0)

    def ldconst(dst, src, bufs):
        S.emit("sync", (lambda d=dst, s=src: (lambda eng: [eng.dma_start(out=d, in_=s)]))(),
               writes=bufs, dma=True)

    ldconst(identf[:], ident_f_d, [B_identf])
    ldconst(identb[:], ident_b_d, [B_identb])
    ldconst(LT_sb[:], LT_d, [B_LT])
    ldconst(BSEL_sb[:], BSEL_d, [B_BSEL])
    ldconst(A_sb[:], A_logr, [B_A])
    ldconst(bin_sb[:], b_inproj, [B_bin])
    ldconst(bdt_sb[:], b_dtproj, [B_bdt])
    ldconst(cw_sb[:], conv_w4, [B_cw])
    ldconst(cb_sb[:], conv_bc, [B_cb])
    ldconst(Dc_sb[:], D_col, [B_Dc])
    ldconst(bout_sb[:], b_outproj, [B_bout])
    ldconst(gw_sb[:], gate_wT, [B_gw])
    ldconst(wxp1_sb[:], w_xproj1, [B_wxp1])
    ldconst(wxpC_sb[:], w_xprojC, [B_wxpC])
    ldconst(wdt_sb[:], w_dtproj, [B_wdt])


    def _actwarm(eng):
        last = None
        for fn in (AF.Square, AF.Ln, AF.Exp, AF.Silu, AF.Sigmoid, AF.Ln, AF.Exp):
            last = eng.activation(out=cst_sb[:, 5:6], in_=cst_sb[:, 1:2], func=fn)
        last = eng.sign(cst_sb[:, 5:6], cst_sb[:, 1:2])
        return last
    S.emit("act", _actwarm, reads=[B_cst], writes=[B_cst])

    S.emit("act", lambda eng: [eng.activation(out=scr0[:], in_=A_sb[:], func=AF.Exp)],
           reads=[B_A], writes=[B_scr0], selfwait=True)
    S.emit("dve", lambda eng: [eng.tensor_scalar(out=A_sb[:], in0=scr0[:], scalar1=-1.0,
                                                 scalar2=None, op0=ALU.mult)],
           reads=[B_scr0], writes=[B_A], selfwait=True)
    S.emit("dve", lambda eng: [eng.tensor_single_scalar(invA_sb[:], scr0[:], 1e-6, ALU.is_lt)],
           reads=[B_scr0], writes=[B_invA], selfwait=True)
    S.emit("dve", lambda eng: [eng.tensor_scalar(out=scr0[:], in0=A_sb[:], scalar1=-1.0,
                                                 scalar2=1.0, op0=ALU.mult, op1=ALU.add)],
           reads=[B_A], writes=[B_scr0], selfwait=True)
    S.emit("dve", lambda eng: [eng.tensor_mul(scr0[:], scr0[:], invA_sb[:])],
           reads=[B_invA], writes=[B_scr0], selfwait=True)
    S.emit("dve", lambda eng: [eng.tensor_add(scr0[:], scr0[:], A_sb[:])],
           reads=[B_A], writes=[B_scr0], selfwait=True)
    S.emit("dve", lambda eng: [eng.reciprocal(invA_sb[:], scr0[:])],
           reads=[B_scr0], writes=[B_invA], selfwait=True)
    S.barrier()

    p1 = ExitStack()
    xnd, B_xnd = sb("xnd", (128, NKD * T), BF16, p1)
    xt, B_xt = sb("xt", (128, DM), F32, p1)
    sq, B_sq = sb("sq", (128, DM), F32, p1)
    mcol, B_mcol = sb("mcol", (128, 8), F32, p1)
    xnf, B_xnf = sb("xnf", (128, DM), F32, p1)
    xnb, B_xnb = sb("xnb", (128, DM), BF16, p1)
    xnTf, B_xnTf = sb("xnTf", (128, NKD * 128), F32, p1)
    LG, B_LG = sb("LG", (128, NCT * NE), F32, p1)
    if DEBUG_TAPS:
        mcol_tap = dscratch("mcol_tap", (128, 8), F32)
        xt_tap = dscratch("xt_tap", (128, DM), F32)
        B_mcol_tap, B_xt_tap = Buf("mcol_tap"), Buf("xt_tap")
        dsl_tap = dscratch("dsl_tap", (128, 64), I32)
        pm_tap = dscratch("pm_tap", (128, 64), F32)
        B_dsl_tap, B_pm_tap = Buf("dsl_tap"), Buf("pm_tap")

    for c in range(NCT):
        xrows = x_hbm.rearrange("(p c) d -> c p d", c=NCT)[c]
        S.emit("sync", (lambda xr=xrows: (lambda eng: [eng.dma_start(out=xt[:], in_=xr)]))(),
               reads=[], writes=[B_xt], dma=True, selfwait=True)
        S.emit("act", lambda eng: [eng.activation(out=sq[:], in_=xt[:], func=AF.Square)],
               reads=[B_xt], writes=[B_sq], selfwait=True)
        S.emit("dve", lambda eng: [eng.reduce_sum(mcol[:, 0:1], sq[:], axis=AX)],
               reads=[B_sq], writes=[B_mcol], selfwait=True)
        S.emit("dve", lambda eng: [eng.reduce_sum(mcol[:, 1:2], xt[:], axis=AX)],
               reads=[B_xt], writes=[B_mcol], selfwait=True)
        def _stats1(eng):
            eng.tensor_scalar(out=mcol[:, 1:2], in0=mcol[:, 1:2], scalar1=1.0 / DM,
                              scalar2=None, op0=ALU.mult)
            return eng.tensor_scalar(out=mcol[:, 2:3], in0=mcol[:, 0:1], scalar1=1.0 / DM,
                                     scalar2=None, op0=ALU.mult)
        S.emit("dve", lambda eng: [_stats1(eng)], reads=[], writes=[B_mcol], selfwait=True)
        S.emit("dve", lambda eng: [eng.tensor_mul(mcol[:, 3:4], mcol[:, 1:2], mcol[:, 1:2])],
               reads=[], writes=[B_mcol], selfwait=True)
        S.emit("dve", lambda eng: [eng.tensor_sub(mcol[:, 2:3], mcol[:, 2:3], mcol[:, 3:4])],
               reads=[], writes=[B_mcol], selfwait=True)
        S.emit("act", lambda eng: [eng.activation(out=mcol[:, 4:5], in_=mcol[:, 2:3],
                                                  func=AF.Ln, bias=EPS)],
               reads=[B_mcol], writes=[B_mcol], selfwait=True)
        S.emit("act", lambda eng: [eng.activation(out=mcol[:, 5:6], in_=mcol[:, 4:5],
                                                  func=AF.Exp, scale=-0.5)],
               reads=[B_mcol], writes=[B_mcol], selfwait=True)
        if DEBUG_TAPS and c == 0:
            S.emit("sync", lambda eng: [eng.dma_start(out=mcol_tap, in_=mcol[:])],
                   reads=[B_mcol], writes=[B_mcol_tap], dma=True)
            S.emit("sync", lambda eng: [eng.dma_start(out=xt_tap, in_=xt[:])],
                   reads=[B_xt], writes=[B_xt_tap], dma=True)
        S.emit("dve", lambda eng: [eng.tensor_scalar(out=xnf[:], in0=xt[:],
                                                     scalar1=mcol[:, 1:2], scalar2=mcol[:, 5:6],
                                                     op0=ALU.subtract, op1=ALU.mult)],
               reads=[B_xt, B_mcol], writes=[B_xnf], selfwait=True)
        S.emit("dve", lambda eng: [eng.tensor_copy(xnb[:], xnf[:])],
               reads=[B_xnf], writes=[B_xnb], selfwait=True)
        mrows = xn_tok_mir.rearrange("(p c) d -> c p d", c=NCT)[c]
        S.emit("sync", (lambda mr=mrows: (lambda eng: [eng.dma_start(out=mr, in_=xnb[:])]))(),
               reads=[B_xnb], writes=[B_xn_tok_mir], dma=True, selfwait=True)
        for kd in range(NKD):
            pk = kd % 2
            S.emit("pe", (lambda kd=kd, pk=pk: (lambda eng: [
                eng.transpose(PS[pk][:, 0:128], xnf[:, kd * 128:(kd + 1) * 128], identf[:])
            ]))(), reads=[B_xnf, B_identf], writes=[B_PS[pk]], selfwait=True)
            def _evac(eng, kd=kd, pk=pk, c=c):
                eng.tensor_copy(xnTf[:, kd * 128:(kd + 1) * 128], PS[pk][:, 0:128])
                dst = xnd[:].rearrange("p (k t) -> p k t", k=NKD)[:, kd]
                dst = bass.AP(dst.tensor, dst.offset + c, [dst.ap[0], [NCT, 128]])
                return eng.tensor_copy(dst, PS[pk][:, 0:128])
            S.emit("dve", (lambda f=_evac: (lambda eng: [f(eng)]))(),
                   reads=[B_PS[pk]], writes=[B_xnTf, B_xnd], selfwait=True)
        def _router(eng, c=c):
            last = None
            for kd in range(NKD):
                last = eng.matmul(PS[2][:, 0:8], xnTf[:, kd * 128:(kd + 1) * 128],
                                  gw_sb[:, kd * NE:(kd + 1) * NE],
                                  start=(kd == 0), stop=(kd == NKD - 1))
            return last
        S.emit("pe", (lambda f=_router: (lambda eng: [f(eng)]))(),
               reads=[B_xnTf, B_gw], writes=[B_PS[2]], selfwait=True)
        S.emit("dve", (lambda c=c: (lambda eng: [
            eng.tensor_copy(LG[:, c * NE:(c + 1) * NE], PS[2][:, 0:8])]))(),
               reads=[B_PS[2]], writes=[B_LG], selfwait=True)

    def _stmir(eng):
        ins = []
        for kd in range(NKD):
            ins.append(eng.dma_start(out=xn_dmaj_mir[kd * 128:(kd + 1) * 128, :],
                                     in_=xnd[:, kd * T:(kd + 1) * T]))
        return ins
    S.emit("sync", lambda eng: _stmir(eng), reads=[B_xnd], writes=[B_xn_dmaj_mir],
           dma=True, ndma=NKD)

    LG3 = LG[:].rearrange("p (c e) -> p c e", e=NE)
    rt = ExitStack()
    m1, B_m1 = sb("m1", (128, NCT), F32, rt)
    nm1, B_nm1 = sb("nm1", (128, NCT), F32, rt)
    ee, B_ee = sb("ee", (128, NCT * NE), F32, rt)
    ssum, B_ssum = sb("ssum", (128, NCT), F32, rt)
    p1c, B_p1c = sb("p1c", (128, NCT), F32, rt)
    d3, B_d3 = sb("d3", (128, NCT * NE), F32, rt)
    sg, B_sg = sb("sgn", (128, NCT * NE), F32, rt)
    oh, B_oh = sb("oh", (128, NCT * NE), F32, rt)
    cand, B_cand = sb("cand", (128, NCT * NE), F32, rt)
    idx1, B_idx1 = sb("idx1", (128, NCT), F32, rt)
    idx2, B_idx2 = sb("idx2", (128, NCT), F32, rt)
    ohx1, B_ohx1 = sb("ohx1", (128, NCT * NE), F32, rt)
    ohx2, B_ohx2 = sb("ohx2", (128, NCT * NE), F32, rt)
    l2t, B_l2t = sb("l2t", (128, NCT * NE), F32, rt)
    m2, B_m2 = sb("m2", (128, NCT), F32, rt)
    ohseq, B_ohseq = sb("ohseq", (128, NE * 64), F32, rt)
    zer64, B_zer64 = sb("zer64", (128, 64), F32, rt)
    Spe, B_Spe = sb("Spe", (128, NE), F32, rt)
    P0sb, B_P0sb = sb("P0sb", (128, NE), F32, rt)
    posi, B_posi = sb("posi", (128, NE * 64), F32, rt)
    qtmp, B_qtmp = sb("qtmp", (128, NCT * NE), F32, rt)
    pos1, B_pos1 = sb("pos1", (128, NCT), F32, rt)
    pos2, B_pos2 = sb("pos2", (128, NCT), F32, rt)
    keep, B_keep = sb("keep", (128, NCT), F32, rt)
    dslf, B_dslf = sb("dslf", (128, NCT), F32, rt)
    xnsb, B_xnsb = sb("xnsb", (128, DM), BF16, rt)

    d33 = d3[:].rearrange("p (c e) -> p c e", e=NE)
    oh3 = oh[:].rearrange("p (c e) -> p c e", e=NE)
    cand3 = cand[:].rearrange("p (c e) -> p c e", e=NE)
    ohx13 = ohx1[:].rearrange("p (c e) -> p c e", e=NE)
    ohx23 = ohx2[:].rearrange("p (c e) -> p c e", e=NE)
    l23 = l2t[:].rearrange("p (c e) -> p c e", e=NE)
    ee3 = ee[:].rearrange("p (c e) -> p c e", e=NE)
    qt3 = qtmp[:].rearrange("p (c e) -> p c e", e=NE)
    oq2 = ohseq[:].rearrange("p (e s) -> p e s", s=64)
    po2 = posi[:].rearrange("p (e s) -> p e s", s=64)

    S.emit("dve", lambda eng: [eng.reduce_max(m1[:], LG3, axis=AX)],
           reads=[B_LG], writes=[B_m1], selfwait=True)
    S.emit("dve", lambda eng: [eng.reduce_max(nm1[:], LG3, axis=AX, negate=True)],
           reads=[B_LG], writes=[B_nm1], selfwait=True)

    def _subm(eng, src3, mm, B=None):
        last = None
        for e in range(NE):
            last = eng.tensor_sub(d33[:, :, e], src3[:, :, e], mm[:])
        return last
    S.emit("dve", lambda eng: [_subm(eng, LG3, m1)], reads=[B_LG, B_m1], writes=[B_d3])
    S.emit("act", lambda eng: [eng.activation(out=ee[:], in_=d3[:], func=AF.Exp)],
           reads=[B_d3], writes=[B_ee])
    S.emit("dve", lambda eng: [eng.reduce_sum(ssum[:], ee3, axis=AX)],
           reads=[B_ee], writes=[B_ssum], selfwait=True)
    S.emit("dve", lambda eng: [eng.reciprocal(p1c[:], ssum[:])],
           reads=[B_ssum], writes=[B_p1c], selfwait=True)
    S.emit("act", lambda eng: [eng.sign(sg[:], d3[:])], reads=[B_d3], writes=[B_sg])
    S.emit("dve", lambda eng: [eng.tensor_scalar(out=oh[:], in0=sg[:], scalar1=1.0,
                                                 scalar2=None, op0=ALU.add)],
           reads=[B_sg], writes=[B_oh], selfwait=True)

    def _cand(eng):
        last = None
        for e in range(NE):
            last = eng.tensor_scalar(out=cand3[:, :, e], in0=oh3[:, :, e],
                                     scalar1=float(e - 99), scalar2=99.0,
                                     op0=ALU.mult, op1=ALU.add)
        return last
    S.emit("dve", lambda eng: [_cand(eng)], reads=[B_oh], writes=[B_cand], selfwait=True)
    S.emit("dve", lambda eng: [eng.tensor_reduce(idx1[:], cand3, axis=AX, op=ALU.min)],
           reads=[B_cand], writes=[B_idx1], selfwait=True)

    def _ohx(eng, idx, dst3):
        last = None
        for e in range(NE):
            last = eng.tensor_scalar(out=dst3[:, :, e], in0=idx[:], scalar1=float(-e),
                                     scalar2=None, op0=ALU.add)
        return last
    S.emit("dve", lambda eng: [_ohx(eng, idx1, ohx13)], reads=[B_idx1], writes=[B_ohx1], selfwait=True)
    S.emit("act", lambda eng: [eng.square(ohx1[:], ohx1[:])], reads=[B_ohx1], writes=[B_ohx1])
    S.emit("dve", lambda eng: [eng.tensor_single_scalar(ohx1[:], ohx1[:], 0.5, ALU.is_lt)],
           reads=[B_ohx1], writes=[B_ohx1], selfwait=True)
    S.emit("dve", lambda eng: [eng.scalar_tensor_tensor(out=l2t[:], in0=ohx1[:],
                                                        scalar=-1e30, in1=LG[:],
                                                        op0=ALU.mult, op1=ALU.add)],
           reads=[B_ohx1, B_LG], writes=[B_l2t], selfwait=True)
    S.emit("dve", lambda eng: [eng.reduce_max(m2[:], l23, axis=AX)],
           reads=[B_l2t], writes=[B_m2], selfwait=True)
    S.emit("dve", lambda eng: [_subm(eng, l23, m2)], reads=[B_l2t, B_m2], writes=[B_d3])
    S.emit("act", lambda eng: [eng.sign(sg[:], d3[:])], reads=[B_d3], writes=[B_sg])
    S.emit("dve", lambda eng: [eng.tensor_scalar(out=oh[:], in0=sg[:], scalar1=1.0,
                                                 scalar2=None, op0=ALU.add)],
           reads=[B_sg], writes=[B_oh], selfwait=True)
    S.emit("dve", lambda eng: [_cand(eng)], reads=[B_oh], writes=[B_cand], selfwait=True)
    S.emit("dve", lambda eng: [eng.tensor_reduce(idx2[:], cand3, axis=AX, op=ALU.min)],
           reads=[B_cand], writes=[B_idx2], selfwait=True)
    S.emit("dve", lambda eng: [_ohx(eng, idx2, ohx23)], reads=[B_idx2], writes=[B_ohx2], selfwait=True)
    S.emit("act", lambda eng: [eng.square(ohx2[:], ohx2[:])], reads=[B_ohx2], writes=[B_ohx2])
    S.emit("dve", lambda eng: [eng.tensor_single_scalar(ohx2[:], ohx2[:], 0.5, ALU.is_lt)],
           reads=[B_ohx2], writes=[B_ohx2], selfwait=True)
    S.emit("dve", lambda eng: [eng.tensor_sub(m2[:], m2[:], m1[:])],
           reads=[B_m1], writes=[B_m2], selfwait=True)
    S.emit("act", lambda eng: [eng.activation(out=m2[:], in_=m2[:], func=AF.Exp)],
           reads=[B_m2], writes=[B_m2])
    S.emit("dve", lambda eng: [eng.tensor_mul(m2[:], m2[:], p1c[:])],
           reads=[B_p1c], writes=[B_m2], selfwait=True)
    def _ilv(eng):
        last = None
        for e in range(NE):
            src1 = ohx13[:, :, e]
            src2 = ohx23[:, :, e]
            d1 = bass.AP(oq2.tensor, oq2.offset + e * 64, [oq2.ap[0], [2, 32]])
            d2 = bass.AP(oq2.tensor, oq2.offset + e * 64 + 1, [oq2.ap[0], [2, 32]])
            eng.tensor_copy(d1, src1)
            last = eng.tensor_copy(d2, src2)
        return last
    S.emit("dve", lambda eng: [_ilv(eng)], reads=[B_ohx1, B_ohx2], writes=[B_ohseq], selfwait=True)
    S.emit("dve", lambda eng: [eng.memset(zer64[:], 0.0)], writes=[B_zer64], selfwait=True)
    S.emit("dve", lambda eng: [eng.reduce_sum(Spe[:], oq2, axis=AX)],
           reads=[B_ohseq], writes=[B_Spe], selfwait=True)
    S.emit("pe", lambda eng: [eng.matmul(PS[2][:, 0:8], LT_sb[:], Spe[:],
                                         start=True, stop=True)],
           reads=[B_LT, B_Spe], writes=[B_PS[2]])
    S.emit("dve", lambda eng: [eng.tensor_copy(P0sb[:], PS[2][:, 0:8])],
           reads=[B_PS[2]], writes=[B_P0sb], selfwait=True)
    for e in range(NE):
        S.emit("dve", (lambda e=e: (lambda eng: [
            eng.tensor_tensor_scan(out=po2[:, e], data0=oq2[:, e], data1=zer64[:],
                                   initial=P0sb[:, e:e + 1], op0=ALU.add, op1=ALU.add)
        ]))(), reads=[B_ohseq, B_zer64, B_P0sb], writes=[B_posi], selfwait=True)

    def _posk(eng, which, dst):
        last = None
        for e in range(NE):
            srcpos = bass.AP(po2.tensor, po2.offset + e * 64 + which, [po2.ap[0], [2, 32]])
            src_oh = (ohx13 if which == 0 else ohx23)[:, :, e]
            last = eng.tensor_mul(qt3[:, :, e], src_oh, srcpos)
        return last
    S.emit("dve", lambda eng: [_posk(eng, 0, pos1)], reads=[B_posi, B_ohx1], writes=[B_qtmp], selfwait=True)
    S.emit("dve", lambda eng: [eng.reduce_sum(pos1[:], qt3, axis=AX)],
           reads=[B_qtmp], writes=[B_pos1], selfwait=True)
    S.emit("dve", lambda eng: [_posk(eng, 1, pos2)], reads=[B_posi, B_ohx2], writes=[B_qtmp], selfwait=True)
    S.emit("dve", lambda eng: [eng.reduce_sum(pos2[:], qt3, axis=AX)],
           reads=[B_qtmp], writes=[B_pos2], selfwait=True)

    def _mkdsl(idxk, posk, pk_src, B_idxk, B_posk, B_pk_src, dsl_i, pm_dst, B_dsl_i, B_pm):
        S.emit("dve", lambda eng: [eng.tensor_single_scalar(keep[:], posk[:],
                                                            float(CAP) + 0.5, ALU.is_lt)],
               reads=[B_posk], writes=[B_keep], selfwait=True)
        S.emit("dve", lambda eng: [eng.tensor_mul(pm_dst[:], pk_src[:], keep[:])],
               reads=[B_pk_src, B_keep], writes=[B_pm], selfwait=True)
        S.emit("dve", lambda eng: [eng.tensor_scalar(out=dslf[:], in0=idxk[:],
                                                     scalar1=float(CAP), scalar2=-1.0,
                                                     op0=ALU.mult, op1=ALU.add)],
               reads=[B_idxk], writes=[B_dslf], selfwait=True)
        S.emit("dve", lambda eng: [eng.tensor_add(dslf[:], dslf[:], posk[:])],
               reads=[B_posk], writes=[B_dslf], selfwait=True)
        S.emit("dve", lambda eng: [eng.tensor_mul(dslf[:], dslf[:], keep[:])],
               reads=[B_keep], writes=[B_dslf], selfwait=True)
        S.emit("dve", lambda eng: [eng.tensor_scalar(out=qtmp[:, 0:NCT], in0=keep[:],
                                                     scalar1=-2e6, scalar2=2e6,
                                                     op0=ALU.mult, op1=ALU.add)],
               reads=[B_keep], writes=[B_qtmp], selfwait=True)
        S.emit("dve", lambda eng: [eng.tensor_add(dslf[:], dslf[:], qtmp[:, 0:NCT])],
               reads=[B_qtmp], writes=[B_dslf], selfwait=True)
        S.emit("dve", lambda eng: [eng.tensor_copy(dsl_i[:], dslf[:])],
               reads=[B_dslf], writes=[B_dsl_i], selfwait=True)

    _mkdsl(idx1, pos1, p1c, B_idx1, B_pos1, B_p1c, dsl1_sb, pm1_sb, B_dsl1, B_pm1)
    _mkdsl(idx2, pos2, m2, B_idx2, B_pos2, B_m2, dsl2_sb, pm2_sb, B_dsl2, B_pm2)

    if DEBUG_TAPS:
        def _dsltap(eng):
            return [eng.dma_start(out=dsl_tap[:, 0:32], in_=dsl1_sb[:]),
                    eng.dma_start(out=dsl_tap[:, 32:64], in_=dsl2_sb[:]),
                    eng.dma_start(out=pm_tap[:, 0:32], in_=pm1_sb[:]),
                    eng.dma_start(out=pm_tap[:, 32:64], in_=pm2_sb[:])]
        S.emit("sync", _dsltap, reads=[B_dsl1, B_dsl2, B_pm1, B_pm2],
               writes=[B_dsl_tap, B_pm_tap], dma=True, ndma=4)

    for c in range(NCT):
        mrows = xn_tok_mir.rearrange("(p c) d -> c p d", c=NCT)[c]
        S.emit("sync", (lambda mr=mrows: (lambda eng: [eng.dma_start(out=xnsb[:], in_=mr)]))(),
               reads=[B_xn_tok_mir], writes=[B_xnsb], dma=True)
        for (dsl_i, B_dsl_i) in ((dsl1_sb, B_dsl1), (dsl2_sb, B_dsl2)):
            S.emit("pool", (lambda c=c, dsl_i=dsl_i: (lambda eng: [
                eng.indirect_dma_start(
                    out=disp_dram, out_offset=IndirectOffsetOnAxis(ap=dsl_i[:, c:c + 1], axis=0),
                    in_=xnsb[:], in_offset=None,
                    bounds_check=BCREG["r"], oob_is_err=False)
            ]))(), reads=[B_xnsb, B_dsl_i], writes=[B_disp_dram], dma=True)

    rt.close()
    p1.close()
    p0.close()
    S.barrier()

    p2 = ExitStack()
    xnch, B_xnch = sb("xnch", (128, NKD * TC), BF16, p2)
    wslab = []
    B_wslab = []
    for i in range(2):
        t, b = sb("wslab%d" % i, (128, 8 * 128), BF16, p2)
        wslab.append(t)
        B_wslab.append(b)
    xc, B_xc = sb("xc", (128, NJ * TC), BF16, p2)
    Ball, B_Ball = sb("Ball", (128, DS * TC), BF16, p2)
    Call, B_Call = sb("Call", (128, DS * TC), BF16, p2)
    dtlow, B_dtlow = sb("dtlow", (64, TC), BF16, p2)
    Brow, B_Brow = sb("Brow", (16, TC), BF16, p2)
    Crow, B_Crow = sb("Crow", (16, TC), BF16, p2)
    xin = []
    B_xin = []
    for i in range(2):
        t, b = sb("xin%d" % i, (128, TC + 3), BF16, p2)
        xin.append(t)
        B_xin.append(b)
    SCRB = []
    B_SCRB = []
    for i in range(3):
        t, b = sb("scrb%d" % i, (128, TC), BF16, p2)
        SCRB.append(t)
        B_SCRB.append(b)
    dtj = []
    B_dtj = []
    silz = []
    B_silz = []
    for i in range(2):
        t, b = sb("dtj%d" % i, (128, TC), BF16, p2)
        dtj.append(t)
        B_dtj.append(b)
        t, b = sb("silz%d" % i, (128, TC), BF16, p2)
        silz.append(t)
        B_silz.append(b)
    av = []
    B_av = []
    wv = []
    B_wv = []
    uv = []
    B_uv = []
    hv = []
    B_hv = []
    pv = []
    B_pv = []
    dg = []
    B_dg = []
    for i in range(1):
        t, b = sb("wv%d" % i, (128, TC), BF16, p2)
        wv.append(t)
        B_wv.append(b)
        t, b = sb("hv%d" % i, (128, TC), BF16, p2)
        hv.append(t)
        B_hv.append(b)
    for i in range(2):
        t, b = sb("av%d" % i, (128, TC), F32, p2)
        av.append(t)
        B_av.append(b)
    for i in range(4):
        t, b = sb("pv%d" % i, (128, TC), BF16, p2)
        pv.append(t)
        B_pv.append(b)
        t, b = sb("dg%d" % i, (128, 128), BF16, p2)
        dg.append(t)
        B_dg.append(b)

    B_gscol = [Buf("gs%d" % i) for i in range(NJ * DS)]
    S.emit("dve", lambda eng: [eng.memset(gstate[:], 0.0)],
           writes=[B_gstate] + B_gscol)
    S.emit("dve", lambda eng: [eng.memset(xinst[:], 0.0)], writes=[B_xinst])

    dspr = []
    B_dspr = []
    for i in range(2):
        t, b = sb("dspr%d" % i, (128, DM), BF16, p2)
        dspr.append(t)
        B_dspr.append(b)
    dispTcc, B_dispTcc = sb("dispTcc", (128, NKD * 512), BF16, p2)
    w1r = []
    B_w1r = []
    w3r = []
    B_w3r = []
    silcc = []
    B_silcc = []
    h3c = []
    B_h3c = []
    gcc = []
    B_gcc = []
    for i in range(2):
        t, b = sb("w1r%d" % i, (128, 8 * 128), BF16, p2)
        w1r.append(t)
        B_w1r.append(b)
        t, b = sb("w3r%d" % i, (128, 8 * 128), BF16, p2)
        w3r.append(t)
        B_w3r.append(b)
    for i in range(3):
        t, b = sb("silcc%d" % i, (128, 512), BF16, p2)
        silcc.append(t)
        B_silcc.append(b)
        t, b = sb("h3c%d" % i, (128, 512), BF16, p2)
        h3c.append(t)
        B_h3c.append(b)
        t, b = sb("gcc%d" % i, (128, 512), BF16, p2)
        gcc.append(t)
        B_gcc.append(b)

    UH = [0]
    MOEA = []
    for e in range(NE):
        for cc in range(3):
            n0 = cc * 512
            ccw = min(CAP, n0 + 512) - n0
            qlist = list(range(cc * 4, min(NCAP, cc * 4 + 4)))
            for qi, q in enumerate(qlist):
                def trans_unit(e=e, q=q, qi=qi, ccw=ccw):
                    dq = q % 2
                    S.emit("sync", (lambda e=e, q=q, dq=dq: (lambda eng: [
                        eng.dma_start(out=dspr[dq][:],
                                      in_=disp_dram[e * CAP + q * 128: e * CAP + (q + 1) * 128, :])]))(),
                           reads=[B_disp_dram], writes=[B_dspr[dq]], dma=True)
                    for half in range(2):
                        def _tr(eng, dq=dq, half=half):
                            last = None
                            for k in range(4):
                                kd = 4 * half + k
                                last = eng.transpose(PSB[:, k * 128:(k + 1) * 128],
                                                     dspr[dq][:, kd * 128:(kd + 1) * 128],
                                                     identb[:])
                            return last
                        S.emit("pe", _tr, reads=[B_dspr[dq], B_identb], writes=[B_PSB])
                        def _trev(eng, half=half, qi=qi, ccw=ccw):
                            last = None
                            for k in range(4):
                                kd = 4 * half + k
                                last = eng.copy(
                                    dispTcc[:, kd * ccw + qi * 128: kd * ccw + qi * 128 + 128],
                                    PSB[:, k * 128:(k + 1) * 128])
                            return last
                        S.emit("act", _trev, reads=[B_PSB], writes=[B_dispTcc])
                    return None
                MOEA.append(trans_unit)
            for ft in range(NF):
                def h13_unit(e=e, cc=cc, ft=ft, n0=n0, ccw=ccw):
                    wr = UH[0] % 2
                    ur = UH[0] % 3
                    UH[0] += 1
                    S.emit("sync", (lambda e=e, ft=ft, wr=wr: (lambda eng: [
                        eng.dma_start(out=w1r[wr][:], in_=w1_dev[e, ft]),
                        eng.dma_start(out=w3r[wr][:], in_=w3_dev[e, ft])]))(),
                           writes=[B_w1r[wr], B_w3r[wr]], dma=True, ndma=2)
                    def _h1(eng, wr=wr, ccw=ccw):
                        last = None
                        for kd in range(NKD):
                            last = eng.matmul(PS[5][:, 0:ccw],
                                              w1r[wr][:, kd * 128:(kd + 1) * 128],
                                              dispTcc[:, kd * ccw: kd * ccw + ccw],
                                              start=(kd == 0), stop=(kd == NKD - 1))
                        return last
                    S.emit("pe", _h1, reads=[B_w1r[wr], B_dispTcc], writes=[B_PS[5]])
                    S.emit("act", (lambda ccw=ccw, ur=ur: (lambda eng: [
                        eng.activation(out=silcc[ur][:, 0:ccw], in_=PS[5][:, 0:ccw],
                                       func=AF.Silu)]))(),
                           reads=[B_PS[5]], writes=[B_silcc[ur]])
                    def _h3(eng, wr=wr, ccw=ccw):
                        last = None
                        for kd in range(NKD):
                            last = eng.matmul(PS[6][:, 0:ccw],
                                              w3r[wr][:, kd * 128:(kd + 1) * 128],
                                              dispTcc[:, kd * ccw: kd * ccw + ccw],
                                              start=(kd == 0), stop=(kd == NKD - 1))
                        return last
                    S.emit("pe", _h3, reads=[B_w3r[wr], B_dispTcc], writes=[B_PS[6]])
                    S.emit("act", (lambda ccw=ccw, ur=ur: (lambda eng: [
                        eng.copy(h3c[ur][:, 0:ccw], PS[6][:, 0:ccw])]))(),
                           reads=[B_PS[6]], writes=[B_h3c[ur]])
                    def tail(e=e, ft=ft, n0=n0, ccw=ccw, ur=ur):
                        S.emit("dve", (lambda ccw=ccw, ur=ur: (lambda eng: [
                            eng.tensor_mul(gcc[ur][:, 0:ccw], h3c[ur][:, 0:ccw],
                                           silcc[ur][:, 0:ccw])]))(),
                               reads=[B_h3c[ur], B_silcc[ur]], writes=[B_gcc[ur]])
                        S.emit("pool", (lambda e=e, ft=ft, n0=n0, ccw=ccw, ur=ur: (lambda eng: [
                            eng.dma_start(out=g_dram[e, ft, :, n0:n0 + ccw],
                                          in_=gcc[ur][:, 0:ccw])]))(),
                               reads=[B_gcc[ur]], writes=[B_g_dram], dma=True)
                    return tail
                MOEA.append(h13_unit)
    MOEA_POS = [0]
    PENDING_TAILS = []

    def drain_pe(k):
        while k > 0 and MOEA_POS[0] < len(MOEA):
            t = MOEA[MOEA_POS[0]]()
            if t is not None:
                PENDING_TAILS.append(t)
            MOEA_POS[0] += 1
            k -= 1

    def flush_tails():
        for t in PENDING_TAILS:
            t()
        PENDING_TAILS.clear()

    WCTR = [0]
    for ch in range(NCH):
        c0 = ch * TC
        def _ldxn(eng, c0=c0):
            ins = []
            for kd in range(NKD):
                ins.append(eng.dma_start(out=xnch[:, kd * TC:(kd + 1) * TC],
                                         in_=xn_dmaj_mir[kd * 128:(kd + 1) * 128, c0:c0 + TC]))
            return ins
        S.emit("sync", _ldxn, reads=[B_xn_dmaj_mir], writes=[B_xnch], dma=True, ndma=NKD)

        for J in range(NJ):
            slab = WCTR[0] % 2
            WCTR[0] += 1
            S.emit("sync", (lambda J=J, slab=slab: (lambda eng: [
                eng.dma_start(out=wslab[slab][:], in_=w_inproj[J])]))(),
                   writes=[B_wslab[slab]], dma=True)
            pj = J % 2
            def _mm(eng, J=J, slab=slab, pj=pj):
                last = None
                for kd in range(NKD):
                    for nn in range(TC // 512):
                        last = eng.matmul(PS[2 * pj + nn][:],
                                          wslab[slab][:, kd * 128:(kd + 1) * 128],
                                          xnch[:].rearrange("p (k t) -> p k t", k=NKD)[:, kd, nn * 512:(nn + 1) * 512],
                                          start=(kd == 0), stop=(kd == NKD - 1))
                return last
            S.emit("pe", _mm, reads=[B_wslab[slab], B_xnch],
                   writes=[B_PS[2 * pj], B_PS[2 * pj + 1]])
            xr = J % 2
            def _evx(eng, J=J, pj=pj, xr=xr):
                eng.add(xin[xr][:, 3:3 + 512], PS[2 * pj][:], bin_sb[:, J:J + 1])
                return eng.add(xin[xr][:, 3 + 512:3 + 1024], PS[2 * pj + 1][:], bin_sb[:, J:J + 1])
            S.emit("act", _evx, reads=[B_PS[2 * pj], B_PS[2 * pj + 1], B_bin],
                   writes=[B_xin[xr]])
            S.emit("dve", (lambda J=J, xr=xr: (lambda eng: [
                eng.tensor_copy(xin[xr][:, 0:3], xinst[:, J * 4:J * 4 + 3])]))(),
                   reads=[B_xinst], writes=[B_xin[xr]])
            def _cprod(eng, J=J, xr=xr):
                last = None
                for k in range(3):
                    last = eng.mul(SCRB[k][:], xin[xr][:, k:k + TC], cw_sb[:, J * 4 + k:J * 4 + k + 1])
                return last
            S.emit("act", _cprod, reads=[B_xin[xr], B_cw],
                   writes=[B_SCRB[0], B_SCRB[1], B_SCRB[2]])
            def _cadd(eng):
                eng.tensor_add(SCRB[0][:], SCRB[0][:], SCRB[1][:])
                return eng.tensor_add(SCRB[0][:], SCRB[0][:], SCRB[2][:])
            S.emit("dve", _cadd, reads=[B_SCRB[1], B_SCRB[2]], writes=[B_SCRB[0]])
            S.emit("act", (lambda J=J, xr=xr: (lambda eng: [
                eng.mul(SCRB[1][:], xin[xr][:, 3:3 + TC], cw_sb[:, J * 4 + 3:J * 4 + 4])]))(),
                   reads=[B_xin[xr], B_cw], writes=[B_SCRB[1]])
            S.emit("dve", lambda eng: [eng.tensor_add(SCRB[0][:], SCRB[0][:], SCRB[1][:])],
                   reads=[B_SCRB[1]], writes=[B_SCRB[0]])
            S.emit("act", (lambda J=J: (lambda eng: [
                eng.activation(out=xc[:, J * TC:(J + 1) * TC], in_=SCRB[0][:], func=AF.Silu,
                               bias=cb_sb[:, J:J + 1])]))(),
                   reads=[B_SCRB[0], B_cb], writes=[B_xc])
            S.emit("dve", (lambda J=J, xr=xr: (lambda eng: [
                eng.tensor_copy(xinst[:, J * 4:J * 4 + 3], xin[xr][:, TC:TC + 3])]))(),
                   reads=[B_xin[xr]], writes=[B_xinst])
            drain_pe(1)
            flush_tails()

        def _xproj(eng):
            last = None
            for kk in range(16):
                for nn in range(TC // 512):
                    last = eng.matmul(PS[nn][0:80, :],
                                      wxp1_sb[:, kk * 80:(kk + 1) * 80],
                                      xc[:].rearrange("p (j t) -> p j t", j=NJ)[:, kk, nn * 512:(nn + 1) * 512],
                                      start=(kk == 0), stop=(kk == 15))
            return last
        S.emit("pe", _xproj, reads=[B_wxp1, B_xc], writes=[B_PS[0], B_PS[1]])
        def _evxp(eng):
            eng.copy(dtlow[:, 0:512], PS[0][0:64, :])
            eng.copy(dtlow[:, 512:1024], PS[1][0:64, :])
            eng.copy(Brow[:, 0:512], PS[0][64:80, :])
            return eng.copy(Brow[:, 512:1024], PS[1][64:80, :])
        S.emit("act", _evxp, reads=[B_PS[0], B_PS[1]], writes=[B_dtlow, B_Brow])
        def _xprojC(eng):
            last = None
            for kk in range(16):
                for nn in range(TC // 512):
                    last = eng.matmul(PS[2 + nn][0:16, :],
                                      wxpC_sb[:, kk * 16:(kk + 1) * 16],
                                      xc[:].rearrange("p (j t) -> p j t", j=NJ)[:, kk, nn * 512:(nn + 1) * 512],
                                      start=(kk == 0), stop=(kk == 15))
            return last
        S.emit("pe", _xprojC, reads=[B_wxpC, B_xc], writes=[B_PS[2], B_PS[3]])
        def _evC(eng):
            eng.copy(Crow[:, 0:512], PS[2][0:16, :])
            return eng.copy(Crow[:, 512:1024], PS[3][0:16, :])
        S.emit("act", _evC, reads=[B_PS[2], B_PS[3]], writes=[B_Crow])

        for s in range(DS):
            for (row, Ba, dstall, B_dstall) in ((Brow, B_Brow, Ball, B_Ball),
                                                (Crow, B_Crow, Call, B_Call)):
                def _bc(eng, s=s, row=row):
                    last = None
                    for nn in range(TC // 512):
                        last = eng.matmul(PS[2 + nn][:], BSEL_sb[:, s * 128:(s + 1) * 128],
                                          row[:, nn * 512:(nn + 1) * 512],
                                          start=True, stop=True)
                    return last
                S.emit("pe", _bc, reads=[Ba, B_BSEL], writes=[B_PS[2], B_PS[3]])
                def _bce(eng, s=s, dstall=dstall):
                    eng.copy(dstall[:, s * TC:s * TC + 512], PS[2][:])
                    return eng.copy(dstall[:, s * TC + 512:(s + 1) * TC], PS[3][:])
                S.emit("act", _bce, reads=[B_PS[2], B_PS[3]], writes=[B_dstall])

        def prep_J(J):
            pj = J % 2
            for nn in range(TC // 512):
                S.emit("pe", (lambda J=J, nn=nn: (lambda eng: [
                    eng.matmul(PS[4][:], wdt_sb[:, J * 128:(J + 1) * 128],
                               dtlow[:, nn * 512:(nn + 1) * 512],
                               start=True, stop=True)]))(),
                       reads=[B_wdt, B_dtlow], writes=[B_PS[4]])
                S.emit("act", (lambda J=J, nn=nn: (lambda eng: [
                    eng.activation(out=SCRB[2][:, nn * 512:(nn + 1) * 512], in_=PS[4][:],
                                   func=AF.Exp, bias=bdt_sb[:, J:J + 1])]))(),
                       reads=[B_PS[4], B_bdt], writes=[B_SCRB[2]])
            S.emit("act", (lambda pj=pj: (lambda eng: [
                eng.activation(out=dtj[pj][:], in_=SCRB[2][:], func=AF.Ln, bias=1.0)]))(),
                   reads=[B_SCRB[2]], writes=[B_dtj[pj]])

            slab = WCTR[0] % 2
            WCTR[0] += 1
            S.emit("sync", (lambda J=J, slab=slab: (lambda eng: [
                eng.dma_start(out=wslab[slab][:], in_=w_inproj[16 + J])]))(),
                   writes=[B_wslab[slab]], dma=True)
            def _zmm(eng, J=J, slab=slab, pj=pj):
                last = None
                for kd in range(NKD):
                    for nn in range(TC // 512):
                        last = eng.matmul(PS[2 * pj + nn][:],
                                          wslab[slab][:, kd * 128:(kd + 1) * 128],
                                          xnch[:].rearrange("p (k t) -> p k t", k=NKD)[:, kd, nn * 512:(nn + 1) * 512],
                                          start=(kd == 0), stop=(kd == NKD - 1))
                return last
            S.emit("pe", _zmm, reads=[B_wslab[slab], B_xnch],
                   writes=[B_PS[2 * pj], B_PS[2 * pj + 1]])
            def _zev(eng, J=J, pj=pj):
                eng.activation(out=silz[pj][:, 0:512], in_=PS[2 * pj][:], func=AF.Silu,
                               bias=bin_sb[:, 16 + J:17 + J])
                return eng.activation(out=silz[pj][:, 512:1024], in_=PS[2 * pj + 1][:],
                                      func=AF.Silu, bias=bin_sb[:, 16 + J:17 + J])
            S.emit("act", _zev, reads=[B_PS[2 * pj], B_PS[2 * pj + 1], B_bin],
                   writes=[B_silz[pj]])

        def s_body(J, s):
            pj = J % 2
            sr = s % 2
            s4 = s % 4
            col = J * DS + s
            S.emit("dve", (lambda J=J, s=s: (lambda eng: [
                eng.tensor_mul(wv[0][:], xc[:, J * TC:(J + 1) * TC],
                               Ball[:, s * TC:(s + 1) * TC])]))(),
                   reads=[B_xc, B_Ball], writes=[B_wv[0]])
            s2 = s % 2
            S.emit("dve", (lambda s2=s2: (lambda eng: [
                eng.scalar_tensor_tensor(out=wv[0][:], in0=av[s2][:], scalar=-1.0,
                                         in1=wv[0][:], op0=ALU.add, op1=ALU.mult)]))(),
                   reads=[B_av[s2]], writes=[B_wv[0]])
            S.emit("dve", (lambda s2=s2, col=col: (lambda eng: [
                eng.tensor_tensor_scan(out=hv[0][:], data0=av[s2][:], data1=wv[0][:],
                                       initial=gstate[:, col:col + 1],
                                       op0=ALU.mult, op1=ALU.add)]))(),
                   reads=[B_av[s2], B_gscol[col]], writes=[B_hv[0]],
                   selfwait=True)
            S.emit("act", (lambda col=col: (lambda eng: [
                eng.copy(gstate[:, col:col + 1], hv[0][:, TC - 1:TC])]))(),
                   reads=[B_hv[0]], writes=[B_gscol[col]])
            S.emit("dve", (lambda s=s, s4=s4: (lambda eng: [
                eng.tensor_mul(pv[s4][:], hv[0][:], Call[:, s * TC:(s + 1) * TC])]))(),
                   reads=[B_hv[0], B_Call], writes=[B_pv[s4]])
            def _yred(eng, s4=s4, s=s, pj=pj):
                last = None
                for nn in range(TC // 512):
                    last = eng.matmul(PS[2 * pj + nn][:], dg[s4][:],
                                      pv[s4][:, nn * 512:(nn + 1) * 512],
                                      start=(s == 0), stop=(s == DS - 1))
                return last
            S.emit("pe", _yred, reads=[B_dg[s4], B_pv[s4]],
                   writes=[B_PS[2 * pj], B_PS[2 * pj + 1]])

        prep_J(0)
        for J in range(NJ):
            pj = J % 2
            for g in range(4):
                for s in range(4 * g, 4 * g + 4):
                    s2 = s % 2
                    s4 = s % 4
                    col = J * DS + s
                    S.emit("act", (lambda pj=pj, s2=s2, col=col: (lambda eng: [
                        eng.activation(out=av[s2][:], in_=dtj[pj][:], func=AF.Exp,
                                       scale=A_sb[:, col:col + 1])]))(),
                           reads=[B_dtj[pj], B_A], writes=[B_av[s2]])
                    S.emit("act", (lambda s4=s4, col=col: (lambda eng: [
                        eng.mul(dg[s4][:], identb[:], invA_sb[:, col:col + 1])]))(),
                           reads=[B_identb, B_invA], writes=[B_dg[s4]])
                drain_pe(2 if g == 3 else 3)
                for s in range(4 * g, 4 * g + 4):
                    s_body(J, s)
                flush_tails()
                if g == 0 and J + 1 < NJ:
                    prep_J(J + 1)

            def _yev(eng, J=J, pj=pj):
                eng.scalar_tensor_tensor(out=SCRB[0][:, 0:512], in0=xc[:, J * TC:J * TC + 512],
                                         scalar=Dc_sb[:, J:J + 1], in1=PS[2 * pj][:],
                                         op0=ALU.mult, op1=ALU.add)
                eng.scalar_tensor_tensor(out=SCRB[0][:, 512:1024],
                                         in0=xc[:, J * TC + 512:(J + 1) * TC],
                                         scalar=Dc_sb[:, J:J + 1], in1=PS[2 * pj + 1][:],
                                         op0=ALU.mult, op1=ALU.add)
                return eng.tensor_mul(SCRB[0][:], SCRB[0][:], silz[pj][:])
            S.emit("dve", _yev, reads=[B_PS[2 * pj], B_PS[2 * pj + 1], B_xc, B_Dc,
                                       B_silz[pj]], writes=[B_SCRB[0]])
            S.emit("sync", (lambda ch=ch, J=J: (lambda eng: [
                eng.dma_start(out=yg_mir[ch, J], in_=SCRB[0][:])]))(),
                   reads=[B_SCRB[0]], writes=[B_yg_mir], dma=True)

    drain_pe(10 ** 9)
    flush_tails()
    p2.close()
    S.barrier()

    p2b = ExitStack()
    yacF, B_yacF = sb("yacF", (128, (TC // 128) * DM), BF16, p2b)
    wsl2 = []
    B_wsl2 = []
    for i in range(4):
        t, b = sb("wsl2_%d" % i, (128, 8 * 128), BF16, p2b)
        wsl2.append(t)
        B_wsl2.append(b)
    hv2 = []
    B_hv2 = []
    for i in range(2):
        t, b = sb("hv2_%d" % i, (128, TC), BF16, p2b)
        hv2.append(t)
        B_hv2.append(b)
    mo2, B_mo2 = sb("mo2", (128, TC), F32, p2b)
    NTB = TC // 128
    for ch in range(NCH):
        def _ldx(eng, ch=ch):
            return [eng.dma_start(out=yacF[:, tb * DM:(tb + 1) * DM],
                                  in_=x_hbm[ch * TC + tb * 128: ch * TC + (tb + 1) * 128, :])
                    for tb in range(NTB)]
        S.emit("pool", _ldx, writes=[B_yacF], dma=True, ndma=NTB)
        for mp in range(4):
            def _ldw_op(eng, mp=mp):
                return [eng.dma_start(out=wsl2[0][:], in_=w_outproj[2 * mp, :, 0:1024]),
                        eng.dma_start(out=wsl2[1][:], in_=w_outproj[2 * mp, :, 1024:2048]),
                        eng.dma_start(out=wsl2[2][:], in_=w_outproj[2 * mp + 1, :, 0:1024]),
                        eng.dma_start(out=wsl2[3][:], in_=w_outproj[2 * mp + 1, :, 1024:2048])]
            S.emit("sync", _ldw_op, writes=[B_wsl2[0], B_wsl2[1], B_wsl2[2], B_wsl2[3]],
                   dma=True, ndma=4)
            for J in range(NJ):
                sr2 = J % 2
                S.emit("sync", (lambda ch=ch, J=J, sr2=sr2: (lambda eng: [
                    eng.dma_start(out=hv2[sr2][:], in_=yg_mir[ch, J])]))(),
                       reads=[B_yg_mir], writes=[B_hv2[sr2]], dma=True)
                def _opmm(eng, J=J, sr2=sr2, mp=mp):
                    last = None
                    for half in range(2):
                        w = wsl2[2 * half + (0 if J < 8 else 1)]
                        kslice = w[:, (J % 8) * 128:((J % 8) + 1) * 128]
                        for nn in range(TC // 512):
                            last = eng.matmul(PS[2 * half + nn][:], kslice,
                                              hv2[sr2][:, nn * 512:(nn + 1) * 512],
                                              start=(J == 0), stop=(J == NJ - 1))
                    return last
                S.emit("pe", _opmm, reads=[B_wsl2[0], B_wsl2[1], B_wsl2[2], B_wsl2[3],
                                           B_hv2[sr2]],
                       writes=[B_PS[0], B_PS[1], B_PS[2], B_PS[3]])
            for half in range(2):
                m = 2 * mp + half
                def _ob(eng, m=m, half=half):
                    eng.add(mo2[:, 0:512], PS[2 * half][:], bout_sb[:, m:m + 1])
                    return eng.add(mo2[:, 512:1024], PS[2 * half + 1][:], bout_sb[:, m:m + 1])
                S.emit("act", _ob, reads=[B_PS[2 * half], B_PS[2 * half + 1], B_bout],
                       writes=[B_mo2])
                for tb in range(NTB):
                    pk = 4 + (tb % 2)
                    S.emit("pe", (lambda tb=tb, pk=pk: (lambda eng: [
                        eng.transpose(PS[pk][:, 0:128], mo2[:, tb * 128:(tb + 1) * 128],
                                      identf[:])]))(),
                           reads=[B_mo2, B_identf], writes=[B_PS[pk]])
                    S.emit("dve", (lambda m=m, pk=pk, tb=tb: (lambda eng: [
                        eng.tensor_add(yacF[:, tb * DM + m * 128: tb * DM + (m + 1) * 128],
                                       yacF[:, tb * DM + m * 128: tb * DM + (m + 1) * 128],
                                       PS[pk][:, 0:128])]))(),
                           reads=[B_PS[pk]], writes=[B_yacF])
        def _stout(eng, ch=ch):
            return [eng.dma_start(out=out_hbm[ch * TC + tb * 128: ch * TC + (tb + 1) * 128, :],
                                  in_=yacF[:, tb * DM:(tb + 1) * DM])
                    for tb in range(NTB)]
        S.emit("pool", _stout, reads=[B_yacF], writes=[B_out_hbm], dma=True, ndma=NTB)
    p2b.close()
    S.barrier()

    p3 = ExitStack()
    HW = CAP // 2
    gb2h = []
    B_gb2h = []
    for i in range(2):
        t, b = sb("gb2h%d" % i, (128, NF * HW), BF16, p3)
        gb2h.append(t)
        B_gb2h.append(b)
    w2s = []
    B_w2s = []
    for i in range(2):
        t, b = sb("w2s_%d" % i, (128, NF * DM), BF16, p3)
        w2s.append(t)
        B_w2s.append(b)
    eo = []
    B_eo = []
    for i in range(2):
        t, b = sb("eo%d" % i, (128, DM), BF16, p3)
        eo.append(t)
        B_eo.append(b)

    for e in range(NE):
        we = e % 2
        def _ldw2(eng, e=e, we=we):
            return [eng.dma_start(out=w2s[we][:, kf * DM:(kf + 1) * DM], in_=w2_dev[e, kf])
                    for kf in range(NF)]
        S.emit("sync", _ldw2, writes=[B_w2s[we]], dma=True, ndma=NF)
        for h in range(2):
            def _ldg(eng, e=e, h=h):
                return [eng.dma_start(out=gb2h[h][:, kf * HW:(kf + 1) * HW],
                                      in_=g_dram[e, kf, :, h * HW:(h + 1) * HW])
                        for kf in range(NF)]
            S.emit("sync", _ldg, reads=[B_g_dram], writes=[B_gb2h[h]], dma=True, ndma=NF)
        for q in range(NCAP):
            h = q // 5
            qh = q - 5 * h
            def _w2mm(eng, q=q, we=we, h=h, qh=qh):
                last = None
                for kf in range(NF):
                    gs = gb2h[h][:, kf * HW + qh * 128: kf * HW + (qh + 1) * 128]
                    for nn in range(2):
                        last = eng.matmul(PS[2 * (q % 2) + nn][:], gs,
                                          w2s[we][:, kf * DM + nn * 512: kf * DM + (nn + 1) * 512],
                                          start=(kf == 0), stop=(kf == NF - 1))
                return last
            S.emit("pe", _w2mm, reads=[B_gb2h[h], B_w2s[we]],
                   writes=[B_PS[2 * (q % 2)], B_PS[2 * (q % 2) + 1]])
            def _eoev(eng, q=q):
                eng.copy(eo[q % 2][:, 0:512], PS[2 * (q % 2)][:])
                return eng.copy(eo[q % 2][:, 512:1024], PS[2 * (q % 2) + 1][:])
            S.emit("act", _eoev, reads=[B_PS[2 * (q % 2)], B_PS[2 * (q % 2) + 1]],
                   writes=[B_eo[q % 2]])
            S.emit("sync", (lambda e=e, q=q: (lambda eng: [
                eng.dma_start(out=eout_dram[e * CAP + q * 128: e * CAP + (q + 1) * 128, :],
                              in_=eo[q % 2][:])]))(),
                   reads=[B_eo[q % 2]], writes=[B_eout_dram], dma=True)

    p3.close()
    S.barrier()

    p4 = ExitStack()
    acc = []
    B_acc = []
    cmb = []
    B_cmb = []
    for i in range(4):
        t, b = sb("acc%d" % i, (128, DM), F32, p4)
        acc.append(t)
        B_acc.append(b)
        t, b = sb("cmb%d" % i, (128, DM), BF16, p4)
        cmb.append(t)
        B_cmb.append(b)
        S.emit("dve", (lambda i=i: (lambda eng: [eng.memset(cmb[i][:], 0.0)]))(),
               writes=[B_cmb[i]])

    for c in range(NCT):
        ia = c % 4
        orows = out_hbm.rearrange("(p c) d -> c p d", c=NCT)[c]
        S.emit("sync", (lambda orows=orows, ia=ia: (lambda eng: [
            eng.dma_start(out=acc[ia][:], in_=orows)]))(),
               reads=[B_out_hbm], writes=[B_acc[ia]], dma=True)
        for (dsl_i, B_dsl_i, pmv, B_pmv) in ((dsl1_sb, B_dsl1, pm1_sb, B_pm1),
                                             (dsl2_sb, B_dsl2, pm2_sb, B_pm2)):
            S.emit("pool", (lambda c=c, ia=ia, dsl_i=dsl_i: (lambda eng: [
                eng.indirect_dma_start(
                    out=cmb[ia][:], out_offset=None,
                    in_=eout_dram, in_offset=IndirectOffsetOnAxis(ap=dsl_i[:, c:c + 1], axis=0),
                    bounds_check=BCREG["r"], oob_is_err=False)
            ]))(), reads=[B_eout_dram, B_dsl_i], writes=[B_cmb[ia]], dma=True)
            S.emit("dve", (lambda c=c, ia=ia, pmv=pmv: (lambda eng: [
                eng.scalar_tensor_tensor(out=acc[ia][:], in0=cmb[ia][:],
                                         scalar=pmv[:, c:c + 1], in1=acc[ia][:],
                                         op0=ALU.mult, op1=ALU.add)]))(),
                   reads=[B_cmb[ia], B_pmv], writes=[B_acc[ia]])
        S.emit("sync", (lambda orows=orows, ia=ia: (lambda eng: [
            eng.dma_start(out=orows, in_=acc[ia][:])]))(),
               reads=[B_acc[ia]], writes=[B_out_hbm], dma=True)

    S.barrier()
    p4.close()

    with nc.Block() as blk:
        def run_stream(eng, name):
            for (waits, body, sem, inc, selfwait, val) in S.streams[name]:
                for (sname, v) in waits:
                    eng.wait_ge(sems[sname], v)
                if body is None:
                    continue
                ins = body(eng)
                if not isinstance(ins, (list, tuple)):
                    ins = [ins]
                if inc >= 16:
                    assert inc == 16 * len(ins), (name, inc, len(ins))
                    for li in ins:
                        li.then_inc(sems[sem], 16)
                else:
                    ins[-1].then_inc(sems[sem], inc)
                if selfwait:
                    eng.wait_ge(sems[sem], val)

        @blk.tensor
        def _(eng):
            run_stream(eng, "pe")

        @blk.vector
        def _(eng):
            run_stream(eng, "dve")

        @blk.scalar
        def _(eng):
            run_stream(eng, "act")

        @blk.gpsimd
        def _(eng):
            run_stream(eng, "pool")

        @blk.sync
        def _(eng):
            run_stream(eng, "sync")

    es.close()
    return nc


def _host_inputs(inp):
    w = {}
    ipw = np.asarray(inp["L0_attn_in_proj_w"], np.float32)
    w["w_inproj"] = trunc_bf16(
        ipw.T.reshape(8, 128, 32, 128).transpose(2, 1, 0, 3).reshape(32, 128, 1024))
    w["b_inproj"] = np.ascontiguousarray(
        np.asarray(inp["L0_attn_in_proj_b"], np.float32).reshape(32, 128).T)
    xp = np.asarray(inp["L0_attn_x_proj_w"], np.float32)
    w["w_xproj1"] = trunc_bf16(
        xp[:80].T.reshape(16, 128, 80).transpose(1, 0, 2).reshape(128, 16 * 80))
    w["w_xprojC"] = trunc_bf16(
        xp[80:96].T.reshape(16, 128, 16).transpose(1, 0, 2).reshape(128, 16 * 16))
    w["w_dtproj"] = trunc_bf16(np.asarray(inp["L0_attn_dt_proj_w"], np.float32).T)
    w["b_dtproj"] = np.ascontiguousarray(
        np.asarray(inp["L0_attn_dt_proj_b"], np.float32).reshape(16, 128).T)
    cw = np.asarray(inp["L0_attn_conv_w"], np.float32)
    w["conv_w4"] = np.ascontiguousarray(
        cw.reshape(16, 128, 4).transpose(1, 0, 2).reshape(128, 64))
    w["conv_bc"] = np.ascontiguousarray(
        np.asarray(inp["L0_attn_conv_b"], np.float32).reshape(16, 128).T)
    al = np.asarray(inp["L0_attn_A_log"], np.float32)
    w["A_logr"] = np.ascontiguousarray(
        al.T.reshape(16, 128, 16).transpose(1, 0, 2).reshape(128, 256))
    w["D_col"] = np.ascontiguousarray(
        np.asarray(inp["L0_attn_D"], np.float32).reshape(16, 128).T)
    opw = np.asarray(inp["L0_attn_out_proj_w"], np.float32)
    w["w_outproj"] = trunc_bf16(
        opw.T.reshape(16, 128, 8, 128).transpose(2, 1, 0, 3).reshape(8, 128, 16 * 128))
    w["b_outproj"] = np.ascontiguousarray(
        np.asarray(inp["L0_attn_out_proj_b"], np.float32).reshape(8, 128).T)
    gwt = np.asarray(inp["L0_ffn_gate_w"], np.float32).T
    w["gate_wT"] = np.ascontiguousarray(
        gwt.reshape(8, 128, 8).transpose(1, 0, 2).reshape(128, 64))
    w1 = np.asarray(inp["L0_ffn_w1"], np.float32)
    w["w1_dev"] = trunc_bf16(
        w1.reshape(8, 8, 128, 28, 128).transpose(0, 3, 2, 1, 4).reshape(8, 28, 128, 1024))
    w3 = np.asarray(inp["L0_ffn_w3"], np.float32)
    w["w3_dev"] = trunc_bf16(
        w3.reshape(8, 8, 128, 28, 128).transpose(0, 3, 2, 1, 4).reshape(8, 28, 128, 1024))
    w2 = np.asarray(inp["L0_ffn_w2"], np.float32)
    w["w2_dev"] = trunc_bf16(w2.reshape(8, 28, 128, 1024))
    w["ident_f"] = np.eye(128, dtype=np.float32)
    w["ident_b"] = trunc_bf16(np.eye(128, dtype=np.float32))
    lt = np.tril(np.ones((128, 128), np.float32), -1).T
    w["LT"] = np.ascontiguousarray(lt)
    bsel = np.zeros((16, 16 * 128), np.float32)
    for s in range(16):
        bsel[s, s * 128:(s + 1) * 128] = 1.0
    w["BSEL"] = trunc_bf16(bsel)
    return w


def kernel(**inputs):
    x = np.asarray(inputs["L0_attn_x"], np.float32)
    shared = _host_inputs(inputs)
    nc = build_program()
    in_maps = []
    for c in range(NCORES):
        m = dict(shared)
        m["x"] = np.ascontiguousarray(x[c])
        in_maps.append(m)
    res = run_bass_kernel_spmd(nc, in_maps, core_ids=list(range(NCORES)))
    out = np.stack([res.results[c]["out"] for c in range(NCORES)], axis=0)
    return out.astype(np.float32)


# version 08
# speedup vs baseline: 1.1771x; 1.1771x over previous
import numpy as np
import ml_dtypes
import concourse.bass as bass
import concourse.mybir as mybir
from concourse.bass_utils import run_bass_kernel_spmd
from concourse.bass_core_rust import IndirectOffsetOnAxis
from contextlib import ExitStack

F32 = mybir.dt.float32
BF16 = mybir.dt.bfloat16
I32 = mybir.dt.int32
ALU = mybir.AluOpType
AF = mybir.ActivationFunctionType
AX = mybir.AxisListType.X
BF = ml_dtypes.bfloat16

T, DM, DI, DS, DTR = 4096, 1024, 2048, 16, 64
NE, FF, CAP = 8, 3584, 1280
NJ = DI // 128
NKD = DM // 128
NF = FF // 128
TC = 1024
NCH = T // TC
NCT = 32
NCAP = CAP // 128
EPS = 1e-5

NCORES = 8
DEBUG_TAPS = False


def trunc_bf16(a):
    a = np.ascontiguousarray(a, dtype=np.float32)
    u = a.view(np.uint16)
    hi = np.ascontiguousarray(u.reshape(-1, 2)[:, 1])
    return hi.reshape(a.shape).view(BF)


ENGINES = ("pe", "dve", "act", "pool", "sync")
SEM_OF = {"pe": "sP", "dve": "sV", "act": "sA", "pool": "sGc", "sync": "sS"}
DMA_POOL = {"sync": ["sSd%d" % i for i in range(12)],
            "pool": ["sGd%d" % i for i in range(8)],
            "act": ["sAd0"]}
ALL_SEMS = sorted(set(SEM_OF.values()) | {s for v in DMA_POOL.values() for s in v})


class Buf:
    __slots__ = ("name", "w", "r")

    def __init__(self, name):
        self.name = name
        self.w = None
        self.r = []


class Sched:
    def __init__(self):
        self.streams = {e: [] for e in ENGINES}
        self.cnt = {s: 0 for s in ALL_SEMS}
        self.waited = {e: {} for e in ENGINES}
        self.rr = {q: 0 for q in DMA_POOL}

    def _wait(self, eng, waits, sem, val):
        seen = self.waited[eng]
        if seen.get(sem, 0) >= val:
            return
        seen[sem] = val
        waits.append((sem, val))

    def emit(self, eng, body, reads=(), writes=(), dma=False, selfwait=False, ndma=1):
        waits = []
        for b in reads:
            if b.w is not None and b.w[0] != eng:
                self._wait(eng, waits, b.w[1], b.w[2])
        for b in writes:
            if b.w is not None and b.w[0] != eng:
                self._wait(eng, waits, b.w[1], b.w[2])
            for (reng, rsem, rval) in b.r:
                if reng != eng:
                    self._wait(eng, waits, rsem, rval)
        if dma:
            pool = DMA_POOL[eng]
            sem = pool[self.rr[eng] % len(pool)]
            self.rr[eng] += 1
            if self.cnt[sem] > 0:
                self._wait(eng, waits, sem, self.cnt[sem])
            inc = 16 * ndma
        else:
            sem = SEM_OF[eng]
            inc = 1
        self.cnt[sem] += inc
        val = self.cnt[sem]
        self.streams[eng].append((waits, body, sem, inc, selfwait, val))
        ev = (eng, sem, val)
        for b in reads:
            b.r.append(ev)
        for b in writes:
            b.w = ev
            b.r = []
        return ev

    def barrier(self):
        snapshot = dict(self.cnt)
        for e in ENGINES:
            waits = []
            for sem, val in snapshot.items():
                if val > 0:
                    self._wait(e, waits, sem, val)
            if waits:
                self.streams[e].append((waits, None, None, 0, False, 0))


def build_program():
    nc = bass.Bass("TRN2", target_bir_lowering=False, debug=False,
                   detect_race_conditions=False)
    S = Sched()
    es = ExitStack()

    def din(name, shape, dt):
        return nc.dram_tensor(name, list(shape), dt, kind="ExternalInput").ap()

    def dscratch(name, shape, dt):
        kind = "ExternalOutput" if DEBUG_TAPS else "Internal"
        return nc.dram_tensor(name, list(shape), dt, kind=kind).ap()

    x_hbm = din("x", (T, DM), F32)
    w_inproj = din("w_inproj", (32, 128, 8 * 128), BF16)
    b_inproj = din("b_inproj", (128, 32), F32)
    w_xproj1 = din("w_xproj1", (128, 16 * 80), BF16)
    w_xprojC = din("w_xprojC", (128, 16 * 16), BF16)
    w_dtproj = din("w_dtproj", (64, DI), BF16)
    b_dtproj = din("b_dtproj", (128, NJ), F32)
    conv_w4 = din("conv_w4", (128, NJ * 4), F32)
    conv_bc = din("conv_bc", (128, NJ), F32)
    A_logr = din("A_logr", (128, NJ * DS), F32)
    D_col = din("D_col", (128, NJ), F32)
    w_outproj = din("w_outproj", (8, 128, 16 * 128), BF16)
    b_outproj = din("b_outproj", (128, 8), F32)
    gate_wT = din("gate_wT", (128, NKD * NE), F32)
    w1_dev = din("w1_dev", (NE, NF, 128, 8 * 128), BF16)
    w3_dev = din("w3_dev", (NE, NF, 128, 8 * 128), BF16)
    w2_dev = din("w2_dev", (NE, NF, 128, DM), BF16)
    ident_f_d = din("ident_f", (128, 128), F32)
    ident_b_d = din("ident_b", (128, 128), BF16)
    LT_d = din("LT", (128, 128), F32)
    BSEL_d = din("BSEL", (16, DS * 128), BF16)

    xn_tok_mir = dscratch("xn_tok_mir", (T, DM), BF16)
    xn_dmaj_mir = dscratch("xn_dmaj_mir", (DM, T), BF16)
    yg_mir = dscratch("yg_mir", (NCH, NJ, 128, TC), BF16)
    disp_dram = dscratch("disp_dram", (NE * CAP, DM), BF16)
    g_dram = dscratch("g_dram", (NE, NF, 128, CAP), BF16)
    eout_dram = dscratch("eout_dram", (NE * CAP, DM), BF16)
    out_hbm = nc.dram_tensor("out", [T, DM], F32, kind="ExternalOutput").ap()

    B_xn_tok_mir = Buf("xn_tok_mir")
    B_xn_dmaj_mir = Buf("xn_dmaj_mir")
    B_yg_mir = Buf("yg_mir")
    B_disp_dram = Buf("disp_dram")
    B_g_dram = Buf("g_dram")
    B_eout_dram = Buf("eout_dram")
    B_out_hbm = Buf("out_hbm")

    sems = {}
    for sname in ALL_SEMS:
        sems[sname] = es.enter_context(nc.semaphore(sname))

    def sb(name, shape, dt, stack=None):
        t = (stack or es).enter_context(nc.sbuf_tensor(name, list(shape), dt))
        return t, Buf(name)

    def psum(name, shape, dt, stack=None):
        t = (stack or es).enter_context(nc.psum_tensor(name, list(shape), dt))
        return t, Buf(name)

    identf, B_identf = sb("identf", (128, 128), F32)
    identb, B_identb = sb("identb", (128, 128), BF16)

    BSEL_sb, B_BSEL = sb("BSEL_sb", (16, DS * 128), BF16)
    A_sb, B_A = sb("A_sb", (128, NJ * DS), F32)
    invA_sb, B_invA = sb("invA_sb", (128, NJ * DS), F32)
    bin_sb, B_bin = sb("bin_sb", (128, 32), F32)
    bdt_sb, B_bdt = sb("bdt_sb", (128, NJ), F32)
    cw_sb, B_cw = sb("cw_sb", (128, NJ * 4), F32)
    cb_sb, B_cb = sb("cb_sb", (128, NJ), F32)
    Dc_sb, B_Dc = sb("Dc_sb", (128, NJ), F32)
    bout_sb, B_bout = sb("bout_sb", (128, 8), F32)
    wxp1_sb, B_wxp1 = sb("wxp1_sb", (128, 16 * 80), BF16)
    wxpC_sb, B_wxpC = sb("wxpC_sb", (128, 16 * 16), BF16)
    wdt_sb, B_wdt = sb("wdt_sb", (64, DI), BF16)
    dsl1_sb, B_dsl1 = sb("dsl1", (128, 32), I32)
    dsl2_sb, B_dsl2 = sb("dsl2", (128, 32), I32)
    pm1_sb, B_pm1 = sb("pm1", (128, 32), F32)
    pm2_sb, B_pm2 = sb("pm2", (128, 32), F32)
    gstate, B_gstate = sb("gstate", (128, NJ * DS), F32)
    xinst, B_xinst = sb("xinst", (128, NJ * 4), BF16)

    cst_sb, B_cst = sb("cst_sb", (128, 8), F32)
    CONSTS = {EPS: 0, 1.0: 1, -0.5: 2, -1.0: 3, 0.0: 4}
    for v, i in CONSTS.items():
        nc.const_aps.register(F32, v, cst_sb[:, i:i + 1])

    def _fillconsts(eng):
        last = None
        for v, i in CONSTS.items():
            last = eng.memset(cst_sb[:, i:i + 1], float(v))
        return last
    S.emit("dve", _fillconsts, writes=[B_cst])

    BCREG = {}

    def _mkbc(eng):
        r = eng.alloc_register("bcreg")
        BCREG["r"] = r
        return eng.reg_mov(r, NE * CAP - 1)
    S.emit("pool", _mkbc)
    S.barrier()

    PS = []
    B_PS = []
    for i in range(7):
        t, b = psum("ps%d" % i, (128, 512), F32)
        PS.append(t)
        B_PS.append(b)
    PSB, B_PSB = psum("psb", (128, 512), BF16)

    p0 = ExitStack()
    LT_sb, B_LT = sb("LT_sb", (128, 128), F32, p0)
    gw_sb, B_gw = sb("gw_sb", (128, NKD * NE), F32, p0)
    scr0, B_scr0 = sb("scr0", (128, NJ * DS), F32, p0)

    def ldconst(dst, src, bufs):
        S.emit("sync", (lambda d=dst, s=src: (lambda eng: [eng.dma_start(out=d, in_=s)]))(),
               writes=bufs, dma=True)

    ldconst(identf[:], ident_f_d, [B_identf])
    ldconst(identb[:], ident_b_d, [B_identb])
    ldconst(LT_sb[:], LT_d, [B_LT])
    ldconst(BSEL_sb[:], BSEL_d, [B_BSEL])
    ldconst(A_sb[:], A_logr, [B_A])
    ldconst(bin_sb[:], b_inproj, [B_bin])
    ldconst(bdt_sb[:], b_dtproj, [B_bdt])
    ldconst(cw_sb[:], conv_w4, [B_cw])
    ldconst(cb_sb[:], conv_bc, [B_cb])
    ldconst(Dc_sb[:], D_col, [B_Dc])
    ldconst(bout_sb[:], b_outproj, [B_bout])
    ldconst(gw_sb[:], gate_wT, [B_gw])
    ldconst(wxp1_sb[:], w_xproj1, [B_wxp1])
    ldconst(wxpC_sb[:], w_xprojC, [B_wxpC])
    ldconst(wdt_sb[:], w_dtproj, [B_wdt])


    def _actwarm(eng):
        last = None
        for fn in (AF.Square, AF.Ln, AF.Exp, AF.Silu, AF.Sigmoid, AF.Ln, AF.Exp):
            last = eng.activation(out=cst_sb[:, 5:6], in_=cst_sb[:, 1:2], func=fn)
        last = eng.sign(cst_sb[:, 5:6], cst_sb[:, 1:2])
        return last
    S.emit("act", _actwarm, reads=[B_cst], writes=[B_cst])

    S.emit("act", lambda eng: [eng.activation(out=scr0[:], in_=A_sb[:], func=AF.Exp)],
           reads=[B_A], writes=[B_scr0], selfwait=True)
    S.emit("dve", lambda eng: [eng.tensor_scalar(out=A_sb[:], in0=scr0[:], scalar1=-1.0,
                                                 scalar2=None, op0=ALU.mult)],
           reads=[B_scr0], writes=[B_A], selfwait=True)
    S.emit("dve", lambda eng: [eng.tensor_single_scalar(invA_sb[:], scr0[:], 1e-6, ALU.is_lt)],
           reads=[B_scr0], writes=[B_invA], selfwait=True)
    S.emit("dve", lambda eng: [eng.tensor_scalar(out=scr0[:], in0=A_sb[:], scalar1=-1.0,
                                                 scalar2=1.0, op0=ALU.mult, op1=ALU.add)],
           reads=[B_A], writes=[B_scr0], selfwait=True)
    S.emit("dve", lambda eng: [eng.tensor_mul(scr0[:], scr0[:], invA_sb[:])],
           reads=[B_invA], writes=[B_scr0], selfwait=True)
    S.emit("dve", lambda eng: [eng.tensor_add(scr0[:], scr0[:], A_sb[:])],
           reads=[B_A], writes=[B_scr0], selfwait=True)
    S.emit("dve", lambda eng: [eng.reciprocal(invA_sb[:], scr0[:])],
           reads=[B_scr0], writes=[B_invA], selfwait=True)
    S.barrier()

    p1 = ExitStack()
    xnd, B_xnd = sb("xnd", (128, NKD * T), BF16, p1)
    xt, B_xt = sb("xt", (128, DM), F32, p1)
    sq, B_sq = sb("sq", (128, DM), F32, p1)
    mcol, B_mcol = sb("mcol", (128, 8), F32, p1)
    xnf, B_xnf = sb("xnf", (128, DM), F32, p1)
    xnb, B_xnb = sb("xnb", (128, DM), BF16, p1)
    xnTf, B_xnTf = sb("xnTf", (128, NKD * 128), F32, p1)
    LG, B_LG = sb("LG", (128, NCT * NE), F32, p1)
    if DEBUG_TAPS:
        mcol_tap = dscratch("mcol_tap", (128, 8), F32)
        xt_tap = dscratch("xt_tap", (128, DM), F32)
        B_mcol_tap, B_xt_tap = Buf("mcol_tap"), Buf("xt_tap")
        dsl_tap = dscratch("dsl_tap", (128, 64), I32)
        pm_tap = dscratch("pm_tap", (128, 64), F32)
        B_dsl_tap, B_pm_tap = Buf("dsl_tap"), Buf("pm_tap")

    for c in range(NCT):
        xrows = x_hbm.rearrange("(p c) d -> c p d", c=NCT)[c]
        S.emit("sync", (lambda xr=xrows: (lambda eng: [eng.dma_start(out=xt[:], in_=xr)]))(),
               reads=[], writes=[B_xt], dma=True, selfwait=True)
        S.emit("act", lambda eng: [eng.activation(out=sq[:], in_=xt[:], func=AF.Square)],
               reads=[B_xt], writes=[B_sq], selfwait=True)
        S.emit("dve", lambda eng: [eng.reduce_sum(mcol[:, 0:1], sq[:], axis=AX)],
               reads=[B_sq], writes=[B_mcol], selfwait=True)
        S.emit("dve", lambda eng: [eng.reduce_sum(mcol[:, 1:2], xt[:], axis=AX)],
               reads=[B_xt], writes=[B_mcol], selfwait=True)
        def _stats1(eng):
            eng.tensor_scalar(out=mcol[:, 1:2], in0=mcol[:, 1:2], scalar1=1.0 / DM,
                              scalar2=None, op0=ALU.mult)
            return eng.tensor_scalar(out=mcol[:, 2:3], in0=mcol[:, 0:1], scalar1=1.0 / DM,
                                     scalar2=None, op0=ALU.mult)
        S.emit("dve", lambda eng: [_stats1(eng)], reads=[], writes=[B_mcol], selfwait=True)
        S.emit("dve", lambda eng: [eng.tensor_mul(mcol[:, 3:4], mcol[:, 1:2], mcol[:, 1:2])],
               reads=[], writes=[B_mcol], selfwait=True)
        S.emit("dve", lambda eng: [eng.tensor_sub(mcol[:, 2:3], mcol[:, 2:3], mcol[:, 3:4])],
               reads=[], writes=[B_mcol], selfwait=True)
        S.emit("act", lambda eng: [eng.activation(out=mcol[:, 4:5], in_=mcol[:, 2:3],
                                                  func=AF.Ln, bias=EPS)],
               reads=[B_mcol], writes=[B_mcol], selfwait=True)
        S.emit("act", lambda eng: [eng.activation(out=mcol[:, 5:6], in_=mcol[:, 4:5],
                                                  func=AF.Exp, scale=-0.5)],
               reads=[B_mcol], writes=[B_mcol], selfwait=True)
        if DEBUG_TAPS and c == 0:
            S.emit("sync", lambda eng: [eng.dma_start(out=mcol_tap, in_=mcol[:])],
                   reads=[B_mcol], writes=[B_mcol_tap], dma=True)
            S.emit("sync", lambda eng: [eng.dma_start(out=xt_tap, in_=xt[:])],
                   reads=[B_xt], writes=[B_xt_tap], dma=True)
        S.emit("dve", lambda eng: [eng.tensor_scalar(out=xnf[:], in0=xt[:],
                                                     scalar1=mcol[:, 1:2], scalar2=mcol[:, 5:6],
                                                     op0=ALU.subtract, op1=ALU.mult)],
               reads=[B_xt, B_mcol], writes=[B_xnf], selfwait=True)
        S.emit("dve", lambda eng: [eng.tensor_copy(xnb[:], xnf[:])],
               reads=[B_xnf], writes=[B_xnb], selfwait=True)
        mrows = xn_tok_mir.rearrange("(p c) d -> c p d", c=NCT)[c]
        S.emit("sync", (lambda mr=mrows: (lambda eng: [eng.dma_start(out=mr, in_=xnb[:])]))(),
               reads=[B_xnb], writes=[B_xn_tok_mir], dma=True, selfwait=True)
        for kd in range(NKD):
            pk = kd % 2
            S.emit("pe", (lambda kd=kd, pk=pk: (lambda eng: [
                eng.transpose(PS[pk][:, 0:128], xnf[:, kd * 128:(kd + 1) * 128], identf[:])
            ]))(), reads=[B_xnf, B_identf], writes=[B_PS[pk]], selfwait=True)
            def _evac(eng, kd=kd, pk=pk, c=c):
                eng.tensor_copy(xnTf[:, kd * 128:(kd + 1) * 128], PS[pk][:, 0:128])
                dst = xnd[:].rearrange("p (k t) -> p k t", k=NKD)[:, kd]
                dst = bass.AP(dst.tensor, dst.offset + c, [dst.ap[0], [NCT, 128]])
                return eng.tensor_copy(dst, PS[pk][:, 0:128])
            S.emit("dve", (lambda f=_evac: (lambda eng: [f(eng)]))(),
                   reads=[B_PS[pk]], writes=[B_xnTf, B_xnd], selfwait=True)
        def _router(eng, c=c):
            last = None
            for kd in range(NKD):
                last = eng.matmul(PS[2][:, 0:8], xnTf[:, kd * 128:(kd + 1) * 128],
                                  gw_sb[:, kd * NE:(kd + 1) * NE],
                                  start=(kd == 0), stop=(kd == NKD - 1))
            return last
        S.emit("pe", (lambda f=_router: (lambda eng: [f(eng)]))(),
               reads=[B_xnTf, B_gw], writes=[B_PS[2]], selfwait=True)
        S.emit("dve", (lambda c=c: (lambda eng: [
            eng.tensor_copy(LG[:, c * NE:(c + 1) * NE], PS[2][:, 0:8])]))(),
               reads=[B_PS[2]], writes=[B_LG], selfwait=True)

    def _stmir(eng):
        ins = []
        for kd in range(NKD):
            ins.append(eng.dma_start(out=xn_dmaj_mir[kd * 128:(kd + 1) * 128, :],
                                     in_=xnd[:, kd * T:(kd + 1) * T]))
        return ins
    S.emit("sync", lambda eng: _stmir(eng), reads=[B_xnd], writes=[B_xn_dmaj_mir],
           dma=True, ndma=NKD)

    LG3 = LG[:].rearrange("p (c e) -> p c e", e=NE)
    rt = ExitStack()
    m1, B_m1 = sb("m1", (128, NCT), F32, rt)
    nm1, B_nm1 = sb("nm1", (128, NCT), F32, rt)
    ee, B_ee = sb("ee", (128, NCT * NE), F32, rt)
    ssum, B_ssum = sb("ssum", (128, NCT), F32, rt)
    p1c, B_p1c = sb("p1c", (128, NCT), F32, rt)
    d3, B_d3 = sb("d3", (128, NCT * NE), F32, rt)
    sg, B_sg = sb("sgn", (128, NCT * NE), F32, rt)
    oh, B_oh = sb("oh", (128, NCT * NE), F32, rt)
    cand, B_cand = sb("cand", (128, NCT * NE), F32, rt)
    idx1, B_idx1 = sb("idx1", (128, NCT), F32, rt)
    idx2, B_idx2 = sb("idx2", (128, NCT), F32, rt)
    ohx1, B_ohx1 = sb("ohx1", (128, NCT * NE), F32, rt)
    ohx2, B_ohx2 = sb("ohx2", (128, NCT * NE), F32, rt)
    l2t, B_l2t = sb("l2t", (128, NCT * NE), F32, rt)
    m2, B_m2 = sb("m2", (128, NCT), F32, rt)
    ohseq, B_ohseq = sb("ohseq", (128, NE * 64), F32, rt)
    zer64, B_zer64 = sb("zer64", (128, 64), F32, rt)
    Spe, B_Spe = sb("Spe", (128, NE), F32, rt)
    P0sb, B_P0sb = sb("P0sb", (128, NE), F32, rt)
    posi, B_posi = sb("posi", (128, NE * 64), F32, rt)
    qtmp, B_qtmp = sb("qtmp", (128, NCT * NE), F32, rt)
    pos1, B_pos1 = sb("pos1", (128, NCT), F32, rt)
    pos2, B_pos2 = sb("pos2", (128, NCT), F32, rt)
    keep, B_keep = sb("keep", (128, NCT), F32, rt)
    dslf, B_dslf = sb("dslf", (128, NCT), F32, rt)
    xnsb, B_xnsb = sb("xnsb", (128, DM), BF16, rt)

    d33 = d3[:].rearrange("p (c e) -> p c e", e=NE)
    oh3 = oh[:].rearrange("p (c e) -> p c e", e=NE)
    cand3 = cand[:].rearrange("p (c e) -> p c e", e=NE)
    ohx13 = ohx1[:].rearrange("p (c e) -> p c e", e=NE)
    ohx23 = ohx2[:].rearrange("p (c e) -> p c e", e=NE)
    l23 = l2t[:].rearrange("p (c e) -> p c e", e=NE)
    ee3 = ee[:].rearrange("p (c e) -> p c e", e=NE)
    qt3 = qtmp[:].rearrange("p (c e) -> p c e", e=NE)
    oq2 = ohseq[:].rearrange("p (e s) -> p e s", s=64)
    po2 = posi[:].rearrange("p (e s) -> p e s", s=64)

    S.emit("dve", lambda eng: [eng.reduce_max(m1[:], LG3, axis=AX)],
           reads=[B_LG], writes=[B_m1], selfwait=True)
    S.emit("dve", lambda eng: [eng.reduce_max(nm1[:], LG3, axis=AX, negate=True)],
           reads=[B_LG], writes=[B_nm1], selfwait=True)

    def _subm(eng, src3, mm, B=None):
        last = None
        for e in range(NE):
            last = eng.tensor_sub(d33[:, :, e], src3[:, :, e], mm[:])
        return last
    S.emit("dve", lambda eng: [_subm(eng, LG3, m1)], reads=[B_LG, B_m1], writes=[B_d3])
    S.emit("act", lambda eng: [eng.activation(out=ee[:], in_=d3[:], func=AF.Exp)],
           reads=[B_d3], writes=[B_ee])
    S.emit("dve", lambda eng: [eng.reduce_sum(ssum[:], ee3, axis=AX)],
           reads=[B_ee], writes=[B_ssum], selfwait=True)
    S.emit("dve", lambda eng: [eng.reciprocal(p1c[:], ssum[:])],
           reads=[B_ssum], writes=[B_p1c], selfwait=True)
    S.emit("act", lambda eng: [eng.sign(sg[:], d3[:])], reads=[B_d3], writes=[B_sg])
    S.emit("dve", lambda eng: [eng.tensor_scalar(out=oh[:], in0=sg[:], scalar1=1.0,
                                                 scalar2=None, op0=ALU.add)],
           reads=[B_sg], writes=[B_oh], selfwait=True)

    def _cand(eng):
        last = None
        for e in range(NE):
            last = eng.tensor_scalar(out=cand3[:, :, e], in0=oh3[:, :, e],
                                     scalar1=float(e - 99), scalar2=99.0,
                                     op0=ALU.mult, op1=ALU.add)
        return last
    S.emit("dve", lambda eng: [_cand(eng)], reads=[B_oh], writes=[B_cand], selfwait=True)
    S.emit("dve", lambda eng: [eng.tensor_reduce(idx1[:], cand3, axis=AX, op=ALU.min)],
           reads=[B_cand], writes=[B_idx1], selfwait=True)

    def _ohx(eng, idx, dst3):
        last = None
        for e in range(NE):
            last = eng.tensor_scalar(out=dst3[:, :, e], in0=idx[:], scalar1=float(-e),
                                     scalar2=None, op0=ALU.add)
        return last
    S.emit("dve", lambda eng: [_ohx(eng, idx1, ohx13)], reads=[B_idx1], writes=[B_ohx1], selfwait=True)
    S.emit("act", lambda eng: [eng.square(ohx1[:], ohx1[:])], reads=[B_ohx1], writes=[B_ohx1])
    S.emit("dve", lambda eng: [eng.tensor_single_scalar(ohx1[:], ohx1[:], 0.5, ALU.is_lt)],
           reads=[B_ohx1], writes=[B_ohx1], selfwait=True)
    S.emit("dve", lambda eng: [eng.scalar_tensor_tensor(out=l2t[:], in0=ohx1[:],
                                                        scalar=-1e30, in1=LG[:],
                                                        op0=ALU.mult, op1=ALU.add)],
           reads=[B_ohx1, B_LG], writes=[B_l2t], selfwait=True)
    S.emit("dve", lambda eng: [eng.reduce_max(m2[:], l23, axis=AX)],
           reads=[B_l2t], writes=[B_m2], selfwait=True)
    S.emit("dve", lambda eng: [_subm(eng, l23, m2)], reads=[B_l2t, B_m2], writes=[B_d3])
    S.emit("act", lambda eng: [eng.sign(sg[:], d3[:])], reads=[B_d3], writes=[B_sg])
    S.emit("dve", lambda eng: [eng.tensor_scalar(out=oh[:], in0=sg[:], scalar1=1.0,
                                                 scalar2=None, op0=ALU.add)],
           reads=[B_sg], writes=[B_oh], selfwait=True)
    S.emit("dve", lambda eng: [_cand(eng)], reads=[B_oh], writes=[B_cand], selfwait=True)
    S.emit("dve", lambda eng: [eng.tensor_reduce(idx2[:], cand3, axis=AX, op=ALU.min)],
           reads=[B_cand], writes=[B_idx2], selfwait=True)
    S.emit("dve", lambda eng: [_ohx(eng, idx2, ohx23)], reads=[B_idx2], writes=[B_ohx2], selfwait=True)
    S.emit("act", lambda eng: [eng.square(ohx2[:], ohx2[:])], reads=[B_ohx2], writes=[B_ohx2])
    S.emit("dve", lambda eng: [eng.tensor_single_scalar(ohx2[:], ohx2[:], 0.5, ALU.is_lt)],
           reads=[B_ohx2], writes=[B_ohx2], selfwait=True)
    S.emit("dve", lambda eng: [eng.tensor_sub(m2[:], m2[:], m1[:])],
           reads=[B_m1], writes=[B_m2], selfwait=True)
    S.emit("act", lambda eng: [eng.activation(out=m2[:], in_=m2[:], func=AF.Exp)],
           reads=[B_m2], writes=[B_m2])
    S.emit("dve", lambda eng: [eng.tensor_mul(m2[:], m2[:], p1c[:])],
           reads=[B_p1c], writes=[B_m2], selfwait=True)
    def _ilv(eng):
        last = None
        for e in range(NE):
            src1 = ohx13[:, :, e]
            src2 = ohx23[:, :, e]
            d1 = bass.AP(oq2.tensor, oq2.offset + e * 64, [oq2.ap[0], [2, 32]])
            d2 = bass.AP(oq2.tensor, oq2.offset + e * 64 + 1, [oq2.ap[0], [2, 32]])
            eng.tensor_copy(d1, src1)
            last = eng.tensor_copy(d2, src2)
        return last
    S.emit("dve", lambda eng: [_ilv(eng)], reads=[B_ohx1, B_ohx2], writes=[B_ohseq], selfwait=True)
    S.emit("dve", lambda eng: [eng.memset(zer64[:], 0.0)], writes=[B_zer64], selfwait=True)
    S.emit("dve", lambda eng: [eng.reduce_sum(Spe[:], oq2, axis=AX)],
           reads=[B_ohseq], writes=[B_Spe], selfwait=True)
    S.emit("pe", lambda eng: [eng.matmul(PS[2][:, 0:8], LT_sb[:], Spe[:],
                                         start=True, stop=True)],
           reads=[B_LT, B_Spe], writes=[B_PS[2]])
    S.emit("dve", lambda eng: [eng.tensor_copy(P0sb[:], PS[2][:, 0:8])],
           reads=[B_PS[2]], writes=[B_P0sb], selfwait=True)
    for e in range(NE):
        S.emit("dve", (lambda e=e: (lambda eng: [
            eng.tensor_tensor_scan(out=po2[:, e], data0=oq2[:, e], data1=zer64[:],
                                   initial=P0sb[:, e:e + 1], op0=ALU.add, op1=ALU.add)
        ]))(), reads=[B_ohseq, B_zer64, B_P0sb], writes=[B_posi], selfwait=True)

    def _posk(eng, which, dst):
        last = None
        for e in range(NE):
            srcpos = bass.AP(po2.tensor, po2.offset + e * 64 + which, [po2.ap[0], [2, 32]])
            src_oh = (ohx13 if which == 0 else ohx23)[:, :, e]
            last = eng.tensor_mul(qt3[:, :, e], src_oh, srcpos)
        return last
    S.emit("dve", lambda eng: [_posk(eng, 0, pos1)], reads=[B_posi, B_ohx1], writes=[B_qtmp], selfwait=True)
    S.emit("dve", lambda eng: [eng.reduce_sum(pos1[:], qt3, axis=AX)],
           reads=[B_qtmp], writes=[B_pos1], selfwait=True)
    S.emit("dve", lambda eng: [_posk(eng, 1, pos2)], reads=[B_posi, B_ohx2], writes=[B_qtmp], selfwait=True)
    S.emit("dve", lambda eng: [eng.reduce_sum(pos2[:], qt3, axis=AX)],
           reads=[B_qtmp], writes=[B_pos2], selfwait=True)

    def _mkdsl(idxk, posk, pk_src, B_idxk, B_posk, B_pk_src, dsl_i, pm_dst, B_dsl_i, B_pm):
        S.emit("dve", lambda eng: [eng.tensor_single_scalar(keep[:], posk[:],
                                                            float(CAP) + 0.5, ALU.is_lt)],
               reads=[B_posk], writes=[B_keep], selfwait=True)
        S.emit("dve", lambda eng: [eng.tensor_mul(pm_dst[:], pk_src[:], keep[:])],
               reads=[B_pk_src, B_keep], writes=[B_pm], selfwait=True)
        S.emit("dve", lambda eng: [eng.tensor_scalar(out=dslf[:], in0=idxk[:],
                                                     scalar1=float(CAP), scalar2=-1.0,
                                                     op0=ALU.mult, op1=ALU.add)],
               reads=[B_idxk], writes=[B_dslf], selfwait=True)
        S.emit("dve", lambda eng: [eng.tensor_add(dslf[:], dslf[:], posk[:])],
               reads=[B_posk], writes=[B_dslf], selfwait=True)
        S.emit("dve", lambda eng: [eng.tensor_mul(dslf[:], dslf[:], keep[:])],
               reads=[B_keep], writes=[B_dslf], selfwait=True)
        S.emit("dve", lambda eng: [eng.tensor_scalar(out=qtmp[:, 0:NCT], in0=keep[:],
                                                     scalar1=-2e6, scalar2=2e6,
                                                     op0=ALU.mult, op1=ALU.add)],
               reads=[B_keep], writes=[B_qtmp], selfwait=True)
        S.emit("dve", lambda eng: [eng.tensor_add(dslf[:], dslf[:], qtmp[:, 0:NCT])],
               reads=[B_qtmp], writes=[B_dslf], selfwait=True)
        S.emit("dve", lambda eng: [eng.tensor_copy(dsl_i[:], dslf[:])],
               reads=[B_dslf], writes=[B_dsl_i], selfwait=True)

    _mkdsl(idx1, pos1, p1c, B_idx1, B_pos1, B_p1c, dsl1_sb, pm1_sb, B_dsl1, B_pm1)
    _mkdsl(idx2, pos2, m2, B_idx2, B_pos2, B_m2, dsl2_sb, pm2_sb, B_dsl2, B_pm2)

    if DEBUG_TAPS:
        def _dsltap(eng):
            return [eng.dma_start(out=dsl_tap[:, 0:32], in_=dsl1_sb[:]),
                    eng.dma_start(out=dsl_tap[:, 32:64], in_=dsl2_sb[:]),
                    eng.dma_start(out=pm_tap[:, 0:32], in_=pm1_sb[:]),
                    eng.dma_start(out=pm_tap[:, 32:64], in_=pm2_sb[:])]
        S.emit("sync", _dsltap, reads=[B_dsl1, B_dsl2, B_pm1, B_pm2],
               writes=[B_dsl_tap, B_pm_tap], dma=True, ndma=4)

    for c in range(NCT):
        mrows = xn_tok_mir.rearrange("(p c) d -> c p d", c=NCT)[c]
        S.emit("sync", (lambda mr=mrows: (lambda eng: [eng.dma_start(out=xnsb[:], in_=mr)]))(),
               reads=[B_xn_tok_mir], writes=[B_xnsb], dma=True)
        for (dsl_i, B_dsl_i) in ((dsl1_sb, B_dsl1), (dsl2_sb, B_dsl2)):
            S.emit("pool", (lambda c=c, dsl_i=dsl_i: (lambda eng: [
                eng.indirect_dma_start(
                    out=disp_dram, out_offset=IndirectOffsetOnAxis(ap=dsl_i[:, c:c + 1], axis=0),
                    in_=xnsb[:], in_offset=None,
                    bounds_check=BCREG["r"], oob_is_err=False)
            ]))(), reads=[B_xnsb, B_dsl_i], writes=[B_disp_dram], dma=True)

    rt.close()
    p1.close()
    p0.close()
    S.barrier()

    p2 = ExitStack()
    xnch, B_xnch = sb("xnch", (128, NKD * TC), BF16, p2)
    wslab = []
    B_wslab = []
    for i in range(2):
        t, b = sb("wslab%d" % i, (128, 8 * 128), BF16, p2)
        wslab.append(t)
        B_wslab.append(b)
    xc, B_xc = sb("xc", (128, NJ * TC), BF16, p2)
    Ball, B_Ball = sb("Ball", (128, DS * TC), BF16, p2)
    Call, B_Call = sb("Call", (128, DS * TC), BF16, p2)
    dtlow, B_dtlow = sb("dtlow", (64, TC), BF16, p2)
    Brow, B_Brow = sb("Brow", (16, TC), BF16, p2)
    Crow, B_Crow = sb("Crow", (16, TC), BF16, p2)
    xin = []
    B_xin = []
    for i in range(2):
        t, b = sb("xin%d" % i, (128, TC + 3), BF16, p2)
        xin.append(t)
        B_xin.append(b)
    SCRB = []
    B_SCRB = []
    for i in range(3):
        t, b = sb("scrb%d" % i, (128, TC), BF16, p2)
        SCRB.append(t)
        B_SCRB.append(b)
    dtj = []
    B_dtj = []
    silz = []
    B_silz = []
    for i in range(2):
        t, b = sb("dtj%d" % i, (128, TC), BF16, p2)
        dtj.append(t)
        B_dtj.append(b)
        t, b = sb("silz%d" % i, (128, TC), BF16, p2)
        silz.append(t)
        B_silz.append(b)
    av = []
    B_av = []
    wv = []
    B_wv = []
    uv = []
    B_uv = []
    hv = []
    B_hv = []
    pv = []
    B_pv = []
    dg = []
    B_dg = []
    for i in range(1):
        t, b = sb("wv%d" % i, (128, TC), BF16, p2)
        wv.append(t)
        B_wv.append(b)
        t, b = sb("hv%d" % i, (128, TC), BF16, p2)
        hv.append(t)
        B_hv.append(b)
    for i in range(2):
        t, b = sb("av%d" % i, (128, TC), F32, p2)
        av.append(t)
        B_av.append(b)
    for i in range(4):
        t, b = sb("pv%d" % i, (128, TC), BF16, p2)
        pv.append(t)
        B_pv.append(b)
        t, b = sb("dg%d" % i, (128, 128), BF16, p2)
        dg.append(t)
        B_dg.append(b)

    B_gscol = [Buf("gs%d" % i) for i in range(NJ * DS)]
    S.emit("dve", lambda eng: [eng.memset(gstate[:], 0.0)],
           writes=[B_gstate] + B_gscol)
    S.emit("dve", lambda eng: [eng.memset(xinst[:], 0.0)], writes=[B_xinst])

    dspr = []
    B_dspr = []
    for i in range(2):
        t, b = sb("dspr%d" % i, (128, DM), BF16, p2)
        dspr.append(t)
        B_dspr.append(b)
    dispTcc, B_dispTcc = sb("dispTcc", (128, NKD * 512), BF16, p2)
    w1r = []
    B_w1r = []
    w3r = []
    B_w3r = []
    silcc = []
    B_silcc = []
    h3c = []
    B_h3c = []
    gcc = []
    B_gcc = []
    for i in range(2):
        t, b = sb("w1r%d" % i, (128, 8 * 128), BF16, p2)
        w1r.append(t)
        B_w1r.append(b)
        t, b = sb("w3r%d" % i, (128, 8 * 128), BF16, p2)
        w3r.append(t)
        B_w3r.append(b)
    for i in range(3):
        t, b = sb("silcc%d" % i, (128, 512), BF16, p2)
        silcc.append(t)
        B_silcc.append(b)
        t, b = sb("h3c%d" % i, (128, 512), BF16, p2)
        h3c.append(t)
        B_h3c.append(b)
        t, b = sb("gcc%d" % i, (128, 512), BF16, p2)
        gcc.append(t)
        B_gcc.append(b)

    UH = [0]
    MOEA = []
    for e in range(NE):
        for cc in range(3):
            n0 = cc * 512
            ccw = min(CAP, n0 + 512) - n0
            qlist = list(range(cc * 4, min(NCAP, cc * 4 + 4)))
            for qi, q in enumerate(qlist):
                def trans_unit(e=e, q=q, qi=qi, ccw=ccw):
                    dq = q % 2
                    S.emit("sync", (lambda e=e, q=q, dq=dq: (lambda eng: [
                        eng.dma_start(out=dspr[dq][:],
                                      in_=disp_dram[e * CAP + q * 128: e * CAP + (q + 1) * 128, :])]))(),
                           reads=[B_disp_dram], writes=[B_dspr[dq]], dma=True)
                    for half in range(2):
                        def _tr(eng, dq=dq, half=half):
                            last = None
                            for k in range(4):
                                kd = 4 * half + k
                                last = eng.transpose(PSB[:, k * 128:(k + 1) * 128],
                                                     dspr[dq][:, kd * 128:(kd + 1) * 128],
                                                     identb[:])
                            return last
                        S.emit("pe", _tr, reads=[B_dspr[dq], B_identb], writes=[B_PSB])
                        def _trev(eng, half=half, qi=qi, ccw=ccw):
                            last = None
                            for k in range(4):
                                kd = 4 * half + k
                                last = eng.copy(
                                    dispTcc[:, kd * ccw + qi * 128: kd * ccw + qi * 128 + 128],
                                    PSB[:, k * 128:(k + 1) * 128])
                            return last
                        S.emit("act", _trev, reads=[B_PSB], writes=[B_dispTcc])
                    return None
                MOEA.append(trans_unit)
            for ft in range(NF):
                def h13_unit(e=e, cc=cc, ft=ft, n0=n0, ccw=ccw):
                    wr = UH[0] % 2
                    ur = UH[0] % 3
                    UH[0] += 1
                    S.emit("sync", (lambda e=e, ft=ft, wr=wr: (lambda eng: [
                        eng.dma_start(out=w1r[wr][:], in_=w1_dev[e, ft]),
                        eng.dma_start(out=w3r[wr][:], in_=w3_dev[e, ft])]))(),
                           writes=[B_w1r[wr], B_w3r[wr]], dma=True, ndma=2)
                    def _h1(eng, wr=wr, ccw=ccw):
                        last = None
                        for kd in range(NKD):
                            last = eng.matmul(PS[5][:, 0:ccw],
                                              w1r[wr][:, kd * 128:(kd + 1) * 128],
                                              dispTcc[:, kd * ccw: kd * ccw + ccw],
                                              start=(kd == 0), stop=(kd == NKD - 1))
                        return last
                    S.emit("pe", _h1, reads=[B_w1r[wr], B_dispTcc], writes=[B_PS[5]])
                    S.emit("act", (lambda ccw=ccw, ur=ur: (lambda eng: [
                        eng.activation(out=silcc[ur][:, 0:ccw], in_=PS[5][:, 0:ccw],
                                       func=AF.Silu)]))(),
                           reads=[B_PS[5]], writes=[B_silcc[ur]])
                    def _h3(eng, wr=wr, ccw=ccw):
                        last = None
                        for kd in range(NKD):
                            last = eng.matmul(PS[6][:, 0:ccw],
                                              w3r[wr][:, kd * 128:(kd + 1) * 128],
                                              dispTcc[:, kd * ccw: kd * ccw + ccw],
                                              start=(kd == 0), stop=(kd == NKD - 1))
                        return last
                    S.emit("pe", _h3, reads=[B_w3r[wr], B_dispTcc], writes=[B_PS[6]])
                    S.emit("act", (lambda ccw=ccw, ur=ur: (lambda eng: [
                        eng.copy(h3c[ur][:, 0:ccw], PS[6][:, 0:ccw])]))(),
                           reads=[B_PS[6]], writes=[B_h3c[ur]])
                    def tail(e=e, ft=ft, n0=n0, ccw=ccw, ur=ur):
                        S.emit("dve", (lambda ccw=ccw, ur=ur: (lambda eng: [
                            eng.tensor_mul(gcc[ur][:, 0:ccw], h3c[ur][:, 0:ccw],
                                           silcc[ur][:, 0:ccw])]))(),
                               reads=[B_h3c[ur], B_silcc[ur]], writes=[B_gcc[ur]])
                        S.emit("pool", (lambda e=e, ft=ft, n0=n0, ccw=ccw, ur=ur: (lambda eng: [
                            eng.dma_start(out=g_dram[e, ft, :, n0:n0 + ccw],
                                          in_=gcc[ur][:, 0:ccw])]))(),
                               reads=[B_gcc[ur]], writes=[B_g_dram], dma=True)
                    return tail
                MOEA.append(h13_unit)
    MOEA_POS = [0]
    PENDING_TAILS = []

    def drain_pe(k):
        while k > 0 and MOEA_POS[0] < len(MOEA):
            t = MOEA[MOEA_POS[0]]()
            if t is not None:
                PENDING_TAILS.append(t)
            MOEA_POS[0] += 1
            k -= 1

    def flush_tails():
        for t in PENDING_TAILS:
            t()
        PENDING_TAILS.clear()

    WCTR = [0]
    for ch in range(NCH):
        c0 = ch * TC
        def _ldxn(eng, c0=c0):
            ins = []
            for kd in range(NKD):
                ins.append(eng.dma_start(out=xnch[:, kd * TC:(kd + 1) * TC],
                                         in_=xn_dmaj_mir[kd * 128:(kd + 1) * 128, c0:c0 + TC]))
            return ins
        S.emit("sync", _ldxn, reads=[B_xn_dmaj_mir], writes=[B_xnch], dma=True, ndma=NKD)

        for J in range(NJ):
            slab = WCTR[0] % 2
            WCTR[0] += 1
            S.emit("sync", (lambda J=J, slab=slab: (lambda eng: [
                eng.dma_start(out=wslab[slab][:], in_=w_inproj[J])]))(),
                   writes=[B_wslab[slab]], dma=True)
            pj = J % 2
            def _mm(eng, J=J, slab=slab, pj=pj):
                last = None
                for kd in range(NKD):
                    for nn in range(TC // 512):
                        last = eng.matmul(PS[2 * pj + nn][:],
                                          wslab[slab][:, kd * 128:(kd + 1) * 128],
                                          xnch[:].rearrange("p (k t) -> p k t", k=NKD)[:, kd, nn * 512:(nn + 1) * 512],
                                          start=(kd == 0), stop=(kd == NKD - 1))
                return last
            S.emit("pe", _mm, reads=[B_wslab[slab], B_xnch],
                   writes=[B_PS[2 * pj], B_PS[2 * pj + 1]])
            xr = J % 2
            def _evx(eng, J=J, pj=pj, xr=xr):
                eng.add(xin[xr][:, 3:3 + 512], PS[2 * pj][:], bin_sb[:, J:J + 1])
                return eng.add(xin[xr][:, 3 + 512:3 + 1024], PS[2 * pj + 1][:], bin_sb[:, J:J + 1])
            S.emit("act", _evx, reads=[B_PS[2 * pj], B_PS[2 * pj + 1], B_bin],
                   writes=[B_xin[xr]])
            S.emit("dve", (lambda J=J, xr=xr: (lambda eng: [
                eng.tensor_copy(xin[xr][:, 0:3], xinst[:, J * 4:J * 4 + 3])]))(),
                   reads=[B_xinst], writes=[B_xin[xr]])
            def _cprod(eng, J=J, xr=xr):
                last = None
                for k in range(3):
                    last = eng.mul(SCRB[k][:], xin[xr][:, k:k + TC], cw_sb[:, J * 4 + k:J * 4 + k + 1])
                return last
            S.emit("act", _cprod, reads=[B_xin[xr], B_cw],
                   writes=[B_SCRB[0], B_SCRB[1], B_SCRB[2]])
            def _cadd(eng):
                eng.tensor_add(SCRB[0][:], SCRB[0][:], SCRB[1][:])
                return eng.tensor_add(SCRB[0][:], SCRB[0][:], SCRB[2][:])
            S.emit("dve", _cadd, reads=[B_SCRB[1], B_SCRB[2]], writes=[B_SCRB[0]])
            S.emit("act", (lambda J=J, xr=xr: (lambda eng: [
                eng.mul(SCRB[1][:], xin[xr][:, 3:3 + TC], cw_sb[:, J * 4 + 3:J * 4 + 4])]))(),
                   reads=[B_xin[xr], B_cw], writes=[B_SCRB[1]])
            S.emit("dve", lambda eng: [eng.tensor_add(SCRB[0][:], SCRB[0][:], SCRB[1][:])],
                   reads=[B_SCRB[1]], writes=[B_SCRB[0]])
            S.emit("act", (lambda J=J: (lambda eng: [
                eng.activation(out=xc[:, J * TC:(J + 1) * TC], in_=SCRB[0][:], func=AF.Silu,
                               bias=cb_sb[:, J:J + 1])]))(),
                   reads=[B_SCRB[0], B_cb], writes=[B_xc])
            S.emit("dve", (lambda J=J, xr=xr: (lambda eng: [
                eng.tensor_copy(xinst[:, J * 4:J * 4 + 3], xin[xr][:, TC:TC + 3])]))(),
                   reads=[B_xin[xr]], writes=[B_xinst])
            drain_pe(1)
            flush_tails()

        def _xproj(eng):
            last = None
            for kk in range(16):
                for nn in range(TC // 512):
                    last = eng.matmul(PS[nn][0:80, :],
                                      wxp1_sb[:, kk * 80:(kk + 1) * 80],
                                      xc[:].rearrange("p (j t) -> p j t", j=NJ)[:, kk, nn * 512:(nn + 1) * 512],
                                      start=(kk == 0), stop=(kk == 15))
            return last
        S.emit("pe", _xproj, reads=[B_wxp1, B_xc], writes=[B_PS[0], B_PS[1]])
        def _evxp(eng):
            eng.copy(dtlow[:, 0:512], PS[0][0:64, :])
            eng.copy(dtlow[:, 512:1024], PS[1][0:64, :])
            eng.copy(Brow[:, 0:512], PS[0][64:80, :])
            return eng.copy(Brow[:, 512:1024], PS[1][64:80, :])
        S.emit("act", _evxp, reads=[B_PS[0], B_PS[1]], writes=[B_dtlow, B_Brow])
        def _xprojC(eng):
            last = None
            for kk in range(16):
                for nn in range(TC // 512):
                    last = eng.matmul(PS[2 + nn][0:16, :],
                                      wxpC_sb[:, kk * 16:(kk + 1) * 16],
                                      xc[:].rearrange("p (j t) -> p j t", j=NJ)[:, kk, nn * 512:(nn + 1) * 512],
                                      start=(kk == 0), stop=(kk == 15))
            return last
        S.emit("pe", _xprojC, reads=[B_wxpC, B_xc], writes=[B_PS[2], B_PS[3]])
        def _evC(eng):
            eng.copy(Crow[:, 0:512], PS[2][0:16, :])
            return eng.copy(Crow[:, 512:1024], PS[3][0:16, :])
        S.emit("act", _evC, reads=[B_PS[2], B_PS[3]], writes=[B_Crow])

        for s in range(DS):
            for (row, Ba, dstall, B_dstall) in ((Brow, B_Brow, Ball, B_Ball),
                                                (Crow, B_Crow, Call, B_Call)):
                def _bc(eng, s=s, row=row):
                    last = None
                    for nn in range(TC // 512):
                        last = eng.matmul(PS[2 + nn][:], BSEL_sb[:, s * 128:(s + 1) * 128],
                                          row[:, nn * 512:(nn + 1) * 512],
                                          start=True, stop=True)
                    return last
                S.emit("pe", _bc, reads=[Ba, B_BSEL], writes=[B_PS[2], B_PS[3]])
                def _bce(eng, s=s, dstall=dstall):
                    eng.copy(dstall[:, s * TC:s * TC + 512], PS[2][:])
                    return eng.copy(dstall[:, s * TC + 512:(s + 1) * TC], PS[3][:])
                S.emit("act", _bce, reads=[B_PS[2], B_PS[3]], writes=[B_dstall])

        def prep_J(J):
            pj = J % 2
            for nn in range(TC // 512):
                S.emit("pe", (lambda J=J, nn=nn: (lambda eng: [
                    eng.matmul(PS[4][:], wdt_sb[:, J * 128:(J + 1) * 128],
                               dtlow[:, nn * 512:(nn + 1) * 512],
                               start=True, stop=True)]))(),
                       reads=[B_wdt, B_dtlow], writes=[B_PS[4]])
                S.emit("act", (lambda J=J, nn=nn: (lambda eng: [
                    eng.activation(out=SCRB[2][:, nn * 512:(nn + 1) * 512], in_=PS[4][:],
                                   func=AF.Exp, bias=bdt_sb[:, J:J + 1])]))(),
                       reads=[B_PS[4], B_bdt], writes=[B_SCRB[2]])
            S.emit("act", (lambda pj=pj: (lambda eng: [
                eng.activation(out=dtj[pj][:], in_=SCRB[2][:], func=AF.Ln, bias=1.0)]))(),
                   reads=[B_SCRB[2]], writes=[B_dtj[pj]])

            slab = WCTR[0] % 2
            WCTR[0] += 1
            S.emit("sync", (lambda J=J, slab=slab: (lambda eng: [
                eng.dma_start(out=wslab[slab][:], in_=w_inproj[16 + J])]))(),
                   writes=[B_wslab[slab]], dma=True)
            def _zmm(eng, J=J, slab=slab, pj=pj):
                last = None
                for kd in range(NKD):
                    for nn in range(TC // 512):
                        last = eng.matmul(PS[2 * pj + nn][:],
                                          wslab[slab][:, kd * 128:(kd + 1) * 128],
                                          xnch[:].rearrange("p (k t) -> p k t", k=NKD)[:, kd, nn * 512:(nn + 1) * 512],
                                          start=(kd == 0), stop=(kd == NKD - 1))
                return last
            S.emit("pe", _zmm, reads=[B_wslab[slab], B_xnch],
                   writes=[B_PS[2 * pj], B_PS[2 * pj + 1]])
            def _zev(eng, J=J, pj=pj):
                eng.activation(out=silz[pj][:, 0:512], in_=PS[2 * pj][:], func=AF.Silu,
                               bias=bin_sb[:, 16 + J:17 + J])
                return eng.activation(out=silz[pj][:, 512:1024], in_=PS[2 * pj + 1][:],
                                      func=AF.Silu, bias=bin_sb[:, 16 + J:17 + J])
            S.emit("act", _zev, reads=[B_PS[2 * pj], B_PS[2 * pj + 1], B_bin],
                   writes=[B_silz[pj]])

        def s_body(J, s):
            pj = J % 2
            sr = s % 2
            s4 = s % 4
            col = J * DS + s
            S.emit("dve", (lambda J=J, s=s: (lambda eng: [
                eng.tensor_mul(wv[0][:], xc[:, J * TC:(J + 1) * TC],
                               Ball[:, s * TC:(s + 1) * TC])]))(),
                   reads=[B_xc, B_Ball], writes=[B_wv[0]])
            s2 = s % 2
            S.emit("dve", (lambda s2=s2: (lambda eng: [
                eng.scalar_tensor_tensor(out=wv[0][:], in0=av[s2][:], scalar=-1.0,
                                         in1=wv[0][:], op0=ALU.add, op1=ALU.mult)]))(),
                   reads=[B_av[s2]], writes=[B_wv[0]])
            S.emit("dve", (lambda s2=s2, col=col: (lambda eng: [
                eng.tensor_tensor_scan(out=hv[0][:], data0=av[s2][:], data1=wv[0][:],
                                       initial=gstate[:, col:col + 1],
                                       op0=ALU.mult, op1=ALU.add)]))(),
                   reads=[B_av[s2], B_gscol[col]], writes=[B_hv[0]],
                   selfwait=True)
            S.emit("dve", (lambda col=col: (lambda eng: [
                eng.tensor_copy(gstate[:, col:col + 1], hv[0][:, TC - 1:TC])]))(),
                   reads=[B_hv[0]], writes=[B_gscol[col]])
            S.emit("dve", (lambda s=s, s4=s4: (lambda eng: [
                eng.tensor_mul(pv[s4][:], hv[0][:], Call[:, s * TC:(s + 1) * TC])]))(),
                   reads=[B_hv[0], B_Call], writes=[B_pv[s4]])
            def _yred(eng, s4=s4, s=s, pj=pj):
                last = None
                for nn in range(TC // 512):
                    last = eng.matmul(PS[2 * pj + nn][:], dg[s4][:],
                                      pv[s4][:, nn * 512:(nn + 1) * 512],
                                      start=(s == 0), stop=(s == DS - 1))
                return last
            S.emit("pe", _yred, reads=[B_dg[s4], B_pv[s4]],
                   writes=[B_PS[2 * pj], B_PS[2 * pj + 1]])

        prep_J(0)
        for J in range(NJ):
            pj = J % 2
            for g in range(4):
                for s in range(4 * g, 4 * g + 4):
                    s2 = s % 2
                    s4 = s % 4
                    col = J * DS + s
                    S.emit("act", (lambda pj=pj, s2=s2, col=col: (lambda eng: [
                        eng.activation(out=av[s2][:], in_=dtj[pj][:], func=AF.Exp,
                                       scale=A_sb[:, col:col + 1])]))(),
                           reads=[B_dtj[pj], B_A], writes=[B_av[s2]])
                    S.emit("act", (lambda s4=s4, col=col: (lambda eng: [
                        eng.mul(dg[s4][:], identb[:], invA_sb[:, col:col + 1])]))(),
                           reads=[B_identb, B_invA], writes=[B_dg[s4]])
                drain_pe(2 if g == 3 else 3)
                for s in range(4 * g, 4 * g + 4):
                    s_body(J, s)
                flush_tails()
                if g == 0 and J + 1 < NJ:
                    prep_J(J + 1)

            def _yev(eng, J=J, pj=pj):
                eng.scalar_tensor_tensor(out=SCRB[0][:, 0:512], in0=xc[:, J * TC:J * TC + 512],
                                         scalar=Dc_sb[:, J:J + 1], in1=PS[2 * pj][:],
                                         op0=ALU.mult, op1=ALU.add)
                eng.scalar_tensor_tensor(out=SCRB[0][:, 512:1024],
                                         in0=xc[:, J * TC + 512:(J + 1) * TC],
                                         scalar=Dc_sb[:, J:J + 1], in1=PS[2 * pj + 1][:],
                                         op0=ALU.mult, op1=ALU.add)
                return eng.tensor_mul(SCRB[0][:], SCRB[0][:], silz[pj][:])
            S.emit("dve", _yev, reads=[B_PS[2 * pj], B_PS[2 * pj + 1], B_xc, B_Dc,
                                       B_silz[pj]], writes=[B_SCRB[0]])
            S.emit("sync", (lambda ch=ch, J=J: (lambda eng: [
                eng.dma_start(out=yg_mir[ch, J], in_=SCRB[0][:])]))(),
                   reads=[B_SCRB[0]], writes=[B_yg_mir], dma=True)

    drain_pe(10 ** 9)
    flush_tails()
    p2.close()
    S.barrier()

    p2b = ExitStack()
    yacF, B_yacF = sb("yacF", (128, (TC // 128) * DM), BF16, p2b)
    wsl2 = []
    B_wsl2 = []
    for i in range(4):
        t, b = sb("wsl2_%d" % i, (128, 8 * 128), BF16, p2b)
        wsl2.append(t)
        B_wsl2.append(b)
    hv2 = []
    B_hv2 = []
    for i in range(2):
        t, b = sb("hv2_%d" % i, (128, TC), BF16, p2b)
        hv2.append(t)
        B_hv2.append(b)
    mo2, B_mo2 = sb("mo2", (128, TC), F32, p2b)
    NTB = TC // 128
    for ch in range(NCH):
        def _ldx(eng, ch=ch):
            return [eng.dma_start(out=yacF[:, tb * DM:(tb + 1) * DM],
                                  in_=x_hbm[ch * TC + tb * 128: ch * TC + (tb + 1) * 128, :])
                    for tb in range(NTB)]
        S.emit("pool", _ldx, writes=[B_yacF], dma=True, ndma=NTB)
        for mp in range(4):
            def _ldw_op(eng, mp=mp):
                return [eng.dma_start(out=wsl2[0][:], in_=w_outproj[2 * mp, :, 0:1024]),
                        eng.dma_start(out=wsl2[1][:], in_=w_outproj[2 * mp, :, 1024:2048]),
                        eng.dma_start(out=wsl2[2][:], in_=w_outproj[2 * mp + 1, :, 0:1024]),
                        eng.dma_start(out=wsl2[3][:], in_=w_outproj[2 * mp + 1, :, 1024:2048])]
            S.emit("sync", _ldw_op, writes=[B_wsl2[0], B_wsl2[1], B_wsl2[2], B_wsl2[3]],
                   dma=True, ndma=4)
            for J in range(NJ):
                sr2 = J % 2
                S.emit("sync", (lambda ch=ch, J=J, sr2=sr2: (lambda eng: [
                    eng.dma_start(out=hv2[sr2][:], in_=yg_mir[ch, J])]))(),
                       reads=[B_yg_mir], writes=[B_hv2[sr2]], dma=True)
                def _opmm(eng, J=J, sr2=sr2, mp=mp):
                    last = None
                    for half in range(2):
                        w = wsl2[2 * half + (0 if J < 8 else 1)]
                        kslice = w[:, (J % 8) * 128:((J % 8) + 1) * 128]
                        for nn in range(TC // 512):
                            last = eng.matmul(PS[2 * half + nn][:], kslice,
                                              hv2[sr2][:, nn * 512:(nn + 1) * 512],
                                              start=(J == 0), stop=(J == NJ - 1))
                    return last
                S.emit("pe", _opmm, reads=[B_wsl2[0], B_wsl2[1], B_wsl2[2], B_wsl2[3],
                                           B_hv2[sr2]],
                       writes=[B_PS[0], B_PS[1], B_PS[2], B_PS[3]])
            for half in range(2):
                m = 2 * mp + half
                def _ob(eng, m=m, half=half):
                    eng.add(mo2[:, 0:512], PS[2 * half][:], bout_sb[:, m:m + 1])
                    return eng.add(mo2[:, 512:1024], PS[2 * half + 1][:], bout_sb[:, m:m + 1])
                S.emit("act", _ob, reads=[B_PS[2 * half], B_PS[2 * half + 1], B_bout],
                       writes=[B_mo2])
                for tb in range(NTB):
                    pk = 4 + (tb % 2)
                    S.emit("pe", (lambda tb=tb, pk=pk: (lambda eng: [
                        eng.transpose(PS[pk][:, 0:128], mo2[:, tb * 128:(tb + 1) * 128],
                                      identf[:])]))(),
                           reads=[B_mo2, B_identf], writes=[B_PS[pk]])
                    S.emit("dve", (lambda m=m, pk=pk, tb=tb: (lambda eng: [
                        eng.tensor_add(yacF[:, tb * DM + m * 128: tb * DM + (m + 1) * 128],
                                       yacF[:, tb * DM + m * 128: tb * DM + (m + 1) * 128],
                                       PS[pk][:, 0:128])]))(),
                           reads=[B_PS[pk]], writes=[B_yacF])
        def _stout(eng, ch=ch):
            return [eng.dma_start(out=out_hbm[ch * TC + tb * 128: ch * TC + (tb + 1) * 128, :],
                                  in_=yacF[:, tb * DM:(tb + 1) * DM])
                    for tb in range(NTB)]
        S.emit("pool", _stout, reads=[B_yacF], writes=[B_out_hbm], dma=True, ndma=NTB)
    p2b.close()
    S.barrier()

    p3 = ExitStack()
    HW = CAP // 2
    gb2h = []
    B_gb2h = []
    for i in range(2):
        t, b = sb("gb2h%d" % i, (128, NF * HW), BF16, p3)
        gb2h.append(t)
        B_gb2h.append(b)
    w2s = []
    B_w2s = []
    for i in range(2):
        t, b = sb("w2s_%d" % i, (128, NF * DM), BF16, p3)
        w2s.append(t)
        B_w2s.append(b)
    eo = []
    B_eo = []
    for i in range(2):
        t, b = sb("eo%d" % i, (128, DM), BF16, p3)
        eo.append(t)
        B_eo.append(b)

    for e in range(NE):
        we = e % 2
        def _ldw2(eng, e=e, we=we):
            return [eng.dma_start(out=w2s[we][:, kf * DM:(kf + 1) * DM], in_=w2_dev[e, kf])
                    for kf in range(NF)]
        S.emit("sync", _ldw2, writes=[B_w2s[we]], dma=True, ndma=NF)
        for h in range(2):
            def _ldg(eng, e=e, h=h):
                return [eng.dma_start(out=gb2h[h][:, kf * HW:(kf + 1) * HW],
                                      in_=g_dram[e, kf, :, h * HW:(h + 1) * HW])
                        for kf in range(NF)]
            S.emit("sync", _ldg, reads=[B_g_dram], writes=[B_gb2h[h]], dma=True, ndma=NF)
        for q in range(NCAP):
            h = q // 5
            qh = q - 5 * h
            def _w2mm(eng, q=q, we=we, h=h, qh=qh):
                last = None
                for kf in range(NF):
                    gs = gb2h[h][:, kf * HW + qh * 128: kf * HW + (qh + 1) * 128]
                    for nn in range(2):
                        last = eng.matmul(PS[2 * (q % 2) + nn][:], gs,
                                          w2s[we][:, kf * DM + nn * 512: kf * DM + (nn + 1) * 512],
                                          start=(kf == 0), stop=(kf == NF - 1))
                return last
            S.emit("pe", _w2mm, reads=[B_gb2h[h], B_w2s[we]],
                   writes=[B_PS[2 * (q % 2)], B_PS[2 * (q % 2) + 1]])
            def _eoev(eng, q=q):
                eng.copy(eo[q % 2][:, 0:512], PS[2 * (q % 2)][:])
                return eng.copy(eo[q % 2][:, 512:1024], PS[2 * (q % 2) + 1][:])
            S.emit("act", _eoev, reads=[B_PS[2 * (q % 2)], B_PS[2 * (q % 2) + 1]],
                   writes=[B_eo[q % 2]])
            S.emit("sync", (lambda e=e, q=q: (lambda eng: [
                eng.dma_start(out=eout_dram[e * CAP + q * 128: e * CAP + (q + 1) * 128, :],
                              in_=eo[q % 2][:])]))(),
                   reads=[B_eo[q % 2]], writes=[B_eout_dram], dma=True)

    p3.close()
    S.barrier()

    p4 = ExitStack()
    acc = []
    B_acc = []
    cmb = []
    B_cmb = []
    for i in range(4):
        t, b = sb("acc%d" % i, (128, DM), F32, p4)
        acc.append(t)
        B_acc.append(b)
        t, b = sb("cmb%d" % i, (128, DM), BF16, p4)
        cmb.append(t)
        B_cmb.append(b)
        S.emit("dve", (lambda i=i: (lambda eng: [eng.memset(cmb[i][:], 0.0)]))(),
               writes=[B_cmb[i]])

    for c in range(NCT):
        ia = c % 4
        orows = out_hbm.rearrange("(p c) d -> c p d", c=NCT)[c]
        S.emit("sync", (lambda orows=orows, ia=ia: (lambda eng: [
            eng.dma_start(out=acc[ia][:], in_=orows)]))(),
               reads=[B_out_hbm], writes=[B_acc[ia]], dma=True)
        for (dsl_i, B_dsl_i, pmv, B_pmv) in ((dsl1_sb, B_dsl1, pm1_sb, B_pm1),
                                             (dsl2_sb, B_dsl2, pm2_sb, B_pm2)):
            S.emit("pool", (lambda c=c, ia=ia, dsl_i=dsl_i: (lambda eng: [
                eng.indirect_dma_start(
                    out=cmb[ia][:], out_offset=None,
                    in_=eout_dram, in_offset=IndirectOffsetOnAxis(ap=dsl_i[:, c:c + 1], axis=0),
                    bounds_check=BCREG["r"], oob_is_err=False)
            ]))(), reads=[B_eout_dram, B_dsl_i], writes=[B_cmb[ia]], dma=True)
            S.emit("dve", (lambda c=c, ia=ia, pmv=pmv: (lambda eng: [
                eng.scalar_tensor_tensor(out=acc[ia][:], in0=cmb[ia][:],
                                         scalar=pmv[:, c:c + 1], in1=acc[ia][:],
                                         op0=ALU.mult, op1=ALU.add)]))(),
                   reads=[B_cmb[ia], B_pmv], writes=[B_acc[ia]])
        S.emit("sync", (lambda orows=orows, ia=ia: (lambda eng: [
            eng.dma_start(out=orows, in_=acc[ia][:])]))(),
               reads=[B_acc[ia]], writes=[B_out_hbm], dma=True)

    S.barrier()
    p4.close()

    with nc.Block() as blk:
        def run_stream(eng, name):
            for (waits, body, sem, inc, selfwait, val) in S.streams[name]:
                for (sname, v) in waits:
                    eng.wait_ge(sems[sname], v)
                if body is None:
                    continue
                ins = body(eng)
                if not isinstance(ins, (list, tuple)):
                    ins = [ins]
                if inc >= 16:
                    assert inc == 16 * len(ins), (name, inc, len(ins))
                    for li in ins:
                        li.then_inc(sems[sem], 16)
                else:
                    ins[-1].then_inc(sems[sem], inc)
                if selfwait:
                    eng.wait_ge(sems[sem], val)

        @blk.tensor
        def _(eng):
            run_stream(eng, "pe")

        @blk.vector
        def _(eng):
            run_stream(eng, "dve")

        @blk.scalar
        def _(eng):
            run_stream(eng, "act")

        @blk.gpsimd
        def _(eng):
            run_stream(eng, "pool")

        @blk.sync
        def _(eng):
            run_stream(eng, "sync")

    es.close()
    return nc


def _host_inputs(inp):
    w = {}
    ipw = np.asarray(inp["L0_attn_in_proj_w"], np.float32)
    w["w_inproj"] = trunc_bf16(
        ipw.T.reshape(8, 128, 32, 128).transpose(2, 1, 0, 3).reshape(32, 128, 1024))
    w["b_inproj"] = np.ascontiguousarray(
        np.asarray(inp["L0_attn_in_proj_b"], np.float32).reshape(32, 128).T)
    xp = np.asarray(inp["L0_attn_x_proj_w"], np.float32)
    w["w_xproj1"] = trunc_bf16(
        xp[:80].T.reshape(16, 128, 80).transpose(1, 0, 2).reshape(128, 16 * 80))
    w["w_xprojC"] = trunc_bf16(
        xp[80:96].T.reshape(16, 128, 16).transpose(1, 0, 2).reshape(128, 16 * 16))
    w["w_dtproj"] = trunc_bf16(np.asarray(inp["L0_attn_dt_proj_w"], np.float32).T)
    w["b_dtproj"] = np.ascontiguousarray(
        np.asarray(inp["L0_attn_dt_proj_b"], np.float32).reshape(16, 128).T)
    cw = np.asarray(inp["L0_attn_conv_w"], np.float32)
    w["conv_w4"] = np.ascontiguousarray(
        cw.reshape(16, 128, 4).transpose(1, 0, 2).reshape(128, 64))
    w["conv_bc"] = np.ascontiguousarray(
        np.asarray(inp["L0_attn_conv_b"], np.float32).reshape(16, 128).T)
    al = np.asarray(inp["L0_attn_A_log"], np.float32)
    w["A_logr"] = np.ascontiguousarray(
        al.T.reshape(16, 128, 16).transpose(1, 0, 2).reshape(128, 256))
    w["D_col"] = np.ascontiguousarray(
        np.asarray(inp["L0_attn_D"], np.float32).reshape(16, 128).T)
    opw = np.asarray(inp["L0_attn_out_proj_w"], np.float32)
    w["w_outproj"] = trunc_bf16(
        opw.T.reshape(16, 128, 8, 128).transpose(2, 1, 0, 3).reshape(8, 128, 16 * 128))
    w["b_outproj"] = np.ascontiguousarray(
        np.asarray(inp["L0_attn_out_proj_b"], np.float32).reshape(8, 128).T)
    gwt = np.asarray(inp["L0_ffn_gate_w"], np.float32).T
    w["gate_wT"] = np.ascontiguousarray(
        gwt.reshape(8, 128, 8).transpose(1, 0, 2).reshape(128, 64))
    w1 = np.asarray(inp["L0_ffn_w1"], np.float32)
    w["w1_dev"] = trunc_bf16(
        w1.reshape(8, 8, 128, 28, 128).transpose(0, 3, 2, 1, 4).reshape(8, 28, 128, 1024))
    w3 = np.asarray(inp["L0_ffn_w3"], np.float32)
    w["w3_dev"] = trunc_bf16(
        w3.reshape(8, 8, 128, 28, 128).transpose(0, 3, 2, 1, 4).reshape(8, 28, 128, 1024))
    w2 = np.asarray(inp["L0_ffn_w2"], np.float32)
    w["w2_dev"] = trunc_bf16(w2.reshape(8, 28, 128, 1024))
    w["ident_f"] = np.eye(128, dtype=np.float32)
    w["ident_b"] = trunc_bf16(np.eye(128, dtype=np.float32))
    lt = np.tril(np.ones((128, 128), np.float32), -1).T
    w["LT"] = np.ascontiguousarray(lt)
    bsel = np.zeros((16, 16 * 128), np.float32)
    for s in range(16):
        bsel[s, s * 128:(s + 1) * 128] = 1.0
    w["BSEL"] = trunc_bf16(bsel)
    return w


def kernel(**inputs):
    x = np.asarray(inputs["L0_attn_x"], np.float32)
    shared = _host_inputs(inputs)
    nc = build_program()
    in_maps = []
    for c in range(NCORES):
        m = dict(shared)
        m["x"] = np.ascontiguousarray(x[c])
        in_maps.append(m)
    res = run_bass_kernel_spmd(nc, in_maps, core_ids=list(range(NCORES)))
    out = np.stack([res.results[c]["out"] for c in range(NCORES)], axis=0)
    return out.astype(np.float32)


# version 10
# speedup vs baseline: 1.2267x; 1.0422x over previous
import numpy as np
import ml_dtypes
import concourse.bass as bass
import concourse.mybir as mybir
from concourse.bass_utils import run_bass_kernel_spmd
from concourse.bass_core_rust import IndirectOffsetOnAxis
from contextlib import ExitStack

F32 = mybir.dt.float32
BF16 = mybir.dt.bfloat16
I32 = mybir.dt.int32
ALU = mybir.AluOpType
AF = mybir.ActivationFunctionType
AX = mybir.AxisListType.X
BF = ml_dtypes.bfloat16

T, DM, DI, DS, DTR = 4096, 1024, 2048, 16, 64
NE, FF, CAP = 8, 3584, 1280
NJ = DI // 128
NKD = DM // 128
NF = FF // 128
TC = 1024
NCH = T // TC
NCT = 32
NCAP = CAP // 128
EPS = 1e-5

NCORES = 8
DEBUG_TAPS = False


def trunc_bf16(a):
    a = np.ascontiguousarray(a, dtype=np.float32)
    u = a.view(np.uint16)
    hi = np.ascontiguousarray(u.reshape(-1, 2)[:, 1])
    return hi.reshape(a.shape).view(BF)


ENGINES = ("pe", "dve", "act", "pool", "sync")
SEM_OF = {"pe": "sP", "dve": "sV", "act": "sA", "pool": "sGc", "sync": "sS"}
DMA_POOL = {"sync": ["sSd%d" % i for i in range(12)],
            "pool": ["sGd%d" % i for i in range(8)],
            "act": ["sAd0"]}
ALL_SEMS = sorted(set(SEM_OF.values()) | {s for v in DMA_POOL.values() for s in v})


class Buf:
    __slots__ = ("name", "w", "r")

    def __init__(self, name):
        self.name = name
        self.w = None
        self.r = []


class Sched:
    def __init__(self):
        self.streams = {e: [] for e in ENGINES}
        self.cnt = {s: 0 for s in ALL_SEMS}
        self.waited = {e: {} for e in ENGINES}
        self.rr = {q: 0 for q in DMA_POOL}

    def _wait(self, eng, waits, sem, val):
        seen = self.waited[eng]
        if seen.get(sem, 0) >= val:
            return
        seen[sem] = val
        waits.append((sem, val))

    def emit(self, eng, body, reads=(), writes=(), dma=False, selfwait=False, ndma=1):
        waits = []
        for b in reads:
            if b.w is not None and b.w[0] != eng:
                self._wait(eng, waits, b.w[1], b.w[2])
        for b in writes:
            if b.w is not None and b.w[0] != eng:
                self._wait(eng, waits, b.w[1], b.w[2])
            for (reng, rsem, rval) in b.r:
                if reng != eng:
                    self._wait(eng, waits, rsem, rval)
        if dma:
            pool = DMA_POOL[eng]
            sem = pool[self.rr[eng] % len(pool)]
            self.rr[eng] += 1
            if self.cnt[sem] > 0:
                self._wait(eng, waits, sem, self.cnt[sem])
            inc = 16 * ndma
        else:
            sem = SEM_OF[eng]
            inc = 1
        self.cnt[sem] += inc
        val = self.cnt[sem]
        self.streams[eng].append((waits, body, sem, inc, selfwait, val))
        ev = (eng, sem, val)
        for b in reads:
            b.r.append(ev)
        for b in writes:
            b.w = ev
            b.r = []
        return ev

    def barrier(self):
        snapshot = dict(self.cnt)
        for e in ENGINES:
            waits = []
            for sem, val in snapshot.items():
                if val > 0:
                    self._wait(e, waits, sem, val)
            if waits:
                self.streams[e].append((waits, None, None, 0, False, 0))


def build_program():
    nc = bass.Bass("TRN2", target_bir_lowering=False, debug=False,
                   detect_race_conditions=False)
    S = Sched()
    es = ExitStack()

    def din(name, shape, dt):
        return nc.dram_tensor(name, list(shape), dt, kind="ExternalInput").ap()

    def dscratch(name, shape, dt):
        kind = "ExternalOutput" if DEBUG_TAPS else "Internal"
        return nc.dram_tensor(name, list(shape), dt, kind=kind).ap()

    x_hbm = din("x", (T, DM), F32)
    w_inproj = din("w_inproj", (32, 128, 8 * 128), BF16)
    b_inproj = din("b_inproj", (128, 32), F32)
    w_xproj1 = din("w_xproj1", (128, 16 * 80), BF16)
    w_xprojC = din("w_xprojC", (128, 16 * 16), BF16)
    w_dtproj = din("w_dtproj", (64, DI), BF16)
    b_dtproj = din("b_dtproj", (128, NJ), F32)
    conv_w4 = din("conv_w4", (128, NJ * 4), F32)
    conv_bc = din("conv_bc", (128, NJ), F32)
    A_logr = din("A_logr", (128, NJ * DS), F32)
    D_col = din("D_col", (128, NJ), F32)
    w_outproj = din("w_outproj", (8, 128, 16 * 128), BF16)
    b_outproj = din("b_outproj", (128, 8), F32)
    gate_wT = din("gate_wT", (128, NKD * NE), F32)
    w1_dev = din("w1_dev", (NE, NF, 128, 8 * 128), BF16)
    w3_dev = din("w3_dev", (NE, NF, 128, 8 * 128), BF16)
    w2_dev = din("w2_dev", (NE, NF, 128, DM), BF16)
    ident_f_d = din("ident_f", (128, 128), F32)
    ident_b_d = din("ident_b", (128, 128), BF16)
    LT_d = din("LT", (128, 128), F32)
    BSEL_d = din("BSEL", (16, DS * 128), BF16)

    xn_tok_mir = dscratch("xn_tok_mir", (T, DM), BF16)
    xn_dmaj_mir = dscratch("xn_dmaj_mir", (DM, T), BF16)
    yg_mir = dscratch("yg_mir", (NCH, NJ, 128, TC), BF16)
    disp_dram = dscratch("disp_dram", (NE * CAP, DM), BF16)
    g_dram = dscratch("g_dram", (NE, NF, 128, CAP), BF16)
    eout_dram = dscratch("eout_dram", (NE * CAP, DM), BF16)
    out_hbm = nc.dram_tensor("out", [T, DM], F32, kind="ExternalOutput").ap()

    B_xn_tok_mir = Buf("xn_tok_mir")
    B_xn_dmaj_mir = Buf("xn_dmaj_mir")
    B_yg_mir = Buf("yg_mir")
    B_disp_dram = Buf("disp_dram")
    B_g_dram = Buf("g_dram")
    B_eout_dram = Buf("eout_dram")
    B_out_hbm = Buf("out_hbm")

    sems = {}
    for sname in ALL_SEMS:
        sems[sname] = es.enter_context(nc.semaphore(sname))

    def sb(name, shape, dt, stack=None):
        t = (stack or es).enter_context(nc.sbuf_tensor(name, list(shape), dt))
        return t, Buf(name)

    def psum(name, shape, dt, stack=None):
        t = (stack or es).enter_context(nc.psum_tensor(name, list(shape), dt))
        return t, Buf(name)

    identf, B_identf = sb("identf", (128, 128), F32)
    identb, B_identb = sb("identb", (128, 128), BF16)

    BSEL_sb, B_BSEL = sb("BSEL_sb", (16, DS * 128), BF16)
    A_sb, B_A = sb("A_sb", (128, NJ * DS), F32)
    invA_sb, B_invA = sb("invA_sb", (128, NJ * DS), F32)
    bin_sb, B_bin = sb("bin_sb", (128, 32), F32)
    bdt_sb, B_bdt = sb("bdt_sb", (128, NJ), F32)
    cw_sb, B_cw = sb("cw_sb", (128, NJ * 4), F32)
    cb_sb, B_cb = sb("cb_sb", (128, NJ), F32)
    Dc_sb, B_Dc = sb("Dc_sb", (128, NJ), F32)
    bout_sb, B_bout = sb("bout_sb", (128, 8), F32)
    wxp1_sb, B_wxp1 = sb("wxp1_sb", (128, 16 * 80), BF16)
    wxpC_sb, B_wxpC = sb("wxpC_sb", (128, 16 * 16), BF16)
    wdt_sb, B_wdt = sb("wdt_sb", (64, DI), BF16)
    dsl1_sb, B_dsl1 = sb("dsl1", (128, 32), I32)
    dsl2_sb, B_dsl2 = sb("dsl2", (128, 32), I32)
    pm1_sb, B_pm1 = sb("pm1", (128, 32), F32)
    pm2_sb, B_pm2 = sb("pm2", (128, 32), F32)
    gstate, B_gstate = sb("gstate", (128, NJ * DS), F32)
    xinst, B_xinst = sb("xinst", (128, NJ * 4), BF16)

    cst_sb, B_cst = sb("cst_sb", (128, 8), F32)
    CONSTS = {EPS: 0, 1.0: 1, -0.5: 2, -1.0: 3, 0.0: 4}
    for v, i in CONSTS.items():
        nc.const_aps.register(F32, v, cst_sb[:, i:i + 1])

    def _fillconsts(eng):
        last = None
        for v, i in CONSTS.items():
            last = eng.memset(cst_sb[:, i:i + 1], float(v))
        return last
    S.emit("dve", _fillconsts, writes=[B_cst])

    BCREG = {}

    def _mkbc(eng):
        r = eng.alloc_register("bcreg")
        BCREG["r"] = r
        return eng.reg_mov(r, NE * CAP - 1)
    S.emit("pool", _mkbc)
    S.barrier()

    PS = []
    B_PS = []
    for i in range(7):
        t, b = psum("ps%d" % i, (128, 512), F32)
        PS.append(t)
        B_PS.append(b)
    PSB, B_PSB = psum("psb", (128, 512), BF16)

    p0 = ExitStack()
    LT_sb, B_LT = sb("LT_sb", (128, 128), F32, p0)
    gw_sb, B_gw = sb("gw_sb", (128, NKD * NE), F32, p0)
    scr0, B_scr0 = sb("scr0", (128, NJ * DS), F32, p0)

    def ldconst(dst, src, bufs):
        S.emit("sync", (lambda d=dst, s=src: (lambda eng: [eng.dma_start(out=d, in_=s)]))(),
               writes=bufs, dma=True)

    ldconst(identf[:], ident_f_d, [B_identf])
    ldconst(identb[:], ident_b_d, [B_identb])
    ldconst(LT_sb[:], LT_d, [B_LT])
    ldconst(BSEL_sb[:], BSEL_d, [B_BSEL])
    ldconst(A_sb[:], A_logr, [B_A])
    ldconst(bin_sb[:], b_inproj, [B_bin])
    ldconst(bdt_sb[:], b_dtproj, [B_bdt])
    ldconst(cw_sb[:], conv_w4, [B_cw])
    ldconst(cb_sb[:], conv_bc, [B_cb])
    ldconst(Dc_sb[:], D_col, [B_Dc])
    ldconst(bout_sb[:], b_outproj, [B_bout])
    ldconst(gw_sb[:], gate_wT, [B_gw])
    ldconst(wxp1_sb[:], w_xproj1, [B_wxp1])
    ldconst(wxpC_sb[:], w_xprojC, [B_wxpC])
    ldconst(wdt_sb[:], w_dtproj, [B_wdt])


    def _actwarm(eng):
        last = None
        for fn in (AF.Square, AF.Ln, AF.Exp, AF.Silu, AF.Sigmoid, AF.Ln, AF.Exp):
            last = eng.activation(out=cst_sb[:, 5:6], in_=cst_sb[:, 1:2], func=fn)
        last = eng.sign(cst_sb[:, 5:6], cst_sb[:, 1:2])
        return last
    S.emit("act", _actwarm, reads=[B_cst], writes=[B_cst])

    S.emit("act", lambda eng: [eng.activation(out=scr0[:], in_=A_sb[:], func=AF.Exp)],
           reads=[B_A], writes=[B_scr0], selfwait=True)
    S.emit("dve", lambda eng: [eng.tensor_scalar(out=A_sb[:], in0=scr0[:], scalar1=-1.0,
                                                 scalar2=None, op0=ALU.mult)],
           reads=[B_scr0], writes=[B_A], selfwait=True)
    S.emit("dve", lambda eng: [eng.tensor_single_scalar(invA_sb[:], scr0[:], 1e-6, ALU.is_lt)],
           reads=[B_scr0], writes=[B_invA], selfwait=True)
    S.emit("dve", lambda eng: [eng.tensor_scalar(out=scr0[:], in0=A_sb[:], scalar1=-1.0,
                                                 scalar2=1.0, op0=ALU.mult, op1=ALU.add)],
           reads=[B_A], writes=[B_scr0], selfwait=True)
    S.emit("dve", lambda eng: [eng.tensor_mul(scr0[:], scr0[:], invA_sb[:])],
           reads=[B_invA], writes=[B_scr0], selfwait=True)
    S.emit("dve", lambda eng: [eng.tensor_add(scr0[:], scr0[:], A_sb[:])],
           reads=[B_A], writes=[B_scr0], selfwait=True)
    S.emit("dve", lambda eng: [eng.reciprocal(invA_sb[:], scr0[:])],
           reads=[B_scr0], writes=[B_invA], selfwait=True)
    S.barrier()

    p1 = ExitStack()
    xnd, B_xnd = sb("xnd", (128, NKD * T), BF16, p1)
    xt, B_xt = sb("xt", (128, DM), F32, p1)
    sq, B_sq = sb("sq", (128, DM), F32, p1)
    mcol, B_mcol = sb("mcol", (128, 8), F32, p1)
    xnf, B_xnf = sb("xnf", (128, DM), F32, p1)
    xnb, B_xnb = sb("xnb", (128, DM), BF16, p1)
    xnTf, B_xnTf = sb("xnTf", (128, NKD * 128), F32, p1)
    LG, B_LG = sb("LG", (128, NCT * NE), F32, p1)
    if DEBUG_TAPS:
        mcol_tap = dscratch("mcol_tap", (128, 8), F32)
        xt_tap = dscratch("xt_tap", (128, DM), F32)
        B_mcol_tap, B_xt_tap = Buf("mcol_tap"), Buf("xt_tap")
        dsl_tap = dscratch("dsl_tap", (128, 64), I32)
        pm_tap = dscratch("pm_tap", (128, 64), F32)
        B_dsl_tap, B_pm_tap = Buf("dsl_tap"), Buf("pm_tap")

    for c in range(NCT):
        xrows = x_hbm.rearrange("(p c) d -> c p d", c=NCT)[c]
        S.emit("sync", (lambda xr=xrows: (lambda eng: [eng.dma_start(out=xt[:], in_=xr)]))(),
               reads=[], writes=[B_xt], dma=True, selfwait=True)
        S.emit("act", lambda eng: [eng.activation(out=sq[:], in_=xt[:], func=AF.Square)],
               reads=[B_xt], writes=[B_sq], selfwait=True)
        S.emit("dve", lambda eng: [eng.reduce_sum(mcol[:, 0:1], sq[:], axis=AX)],
               reads=[B_sq], writes=[B_mcol], selfwait=True)
        S.emit("dve", lambda eng: [eng.reduce_sum(mcol[:, 1:2], xt[:], axis=AX)],
               reads=[B_xt], writes=[B_mcol], selfwait=True)
        def _stats1(eng):
            eng.tensor_scalar(out=mcol[:, 1:2], in0=mcol[:, 1:2], scalar1=1.0 / DM,
                              scalar2=None, op0=ALU.mult)
            return eng.tensor_scalar(out=mcol[:, 2:3], in0=mcol[:, 0:1], scalar1=1.0 / DM,
                                     scalar2=None, op0=ALU.mult)
        S.emit("dve", lambda eng: [_stats1(eng)], reads=[], writes=[B_mcol], selfwait=True)
        S.emit("dve", lambda eng: [eng.tensor_mul(mcol[:, 3:4], mcol[:, 1:2], mcol[:, 1:2])],
               reads=[], writes=[B_mcol], selfwait=True)
        S.emit("dve", lambda eng: [eng.tensor_sub(mcol[:, 2:3], mcol[:, 2:3], mcol[:, 3:4])],
               reads=[], writes=[B_mcol], selfwait=True)
        S.emit("act", lambda eng: [eng.activation(out=mcol[:, 4:5], in_=mcol[:, 2:3],
                                                  func=AF.Ln, bias=EPS)],
               reads=[B_mcol], writes=[B_mcol], selfwait=True)
        S.emit("act", lambda eng: [eng.activation(out=mcol[:, 5:6], in_=mcol[:, 4:5],
                                                  func=AF.Exp, scale=-0.5)],
               reads=[B_mcol], writes=[B_mcol], selfwait=True)
        if DEBUG_TAPS and c == 0:
            S.emit("sync", lambda eng: [eng.dma_start(out=mcol_tap, in_=mcol[:])],
                   reads=[B_mcol], writes=[B_mcol_tap], dma=True)
            S.emit("sync", lambda eng: [eng.dma_start(out=xt_tap, in_=xt[:])],
                   reads=[B_xt], writes=[B_xt_tap], dma=True)
        S.emit("dve", lambda eng: [eng.tensor_scalar(out=xnf[:], in0=xt[:],
                                                     scalar1=mcol[:, 1:2], scalar2=mcol[:, 5:6],
                                                     op0=ALU.subtract, op1=ALU.mult)],
               reads=[B_xt, B_mcol], writes=[B_xnf], selfwait=True)
        S.emit("dve", lambda eng: [eng.tensor_copy(xnb[:], xnf[:])],
               reads=[B_xnf], writes=[B_xnb], selfwait=True)
        mrows = xn_tok_mir.rearrange("(p c) d -> c p d", c=NCT)[c]
        S.emit("sync", (lambda mr=mrows: (lambda eng: [eng.dma_start(out=mr, in_=xnb[:])]))(),
               reads=[B_xnb], writes=[B_xn_tok_mir], dma=True, selfwait=True)
        for kd in range(NKD):
            pk = kd % 2
            S.emit("pe", (lambda kd=kd, pk=pk: (lambda eng: [
                eng.transpose(PS[pk][:, 0:128], xnf[:, kd * 128:(kd + 1) * 128], identf[:])
            ]))(), reads=[B_xnf, B_identf], writes=[B_PS[pk]], selfwait=True)
            def _evac(eng, kd=kd, pk=pk, c=c):
                eng.tensor_copy(xnTf[:, kd * 128:(kd + 1) * 128], PS[pk][:, 0:128])
                dst = xnd[:].rearrange("p (k t) -> p k t", k=NKD)[:, kd]
                dst = bass.AP(dst.tensor, dst.offset + c, [dst.ap[0], [NCT, 128]])
                return eng.tensor_copy(dst, PS[pk][:, 0:128])
            S.emit("dve", (lambda f=_evac: (lambda eng: [f(eng)]))(),
                   reads=[B_PS[pk]], writes=[B_xnTf, B_xnd], selfwait=True)
        def _router(eng, c=c):
            last = None
            for kd in range(NKD):
                last = eng.matmul(PS[2][:, 0:8], xnTf[:, kd * 128:(kd + 1) * 128],
                                  gw_sb[:, kd * NE:(kd + 1) * NE],
                                  start=(kd == 0), stop=(kd == NKD - 1))
            return last
        S.emit("pe", (lambda f=_router: (lambda eng: [f(eng)]))(),
               reads=[B_xnTf, B_gw], writes=[B_PS[2]], selfwait=True)
        S.emit("dve", (lambda c=c: (lambda eng: [
            eng.tensor_copy(LG[:, c * NE:(c + 1) * NE], PS[2][:, 0:8])]))(),
               reads=[B_PS[2]], writes=[B_LG], selfwait=True)

    def _stmir(eng):
        ins = []
        for kd in range(NKD):
            ins.append(eng.dma_start(out=xn_dmaj_mir[kd * 128:(kd + 1) * 128, :],
                                     in_=xnd[:, kd * T:(kd + 1) * T]))
        return ins
    S.emit("sync", lambda eng: _stmir(eng), reads=[B_xnd], writes=[B_xn_dmaj_mir],
           dma=True, ndma=NKD)

    LG3 = LG[:].rearrange("p (c e) -> p c e", e=NE)
    rt = ExitStack()
    m1, B_m1 = sb("m1", (128, NCT), F32, rt)
    nm1, B_nm1 = sb("nm1", (128, NCT), F32, rt)
    ee, B_ee = sb("ee", (128, NCT * NE), F32, rt)
    ssum, B_ssum = sb("ssum", (128, NCT), F32, rt)
    p1c, B_p1c = sb("p1c", (128, NCT), F32, rt)
    d3, B_d3 = sb("d3", (128, NCT * NE), F32, rt)
    sg, B_sg = sb("sgn", (128, NCT * NE), F32, rt)
    oh, B_oh = sb("oh", (128, NCT * NE), F32, rt)
    cand, B_cand = sb("cand", (128, NCT * NE), F32, rt)
    idx1, B_idx1 = sb("idx1", (128, NCT), F32, rt)
    idx2, B_idx2 = sb("idx2", (128, NCT), F32, rt)
    ohx1, B_ohx1 = sb("ohx1", (128, NCT * NE), F32, rt)
    ohx2, B_ohx2 = sb("ohx2", (128, NCT * NE), F32, rt)
    l2t, B_l2t = sb("l2t", (128, NCT * NE), F32, rt)
    m2, B_m2 = sb("m2", (128, NCT), F32, rt)
    ohseq, B_ohseq = sb("ohseq", (128, NE * 64), F32, rt)
    zer64, B_zer64 = sb("zer64", (128, 64), F32, rt)
    Spe, B_Spe = sb("Spe", (128, NE), F32, rt)
    P0sb, B_P0sb = sb("P0sb", (128, NE), F32, rt)
    posi, B_posi = sb("posi", (128, NE * 64), F32, rt)
    qtmp, B_qtmp = sb("qtmp", (128, NCT * NE), F32, rt)
    pos1, B_pos1 = sb("pos1", (128, NCT), F32, rt)
    pos2, B_pos2 = sb("pos2", (128, NCT), F32, rt)
    keep, B_keep = sb("keep", (128, NCT), F32, rt)
    dslf, B_dslf = sb("dslf", (128, NCT), F32, rt)
    xnsb, B_xnsb = sb("xnsb", (128, DM), BF16, rt)

    d33 = d3[:].rearrange("p (c e) -> p c e", e=NE)
    oh3 = oh[:].rearrange("p (c e) -> p c e", e=NE)
    cand3 = cand[:].rearrange("p (c e) -> p c e", e=NE)
    ohx13 = ohx1[:].rearrange("p (c e) -> p c e", e=NE)
    ohx23 = ohx2[:].rearrange("p (c e) -> p c e", e=NE)
    l23 = l2t[:].rearrange("p (c e) -> p c e", e=NE)
    ee3 = ee[:].rearrange("p (c e) -> p c e", e=NE)
    qt3 = qtmp[:].rearrange("p (c e) -> p c e", e=NE)
    oq2 = ohseq[:].rearrange("p (e s) -> p e s", s=64)
    po2 = posi[:].rearrange("p (e s) -> p e s", s=64)

    S.emit("dve", lambda eng: [eng.reduce_max(m1[:], LG3, axis=AX)],
           reads=[B_LG], writes=[B_m1], selfwait=True)
    S.emit("dve", lambda eng: [eng.reduce_max(nm1[:], LG3, axis=AX, negate=True)],
           reads=[B_LG], writes=[B_nm1], selfwait=True)

    def _subm(eng, src3, mm, B=None):
        last = None
        for e in range(NE):
            last = eng.tensor_sub(d33[:, :, e], src3[:, :, e], mm[:])
        return last
    S.emit("dve", lambda eng: [_subm(eng, LG3, m1)], reads=[B_LG, B_m1], writes=[B_d3])
    S.emit("act", lambda eng: [eng.activation(out=ee[:], in_=d3[:], func=AF.Exp)],
           reads=[B_d3], writes=[B_ee])
    S.emit("dve", lambda eng: [eng.reduce_sum(ssum[:], ee3, axis=AX)],
           reads=[B_ee], writes=[B_ssum], selfwait=True)
    S.emit("dve", lambda eng: [eng.reciprocal(p1c[:], ssum[:])],
           reads=[B_ssum], writes=[B_p1c], selfwait=True)
    S.emit("act", lambda eng: [eng.sign(sg[:], d3[:])], reads=[B_d3], writes=[B_sg])
    S.emit("dve", lambda eng: [eng.tensor_scalar(out=oh[:], in0=sg[:], scalar1=1.0,
                                                 scalar2=None, op0=ALU.add)],
           reads=[B_sg], writes=[B_oh], selfwait=True)

    def _cand(eng):
        last = None
        for e in range(NE):
            last = eng.tensor_scalar(out=cand3[:, :, e], in0=oh3[:, :, e],
                                     scalar1=float(e - 99), scalar2=99.0,
                                     op0=ALU.mult, op1=ALU.add)
        return last
    S.emit("dve", lambda eng: [_cand(eng)], reads=[B_oh], writes=[B_cand], selfwait=True)
    S.emit("dve", lambda eng: [eng.tensor_reduce(idx1[:], cand3, axis=AX, op=ALU.min)],
           reads=[B_cand], writes=[B_idx1], selfwait=True)

    def _ohx(eng, idx, dst3):
        last = None
        for e in range(NE):
            last = eng.tensor_scalar(out=dst3[:, :, e], in0=idx[:], scalar1=float(-e),
                                     scalar2=None, op0=ALU.add)
        return last
    S.emit("dve", lambda eng: [_ohx(eng, idx1, ohx13)], reads=[B_idx1], writes=[B_ohx1], selfwait=True)
    S.emit("act", lambda eng: [eng.square(ohx1[:], ohx1[:])], reads=[B_ohx1], writes=[B_ohx1])
    S.emit("dve", lambda eng: [eng.tensor_single_scalar(ohx1[:], ohx1[:], 0.5, ALU.is_lt)],
           reads=[B_ohx1], writes=[B_ohx1], selfwait=True)
    S.emit("dve", lambda eng: [eng.scalar_tensor_tensor(out=l2t[:], in0=ohx1[:],
                                                        scalar=-1e30, in1=LG[:],
                                                        op0=ALU.mult, op1=ALU.add)],
           reads=[B_ohx1, B_LG], writes=[B_l2t], selfwait=True)
    S.emit("dve", lambda eng: [eng.reduce_max(m2[:], l23, axis=AX)],
           reads=[B_l2t], writes=[B_m2], selfwait=True)
    S.emit("dve", lambda eng: [_subm(eng, l23, m2)], reads=[B_l2t, B_m2], writes=[B_d3])
    S.emit("act", lambda eng: [eng.sign(sg[:], d3[:])], reads=[B_d3], writes=[B_sg])
    S.emit("dve", lambda eng: [eng.tensor_scalar(out=oh[:], in0=sg[:], scalar1=1.0,
                                                 scalar2=None, op0=ALU.add)],
           reads=[B_sg], writes=[B_oh], selfwait=True)
    S.emit("dve", lambda eng: [_cand(eng)], reads=[B_oh], writes=[B_cand], selfwait=True)
    S.emit("dve", lambda eng: [eng.tensor_reduce(idx2[:], cand3, axis=AX, op=ALU.min)],
           reads=[B_cand], writes=[B_idx2], selfwait=True)
    S.emit("dve", lambda eng: [_ohx(eng, idx2, ohx23)], reads=[B_idx2], writes=[B_ohx2], selfwait=True)
    S.emit("act", lambda eng: [eng.square(ohx2[:], ohx2[:])], reads=[B_ohx2], writes=[B_ohx2])
    S.emit("dve", lambda eng: [eng.tensor_single_scalar(ohx2[:], ohx2[:], 0.5, ALU.is_lt)],
           reads=[B_ohx2], writes=[B_ohx2], selfwait=True)
    S.emit("dve", lambda eng: [eng.tensor_sub(m2[:], m2[:], m1[:])],
           reads=[B_m1], writes=[B_m2], selfwait=True)
    S.emit("act", lambda eng: [eng.activation(out=m2[:], in_=m2[:], func=AF.Exp)],
           reads=[B_m2], writes=[B_m2])
    S.emit("dve", lambda eng: [eng.tensor_mul(m2[:], m2[:], p1c[:])],
           reads=[B_p1c], writes=[B_m2], selfwait=True)
    def _ilv(eng):
        last = None
        for e in range(NE):
            src1 = ohx13[:, :, e]
            src2 = ohx23[:, :, e]
            d1 = bass.AP(oq2.tensor, oq2.offset + e * 64, [oq2.ap[0], [2, 32]])
            d2 = bass.AP(oq2.tensor, oq2.offset + e * 64 + 1, [oq2.ap[0], [2, 32]])
            eng.tensor_copy(d1, src1)
            last = eng.tensor_copy(d2, src2)
        return last
    S.emit("dve", lambda eng: [_ilv(eng)], reads=[B_ohx1, B_ohx2], writes=[B_ohseq], selfwait=True)
    S.emit("dve", lambda eng: [eng.memset(zer64[:], 0.0)], writes=[B_zer64], selfwait=True)
    S.emit("dve", lambda eng: [eng.reduce_sum(Spe[:], oq2, axis=AX)],
           reads=[B_ohseq], writes=[B_Spe], selfwait=True)
    S.emit("pe", lambda eng: [eng.matmul(PS[2][:, 0:8], LT_sb[:], Spe[:],
                                         start=True, stop=True)],
           reads=[B_LT, B_Spe], writes=[B_PS[2]])
    S.emit("dve", lambda eng: [eng.tensor_copy(P0sb[:], PS[2][:, 0:8])],
           reads=[B_PS[2]], writes=[B_P0sb], selfwait=True)
    for e in range(NE):
        S.emit("dve", (lambda e=e: (lambda eng: [
            eng.tensor_tensor_scan(out=po2[:, e], data0=oq2[:, e], data1=zer64[:],
                                   initial=P0sb[:, e:e + 1], op0=ALU.add, op1=ALU.add)
        ]))(), reads=[B_ohseq, B_zer64, B_P0sb], writes=[B_posi], selfwait=True)

    def _posk(eng, which, dst):
        last = None
        for e in range(NE):
            srcpos = bass.AP(po2.tensor, po2.offset + e * 64 + which, [po2.ap[0], [2, 32]])
            src_oh = (ohx13 if which == 0 else ohx23)[:, :, e]
            last = eng.tensor_mul(qt3[:, :, e], src_oh, srcpos)
        return last
    S.emit("dve", lambda eng: [_posk(eng, 0, pos1)], reads=[B_posi, B_ohx1], writes=[B_qtmp], selfwait=True)
    S.emit("dve", lambda eng: [eng.reduce_sum(pos1[:], qt3, axis=AX)],
           reads=[B_qtmp], writes=[B_pos1], selfwait=True)
    S.emit("dve", lambda eng: [_posk(eng, 1, pos2)], reads=[B_posi, B_ohx2], writes=[B_qtmp], selfwait=True)
    S.emit("dve", lambda eng: [eng.reduce_sum(pos2[:], qt3, axis=AX)],
           reads=[B_qtmp], writes=[B_pos2], selfwait=True)

    def _mkdsl(idxk, posk, pk_src, B_idxk, B_posk, B_pk_src, dsl_i, pm_dst, B_dsl_i, B_pm):
        S.emit("dve", lambda eng: [eng.tensor_single_scalar(keep[:], posk[:],
                                                            float(CAP) + 0.5, ALU.is_lt)],
               reads=[B_posk], writes=[B_keep], selfwait=True)
        S.emit("dve", lambda eng: [eng.tensor_mul(pm_dst[:], pk_src[:], keep[:])],
               reads=[B_pk_src, B_keep], writes=[B_pm], selfwait=True)
        S.emit("dve", lambda eng: [eng.tensor_scalar(out=dslf[:], in0=idxk[:],
                                                     scalar1=float(CAP), scalar2=-1.0,
                                                     op0=ALU.mult, op1=ALU.add)],
               reads=[B_idxk], writes=[B_dslf], selfwait=True)
        S.emit("dve", lambda eng: [eng.tensor_add(dslf[:], dslf[:], posk[:])],
               reads=[B_posk], writes=[B_dslf], selfwait=True)
        S.emit("dve", lambda eng: [eng.tensor_mul(dslf[:], dslf[:], keep[:])],
               reads=[B_keep], writes=[B_dslf], selfwait=True)
        S.emit("dve", lambda eng: [eng.tensor_scalar(out=qtmp[:, 0:NCT], in0=keep[:],
                                                     scalar1=-2e6, scalar2=2e6,
                                                     op0=ALU.mult, op1=ALU.add)],
               reads=[B_keep], writes=[B_qtmp], selfwait=True)
        S.emit("dve", lambda eng: [eng.tensor_add(dslf[:], dslf[:], qtmp[:, 0:NCT])],
               reads=[B_qtmp], writes=[B_dslf], selfwait=True)
        S.emit("dve", lambda eng: [eng.tensor_copy(dsl_i[:], dslf[:])],
               reads=[B_dslf], writes=[B_dsl_i], selfwait=True)

    _mkdsl(idx1, pos1, p1c, B_idx1, B_pos1, B_p1c, dsl1_sb, pm1_sb, B_dsl1, B_pm1)
    _mkdsl(idx2, pos2, m2, B_idx2, B_pos2, B_m2, dsl2_sb, pm2_sb, B_dsl2, B_pm2)

    if DEBUG_TAPS:
        def _dsltap(eng):
            return [eng.dma_start(out=dsl_tap[:, 0:32], in_=dsl1_sb[:]),
                    eng.dma_start(out=dsl_tap[:, 32:64], in_=dsl2_sb[:]),
                    eng.dma_start(out=pm_tap[:, 0:32], in_=pm1_sb[:]),
                    eng.dma_start(out=pm_tap[:, 32:64], in_=pm2_sb[:])]
        S.emit("sync", _dsltap, reads=[B_dsl1, B_dsl2, B_pm1, B_pm2],
               writes=[B_dsl_tap, B_pm_tap], dma=True, ndma=4)

    for c in range(NCT):
        mrows = xn_tok_mir.rearrange("(p c) d -> c p d", c=NCT)[c]
        S.emit("sync", (lambda mr=mrows: (lambda eng: [eng.dma_start(out=xnsb[:], in_=mr)]))(),
               reads=[B_xn_tok_mir], writes=[B_xnsb], dma=True)
        for (dsl_i, B_dsl_i) in ((dsl1_sb, B_dsl1), (dsl2_sb, B_dsl2)):
            S.emit("pool", (lambda c=c, dsl_i=dsl_i: (lambda eng: [
                eng.indirect_dma_start(
                    out=disp_dram, out_offset=IndirectOffsetOnAxis(ap=dsl_i[:, c:c + 1], axis=0),
                    in_=xnsb[:], in_offset=None,
                    bounds_check=BCREG["r"], oob_is_err=False)
            ]))(), reads=[B_xnsb, B_dsl_i], writes=[B_disp_dram], dma=True)

    rt.close()
    p1.close()
    p0.close()
    S.barrier()

    p2 = ExitStack()
    xnch, B_xnch = sb("xnch", (128, NKD * TC), BF16, p2)
    wslab = []
    B_wslab = []
    for i in range(2):
        t, b = sb("wslab%d" % i, (128, 8 * 128), BF16, p2)
        wslab.append(t)
        B_wslab.append(b)
    xc, B_xc = sb("xc", (128, NJ * TC), BF16, p2)
    Ball, B_Ball = sb("Ball", (128, DS * TC), BF16, p2)
    Call, B_Call = sb("Call", (128, DS * TC), BF16, p2)
    dtlow, B_dtlow = sb("dtlow", (64, TC), BF16, p2)
    Brow, B_Brow = sb("Brow", (16, TC), BF16, p2)
    Crow, B_Crow = sb("Crow", (16, TC), BF16, p2)
    xin = []
    B_xin = []
    for i in range(2):
        t, b = sb("xin%d" % i, (128, TC + 3), BF16, p2)
        xin.append(t)
        B_xin.append(b)
    SCRB = []
    B_SCRB = []
    for i in range(3):
        t, b = sb("scrb%d" % i, (128, TC), BF16, p2)
        SCRB.append(t)
        B_SCRB.append(b)
    dtj = []
    B_dtj = []
    silz = []
    B_silz = []
    for i in range(2):
        t, b = sb("dtj%d" % i, (128, TC), BF16, p2)
        dtj.append(t)
        B_dtj.append(b)
        t, b = sb("silz%d" % i, (128, TC), BF16, p2)
        silz.append(t)
        B_silz.append(b)
    av = []
    B_av = []
    wv = []
    B_wv = []
    uv = []
    B_uv = []
    hv = []
    B_hv = []
    pv = []
    B_pv = []
    dg = []
    B_dg = []
    for i in range(1):
        t, b = sb("wv%d" % i, (128, TC), BF16, p2)
        wv.append(t)
        B_wv.append(b)
        t, b = sb("hv%d" % i, (128, TC), BF16, p2)
        hv.append(t)
        B_hv.append(b)
    for i in range(2):
        t, b = sb("av%d" % i, (128, TC), F32, p2)
        av.append(t)
        B_av.append(b)
    for i in range(4):
        t, b = sb("pv%d" % i, (128, TC), BF16, p2)
        pv.append(t)
        B_pv.append(b)
        t, b = sb("dg%d" % i, (128, 128), BF16, p2)
        dg.append(t)
        B_dg.append(b)

    B_gscol = [Buf("gs%d" % i) for i in range(NJ * DS)]
    S.emit("dve", lambda eng: [eng.memset(gstate[:], 0.0)],
           writes=[B_gstate] + B_gscol)
    S.emit("dve", lambda eng: [eng.memset(xinst[:], 0.0)], writes=[B_xinst])

    dspr = []
    B_dspr = []
    for i in range(2):
        t, b = sb("dspr%d" % i, (128, DM), BF16, p2)
        dspr.append(t)
        B_dspr.append(b)
    dispTcc, B_dispTcc = sb("dispTcc", (128, NKD * 512), BF16, p2)
    w1r = []
    B_w1r = []
    w3r = []
    B_w3r = []
    silcc = []
    B_silcc = []
    h3c = []
    B_h3c = []
    gcc = []
    B_gcc = []
    for i in range(2):
        t, b = sb("w1r%d" % i, (128, 8 * 128), BF16, p2)
        w1r.append(t)
        B_w1r.append(b)
        t, b = sb("w3r%d" % i, (128, 8 * 128), BF16, p2)
        w3r.append(t)
        B_w3r.append(b)
    for i in range(3):
        t, b = sb("silcc%d" % i, (128, 512), BF16, p2)
        silcc.append(t)
        B_silcc.append(b)
        t, b = sb("h3c%d" % i, (128, 512), BF16, p2)
        h3c.append(t)
        B_h3c.append(b)
        t, b = sb("gcc%d" % i, (128, 512), BF16, p2)
        gcc.append(t)
        B_gcc.append(b)

    UH = [0]
    MOEA = []
    for e in range(NE):
        for cc in range(3):
            n0 = cc * 512
            ccw = min(CAP, n0 + 512) - n0
            qlist = list(range(cc * 4, min(NCAP, cc * 4 + 4)))
            for qi, q in enumerate(qlist):
                def trans_unit(e=e, q=q, qi=qi, ccw=ccw):
                    dq = q % 2
                    S.emit("sync", (lambda e=e, q=q, dq=dq: (lambda eng: [
                        eng.dma_start(out=dspr[dq][:],
                                      in_=disp_dram[e * CAP + q * 128: e * CAP + (q + 1) * 128, :])]))(),
                           reads=[B_disp_dram], writes=[B_dspr[dq]], dma=True)
                    for half in range(2):
                        def _tr(eng, dq=dq, half=half):
                            last = None
                            for k in range(4):
                                kd = 4 * half + k
                                last = eng.transpose(PSB[:, k * 128:(k + 1) * 128],
                                                     dspr[dq][:, kd * 128:(kd + 1) * 128],
                                                     identb[:])
                            return last
                        S.emit("pe", _tr, reads=[B_dspr[dq], B_identb], writes=[B_PSB])
                        def _trev(eng, half=half, qi=qi, ccw=ccw):
                            last = None
                            for k in range(4):
                                kd = 4 * half + k
                                last = eng.copy(
                                    dispTcc[:, kd * ccw + qi * 128: kd * ccw + qi * 128 + 128],
                                    PSB[:, k * 128:(k + 1) * 128])
                            return last
                        S.emit("act", _trev, reads=[B_PSB], writes=[B_dispTcc])
                    return None
                MOEA.append(trans_unit)
            for ft in range(NF):
                def h13_unit(e=e, cc=cc, ft=ft, n0=n0, ccw=ccw):
                    wr = UH[0] % 2
                    ur = UH[0] % 3
                    UH[0] += 1
                    S.emit("sync", (lambda e=e, ft=ft, wr=wr: (lambda eng: [
                        eng.dma_start(out=w1r[wr][:], in_=w1_dev[e, ft]),
                        eng.dma_start(out=w3r[wr][:], in_=w3_dev[e, ft])]))(),
                           writes=[B_w1r[wr], B_w3r[wr]], dma=True, ndma=2)
                    def _h1(eng, wr=wr, ccw=ccw):
                        last = None
                        for kd in range(NKD):
                            last = eng.matmul(PS[5][:, 0:ccw],
                                              w1r[wr][:, kd * 128:(kd + 1) * 128],
                                              dispTcc[:, kd * ccw: kd * ccw + ccw],
                                              start=(kd == 0), stop=(kd == NKD - 1))
                        return last
                    S.emit("pe", _h1, reads=[B_w1r[wr], B_dispTcc], writes=[B_PS[5]])
                    S.emit("act", (lambda ccw=ccw, ur=ur: (lambda eng: [
                        eng.activation(out=silcc[ur][:, 0:ccw], in_=PS[5][:, 0:ccw],
                                       func=AF.Silu)]))(),
                           reads=[B_PS[5]], writes=[B_silcc[ur]])
                    def _h3(eng, wr=wr, ccw=ccw):
                        last = None
                        for kd in range(NKD):
                            last = eng.matmul(PS[6][:, 0:ccw],
                                              w3r[wr][:, kd * 128:(kd + 1) * 128],
                                              dispTcc[:, kd * ccw: kd * ccw + ccw],
                                              start=(kd == 0), stop=(kd == NKD - 1))
                        return last
                    S.emit("pe", _h3, reads=[B_w3r[wr], B_dispTcc], writes=[B_PS[6]])
                    S.emit("act", (lambda ccw=ccw, ur=ur: (lambda eng: [
                        eng.copy(h3c[ur][:, 0:ccw], PS[6][:, 0:ccw])]))(),
                           reads=[B_PS[6]], writes=[B_h3c[ur]])
                    def tail(e=e, ft=ft, n0=n0, ccw=ccw, ur=ur):
                        S.emit("dve", (lambda ccw=ccw, ur=ur: (lambda eng: [
                            eng.tensor_mul(gcc[ur][:, 0:ccw], h3c[ur][:, 0:ccw],
                                           silcc[ur][:, 0:ccw])]))(),
                               reads=[B_h3c[ur], B_silcc[ur]], writes=[B_gcc[ur]])
                        S.emit("pool", (lambda e=e, ft=ft, n0=n0, ccw=ccw, ur=ur: (lambda eng: [
                            eng.dma_start(out=g_dram[e, ft, :, n0:n0 + ccw],
                                          in_=gcc[ur][:, 0:ccw])]))(),
                               reads=[B_gcc[ur]], writes=[B_g_dram], dma=True)
                    return tail
                MOEA.append(h13_unit)
    MOEA_POS = [0]
    PENDING_TAILS = []

    def drain_pe(k):
        while k > 0 and MOEA_POS[0] < len(MOEA):
            t = MOEA[MOEA_POS[0]]()
            if t is not None:
                PENDING_TAILS.append(t)
            MOEA_POS[0] += 1
            k -= 1

    def flush_tails():
        for t in PENDING_TAILS:
            t()
        PENDING_TAILS.clear()

    WCTR = [0]
    for ch in range(NCH):
        c0 = ch * TC
        def _ldxn(eng, c0=c0):
            ins = []
            for kd in range(NKD):
                ins.append(eng.dma_start(out=xnch[:, kd * TC:(kd + 1) * TC],
                                         in_=xn_dmaj_mir[kd * 128:(kd + 1) * 128, c0:c0 + TC]))
            return ins
        S.emit("sync", _ldxn, reads=[B_xn_dmaj_mir], writes=[B_xnch], dma=True, ndma=NKD)

        for J in range(NJ):
            slab = WCTR[0] % 2
            WCTR[0] += 1
            S.emit("sync", (lambda J=J, slab=slab: (lambda eng: [
                eng.dma_start(out=wslab[slab][:], in_=w_inproj[J])]))(),
                   writes=[B_wslab[slab]], dma=True)
            pj = J % 2
            def _mm(eng, J=J, slab=slab, pj=pj):
                last = None
                for kd in range(NKD):
                    for nn in range(TC // 512):
                        last = eng.matmul(PS[2 * pj + nn][:],
                                          wslab[slab][:, kd * 128:(kd + 1) * 128],
                                          xnch[:].rearrange("p (k t) -> p k t", k=NKD)[:, kd, nn * 512:(nn + 1) * 512],
                                          start=(kd == 0), stop=(kd == NKD - 1))
                return last
            S.emit("pe", _mm, reads=[B_wslab[slab], B_xnch],
                   writes=[B_PS[2 * pj], B_PS[2 * pj + 1]])
            xr = J % 2
            def _evx(eng, J=J, pj=pj, xr=xr):
                eng.add(xin[xr][:, 3:3 + 512], PS[2 * pj][:], bin_sb[:, J:J + 1])
                return eng.add(xin[xr][:, 3 + 512:3 + 1024], PS[2 * pj + 1][:], bin_sb[:, J:J + 1])
            S.emit("act", _evx, reads=[B_PS[2 * pj], B_PS[2 * pj + 1], B_bin],
                   writes=[B_xin[xr]])
            S.emit("dve", (lambda J=J, xr=xr: (lambda eng: [
                eng.tensor_copy(xin[xr][:, 0:3], xinst[:, J * 4:J * 4 + 3])]))(),
                   reads=[B_xinst], writes=[B_xin[xr]])
            def _cprod(eng, J=J, xr=xr):
                last = None
                for k in range(3):
                    last = eng.mul(SCRB[k][:], xin[xr][:, k:k + TC], cw_sb[:, J * 4 + k:J * 4 + k + 1])
                return last
            S.emit("act", _cprod, reads=[B_xin[xr], B_cw],
                   writes=[B_SCRB[0], B_SCRB[1], B_SCRB[2]])
            def _cadd(eng):
                eng.tensor_add(SCRB[0][:], SCRB[0][:], SCRB[1][:])
                return eng.tensor_add(SCRB[0][:], SCRB[0][:], SCRB[2][:])
            S.emit("dve", _cadd, reads=[B_SCRB[1], B_SCRB[2]], writes=[B_SCRB[0]])
            S.emit("act", (lambda J=J, xr=xr: (lambda eng: [
                eng.mul(SCRB[1][:], xin[xr][:, 3:3 + TC], cw_sb[:, J * 4 + 3:J * 4 + 4])]))(),
                   reads=[B_xin[xr], B_cw], writes=[B_SCRB[1]])
            S.emit("dve", lambda eng: [eng.tensor_add(SCRB[0][:], SCRB[0][:], SCRB[1][:])],
                   reads=[B_SCRB[1]], writes=[B_SCRB[0]])
            S.emit("act", (lambda J=J: (lambda eng: [
                eng.activation(out=xc[:, J * TC:(J + 1) * TC], in_=SCRB[0][:], func=AF.Silu,
                               bias=cb_sb[:, J:J + 1])]))(),
                   reads=[B_SCRB[0], B_cb], writes=[B_xc])
            S.emit("dve", (lambda J=J, xr=xr: (lambda eng: [
                eng.tensor_copy(xinst[:, J * 4:J * 4 + 3], xin[xr][:, TC:TC + 3])]))(),
                   reads=[B_xin[xr]], writes=[B_xinst])
            drain_pe(1)
            flush_tails()

        def _xproj(eng):
            last = None
            for kk in range(16):
                for nn in range(TC // 512):
                    last = eng.matmul(PS[nn][0:80, :],
                                      wxp1_sb[:, kk * 80:(kk + 1) * 80],
                                      xc[:].rearrange("p (j t) -> p j t", j=NJ)[:, kk, nn * 512:(nn + 1) * 512],
                                      start=(kk == 0), stop=(kk == 15))
            return last
        S.emit("pe", _xproj, reads=[B_wxp1, B_xc], writes=[B_PS[0], B_PS[1]])
        def _evxp(eng):
            eng.copy(dtlow[:, 0:512], PS[0][0:64, :])
            eng.copy(dtlow[:, 512:1024], PS[1][0:64, :])
            eng.copy(Brow[:, 0:512], PS[0][64:80, :])
            return eng.copy(Brow[:, 512:1024], PS[1][64:80, :])
        S.emit("act", _evxp, reads=[B_PS[0], B_PS[1]], writes=[B_dtlow, B_Brow])
        def _xprojC(eng):
            last = None
            for kk in range(16):
                for nn in range(TC // 512):
                    last = eng.matmul(PS[2 + nn][0:16, :],
                                      wxpC_sb[:, kk * 16:(kk + 1) * 16],
                                      xc[:].rearrange("p (j t) -> p j t", j=NJ)[:, kk, nn * 512:(nn + 1) * 512],
                                      start=(kk == 0), stop=(kk == 15))
            return last
        S.emit("pe", _xprojC, reads=[B_wxpC, B_xc], writes=[B_PS[2], B_PS[3]])
        def _evC(eng):
            eng.copy(Crow[:, 0:512], PS[2][0:16, :])
            return eng.copy(Crow[:, 512:1024], PS[3][0:16, :])
        S.emit("act", _evC, reads=[B_PS[2], B_PS[3]], writes=[B_Crow])

        for s in range(DS):
            for (row, Ba, dstall, B_dstall) in ((Brow, B_Brow, Ball, B_Ball),
                                                (Crow, B_Crow, Call, B_Call)):
                def _bc(eng, s=s, row=row):
                    last = None
                    for nn in range(TC // 512):
                        last = eng.matmul(PS[2 + nn][:], BSEL_sb[:, s * 128:(s + 1) * 128],
                                          row[:, nn * 512:(nn + 1) * 512],
                                          start=True, stop=True)
                    return last
                S.emit("pe", _bc, reads=[Ba, B_BSEL], writes=[B_PS[2], B_PS[3]])
                def _bce(eng, s=s, dstall=dstall):
                    eng.copy(dstall[:, s * TC:s * TC + 512], PS[2][:])
                    return eng.copy(dstall[:, s * TC + 512:(s + 1) * TC], PS[3][:])
                S.emit("act", _bce, reads=[B_PS[2], B_PS[3]], writes=[B_dstall])

        def prep_J(J):
            pj = J % 2
            for nn in range(TC // 512):
                S.emit("pe", (lambda J=J, nn=nn: (lambda eng: [
                    eng.matmul(PS[4][:], wdt_sb[:, J * 128:(J + 1) * 128],
                               dtlow[:, nn * 512:(nn + 1) * 512],
                               start=True, stop=True)]))(),
                       reads=[B_wdt, B_dtlow], writes=[B_PS[4]])
                S.emit("act", (lambda J=J, nn=nn: (lambda eng: [
                    eng.activation(out=SCRB[2][:, nn * 512:(nn + 1) * 512], in_=PS[4][:],
                                   func=AF.Exp, bias=bdt_sb[:, J:J + 1])]))(),
                       reads=[B_PS[4], B_bdt], writes=[B_SCRB[2]])
            S.emit("act", (lambda pj=pj: (lambda eng: [
                eng.activation(out=dtj[pj][:], in_=SCRB[2][:], func=AF.Ln, bias=1.0)]))(),
                   reads=[B_SCRB[2]], writes=[B_dtj[pj]])

            slab = WCTR[0] % 2
            WCTR[0] += 1
            S.emit("sync", (lambda J=J, slab=slab: (lambda eng: [
                eng.dma_start(out=wslab[slab][:], in_=w_inproj[16 + J])]))(),
                   writes=[B_wslab[slab]], dma=True)
            def _zmm(eng, J=J, slab=slab, pj=pj):
                last = None
                for kd in range(NKD):
                    for nn in range(TC // 512):
                        last = eng.matmul(PS[2 * pj + nn][:],
                                          wslab[slab][:, kd * 128:(kd + 1) * 128],
                                          xnch[:].rearrange("p (k t) -> p k t", k=NKD)[:, kd, nn * 512:(nn + 1) * 512],
                                          start=(kd == 0), stop=(kd == NKD - 1))
                return last
            S.emit("pe", _zmm, reads=[B_wslab[slab], B_xnch],
                   writes=[B_PS[2 * pj], B_PS[2 * pj + 1]])
            def _zev(eng, J=J, pj=pj):
                eng.activation(out=silz[pj][:, 0:512], in_=PS[2 * pj][:], func=AF.Silu,
                               bias=bin_sb[:, 16 + J:17 + J])
                return eng.activation(out=silz[pj][:, 512:1024], in_=PS[2 * pj + 1][:],
                                      func=AF.Silu, bias=bin_sb[:, 16 + J:17 + J])
            S.emit("act", _zev, reads=[B_PS[2 * pj], B_PS[2 * pj + 1], B_bin],
                   writes=[B_silz[pj]])

        def s_body(J, s):
            pj = J % 2
            sr = s % 2
            s4 = s % 4
            col = J * DS + s
            S.emit("dve", (lambda J=J, s=s: (lambda eng: [
                eng.tensor_mul(wv[0][:], xc[:, J * TC:(J + 1) * TC],
                               Ball[:, s * TC:(s + 1) * TC])]))(),
                   reads=[B_xc, B_Ball], writes=[B_wv[0]])
            s2 = s % 2
            S.emit("dve", (lambda s2=s2: (lambda eng: [
                eng.scalar_tensor_tensor(out=wv[0][:], in0=av[s2][:], scalar=-1.0,
                                         in1=wv[0][:], op0=ALU.add, op1=ALU.mult)]))(),
                   reads=[B_av[s2]], writes=[B_wv[0]])
            S.emit("dve", (lambda s2=s2, col=col: (lambda eng: [
                eng.tensor_tensor_scan(out=hv[0][:], data0=av[s2][:], data1=wv[0][:],
                                       initial=gstate[:, col:col + 1],
                                       op0=ALU.mult, op1=ALU.add)]))(),
                   reads=[B_av[s2], B_gscol[col]], writes=[B_hv[0]],
                   selfwait=True)
            S.emit("dve", (lambda col=col: (lambda eng: [
                eng.tensor_copy(gstate[:, col:col + 1], hv[0][:, TC - 1:TC])]))(),
                   reads=[B_hv[0]], writes=[B_gscol[col]])
            S.emit("dve", (lambda s=s, s4=s4: (lambda eng: [
                eng.tensor_mul(pv[s4][:], hv[0][:], Call[:, s * TC:(s + 1) * TC])]))(),
                   reads=[B_hv[0], B_Call], writes=[B_pv[s4]])
            def _yred(eng, s4=s4, s=s, pj=pj):
                last = None
                for nn in range(TC // 512):
                    last = eng.matmul(PS[2 * pj + nn][:], dg[s4][:],
                                      pv[s4][:, nn * 512:(nn + 1) * 512],
                                      start=(s == 0), stop=(s == DS - 1))
                return last
            S.emit("pe", _yred, reads=[B_dg[s4], B_pv[s4]],
                   writes=[B_PS[2 * pj], B_PS[2 * pj + 1]])

        prep_J(0)
        for J in range(NJ):
            pj = J % 2
            for g in range(4):
                for s in range(4 * g, 4 * g + 4):
                    s2 = s % 2
                    s4 = s % 4
                    col = J * DS + s
                    S.emit("act", (lambda pj=pj, s2=s2, col=col: (lambda eng: [
                        eng.activation(out=av[s2][:], in_=dtj[pj][:], func=AF.Exp,
                                       scale=A_sb[:, col:col + 1])]))(),
                           reads=[B_dtj[pj], B_A], writes=[B_av[s2]])
                    S.emit("act", (lambda s4=s4, col=col: (lambda eng: [
                        eng.mul(dg[s4][:], identb[:], invA_sb[:, col:col + 1])]))(),
                           reads=[B_identb, B_invA], writes=[B_dg[s4]])
                drain_pe(2 if g == 3 else 3)
                for s in range(4 * g, 4 * g + 4):
                    s_body(J, s)
                flush_tails()
                if g == 0 and J + 1 < NJ:
                    prep_J(J + 1)

            def _yev(eng, J=J, pj=pj):
                eng.scalar_tensor_tensor(out=SCRB[0][:, 0:512], in0=xc[:, J * TC:J * TC + 512],
                                         scalar=Dc_sb[:, J:J + 1], in1=PS[2 * pj][:],
                                         op0=ALU.mult, op1=ALU.add)
                eng.scalar_tensor_tensor(out=SCRB[0][:, 512:1024],
                                         in0=xc[:, J * TC + 512:(J + 1) * TC],
                                         scalar=Dc_sb[:, J:J + 1], in1=PS[2 * pj + 1][:],
                                         op0=ALU.mult, op1=ALU.add)
                return eng.tensor_mul(SCRB[0][:], SCRB[0][:], silz[pj][:])
            S.emit("dve", _yev, reads=[B_PS[2 * pj], B_PS[2 * pj + 1], B_xc, B_Dc,
                                       B_silz[pj]], writes=[B_SCRB[0]])
            S.emit("sync", (lambda ch=ch, J=J: (lambda eng: [
                eng.dma_start(out=yg_mir[ch, J], in_=SCRB[0][:])]))(),
                   reads=[B_SCRB[0]], writes=[B_yg_mir], dma=True)

    drain_pe(10 ** 9)
    flush_tails()
    p2.close()
    S.barrier()

    p2b = ExitStack()
    yacF, B_yacF = sb("yacF", (128, (TC // 128) * DM), BF16, p2b)
    wop, B_wop = sb("wop", (128, 8 * NJ * 128), BF16, p2b)
    hv2 = []
    B_hv2 = []
    for i in range(3):
        t, b = sb("hv2_%d" % i, (128, 512), BF16, p2b)
        hv2.append(t)
        B_hv2.append(b)
    mo2b, B_mo2b = sb("mo2b", (128, 512), BF16, p2b)
    NTB = TC // 128
    S.emit("sync", lambda eng: [eng.dma_start(out=wop[:, m * 2048:(m + 1) * 2048],
                                              in_=w_outproj[m])
                                for m in range(8)],
           writes=[B_wop], dma=True, ndma=8)
    HCT = [0]
    yac4 = yacF[:].rearrange("p (tb d) -> p tb d", tb=NTB)
    for ch in range(NCH):
        def _ldx(eng, ch=ch):
            return [eng.dma_start(out=yacF[:, tb * DM:(tb + 1) * DM],
                                  in_=x_hbm[ch * TC + tb * 128: ch * TC + (tb + 1) * 128, :])
                    for tb in range(NTB)]
        S.emit("pool", _ldx, writes=[B_yacF], dma=True, ndma=NTB)
        for hf in range(2):
            for mh in range(2):
                for J in range(NJ):
                    r = HCT[0] % 3
                    HCT[0] += 1
                    S.emit("sync", (lambda ch=ch, J=J, hf=hf, r=r: (lambda eng: [
                        eng.dma_start(out=hv2[r][:],
                                      in_=yg_mir[ch, J, :, hf * 512:(hf + 1) * 512])]))(),
                           reads=[B_yg_mir], writes=[B_hv2[r]], dma=True)
                    def _opmm(eng, J=J, r=r, mh=mh):
                        last = None
                        for m in range(4):
                            gm = mh * 4 + m
                            last = eng.matmul(PS[m][:], wop[:, gm * 2048 + J * 128:
                                                            gm * 2048 + (J + 1) * 128],
                                              hv2[r][:], start=(J == 0), stop=(J == NJ - 1))
                        return last
                    S.emit("pe", _opmm, reads=[B_wop, B_hv2[r]],
                           writes=[B_PS[0], B_PS[1], B_PS[2], B_PS[3]])
                for m in range(4):
                    gm = mh * 4 + m
                    S.emit("act", (lambda m=m, gm=gm: (lambda eng: [
                        eng.add(mo2b[:], PS[m][:], bout_sb[:, gm:gm + 1])]))(),
                           reads=[B_PS[m], B_bout], writes=[B_mo2b])
                    def _tr4(eng):
                        last = None
                        for tb4 in range(4):
                            last = eng.transpose(PSB[:, tb4 * 128:(tb4 + 1) * 128],
                                                 mo2b[:, tb4 * 128:(tb4 + 1) * 128],
                                                 identb[:])
                        return last
                    S.emit("pe", _tr4, reads=[B_mo2b, B_identb], writes=[B_PSB])
                    def _acc(eng, hf=hf, gm=gm):
                        dst = yac4[:, 4 * hf:4 * hf + 4, gm * 128:(gm + 1) * 128]
                        src_ = PSB[:].rearrange("p (tb c) -> p tb c", tb=4)
                        return eng.tensor_add(dst, dst, src_)
                    S.emit("dve", _acc, reads=[B_PSB], writes=[B_yacF])
        def _stout(eng, ch=ch):
            return [eng.dma_start(out=out_hbm[ch * TC + tb * 128: ch * TC + (tb + 1) * 128, :],
                                  in_=yacF[:, tb * DM:(tb + 1) * DM])
                    for tb in range(NTB)]
        S.emit("pool", _stout, reads=[B_yacF], writes=[B_out_hbm], dma=True, ndma=NTB)
    p2b.close()
    S.barrier()

    p3 = ExitStack()
    HW = CAP // 2
    gb2h = []
    B_gb2h = []
    for i in range(2):
        t, b = sb("gb2h%d" % i, (128, NF * HW), BF16, p3)
        gb2h.append(t)
        B_gb2h.append(b)
    w2s = []
    B_w2s = []
    for i in range(2):
        t, b = sb("w2s_%d" % i, (128, NF * DM), BF16, p3)
        w2s.append(t)
        B_w2s.append(b)
    eo = []
    B_eo = []
    for i in range(2):
        t, b = sb("eo%d" % i, (128, DM), BF16, p3)
        eo.append(t)
        B_eo.append(b)

    for e in range(NE):
        we = e % 2
        def _ldw2(eng, e=e, we=we):
            return [eng.dma_start(out=w2s[we][:, kf * DM:(kf + 1) * DM], in_=w2_dev[e, kf])
                    for kf in range(NF)]
        S.emit("sync", _ldw2, writes=[B_w2s[we]], dma=True, ndma=NF)
        for h in range(2):
            def _ldg(eng, e=e, h=h):
                return [eng.dma_start(out=gb2h[h][:, kf * HW:(kf + 1) * HW],
                                      in_=g_dram[e, kf, :, h * HW:(h + 1) * HW])
                        for kf in range(NF)]
            S.emit("sync", _ldg, reads=[B_g_dram], writes=[B_gb2h[h]], dma=True, ndma=NF)
        for q in range(NCAP):
            h = q // 5
            qh = q - 5 * h
            def _w2mm(eng, q=q, we=we, h=h, qh=qh):
                last = None
                for kf in range(NF):
                    gs = gb2h[h][:, kf * HW + qh * 128: kf * HW + (qh + 1) * 128]
                    for nn in range(2):
                        last = eng.matmul(PS[2 * (q % 2) + nn][:], gs,
                                          w2s[we][:, kf * DM + nn * 512: kf * DM + (nn + 1) * 512],
                                          start=(kf == 0), stop=(kf == NF - 1))
                return last
            S.emit("pe", _w2mm, reads=[B_gb2h[h], B_w2s[we]],
                   writes=[B_PS[2 * (q % 2)], B_PS[2 * (q % 2) + 1]])
            def _eoev(eng, q=q):
                eng.copy(eo[q % 2][:, 0:512], PS[2 * (q % 2)][:])
                return eng.copy(eo[q % 2][:, 512:1024], PS[2 * (q % 2) + 1][:])
            S.emit("act", _eoev, reads=[B_PS[2 * (q % 2)], B_PS[2 * (q % 2) + 1]],
                   writes=[B_eo[q % 2]])
            S.emit("sync", (lambda e=e, q=q: (lambda eng: [
                eng.dma_start(out=eout_dram[e * CAP + q * 128: e * CAP + (q + 1) * 128, :],
                              in_=eo[q % 2][:])]))(),
                   reads=[B_eo[q % 2]], writes=[B_eout_dram], dma=True)

    p3.close()
    S.barrier()

    p4 = ExitStack()
    acc = []
    B_acc = []
    cmb = []
    B_cmb = []
    cmb2 = []
    B_cmb2 = []
    for i in range(4):
        t, b = sb("acc%d" % i, (128, DM), F32, p4)
        acc.append(t)
        B_acc.append(b)
        t, b = sb("cmb%d" % i, (128, DM), BF16, p4)
        cmb.append(t)
        B_cmb.append(b)
        t, b = sb("cmb2_%d" % i, (128, DM), BF16, p4)
        cmb2.append(t)
        B_cmb2.append(b)
        S.emit("dve", (lambda i=i: (lambda eng: [eng.memset(cmb[i][:], 0.0)]))(),
               writes=[B_cmb[i]])
        S.emit("dve", (lambda i=i: (lambda eng: [eng.memset(cmb2[i][:], 0.0)]))(),
               writes=[B_cmb2[i]])

    for c in range(NCT):
        ia = c % 4
        orows = out_hbm.rearrange("(p c) d -> c p d", c=NCT)[c]
        S.emit("sync", (lambda orows=orows, ia=ia: (lambda eng: [
            eng.dma_start(out=acc[ia][:], in_=orows)]))(),
               reads=[B_out_hbm], writes=[B_acc[ia]], dma=True)
        for (dsl_i, B_dsl_i, pmv, B_pmv, cb_t, B_cb_t) in (
                (dsl1_sb, B_dsl1, pm1_sb, B_pm1, cmb, B_cmb),
                (dsl2_sb, B_dsl2, pm2_sb, B_pm2, cmb2, B_cmb2)):
            S.emit("pool", (lambda c=c, ia=ia, dsl_i=dsl_i, cb_t=cb_t: (lambda eng: [
                eng.indirect_dma_start(
                    out=cb_t[ia][:], out_offset=None,
                    in_=eout_dram, in_offset=IndirectOffsetOnAxis(ap=dsl_i[:, c:c + 1], axis=0),
                    bounds_check=BCREG["r"], oob_is_err=False)
            ]))(), reads=[B_eout_dram, B_dsl_i], writes=[B_cb_t[ia]], dma=True)
            S.emit("dve", (lambda c=c, ia=ia, pmv=pmv, cb_t=cb_t: (lambda eng: [
                eng.scalar_tensor_tensor(out=acc[ia][:], in0=cb_t[ia][:],
                                         scalar=pmv[:, c:c + 1], in1=acc[ia][:],
                                         op0=ALU.mult, op1=ALU.add)]))(),
                   reads=[B_cb_t[ia], B_pmv], writes=[B_acc[ia]])
        S.emit("sync", (lambda orows=orows, ia=ia: (lambda eng: [
            eng.dma_start(out=orows, in_=acc[ia][:])]))(),
               reads=[B_acc[ia]], writes=[B_out_hbm], dma=True)

    S.barrier()
    p4.close()

    with nc.Block() as blk:
        def run_stream(eng, name):
            for (waits, body, sem, inc, selfwait, val) in S.streams[name]:
                for (sname, v) in waits:
                    eng.wait_ge(sems[sname], v)
                if body is None:
                    continue
                ins = body(eng)
                if not isinstance(ins, (list, tuple)):
                    ins = [ins]
                if inc >= 16:
                    assert inc == 16 * len(ins), (name, inc, len(ins))
                    for li in ins:
                        li.then_inc(sems[sem], 16)
                else:
                    ins[-1].then_inc(sems[sem], inc)
                if selfwait:
                    eng.wait_ge(sems[sem], val)

        @blk.tensor
        def _(eng):
            run_stream(eng, "pe")

        @blk.vector
        def _(eng):
            run_stream(eng, "dve")

        @blk.scalar
        def _(eng):
            run_stream(eng, "act")

        @blk.gpsimd
        def _(eng):
            run_stream(eng, "pool")

        @blk.sync
        def _(eng):
            run_stream(eng, "sync")

    es.close()
    return nc


def _host_inputs(inp):
    w = {}
    ipw = np.asarray(inp["L0_attn_in_proj_w"], np.float32)
    w["w_inproj"] = trunc_bf16(
        ipw.T.reshape(8, 128, 32, 128).transpose(2, 1, 0, 3).reshape(32, 128, 1024))
    w["b_inproj"] = np.ascontiguousarray(
        np.asarray(inp["L0_attn_in_proj_b"], np.float32).reshape(32, 128).T)
    xp = np.asarray(inp["L0_attn_x_proj_w"], np.float32)
    w["w_xproj1"] = trunc_bf16(
        xp[:80].T.reshape(16, 128, 80).transpose(1, 0, 2).reshape(128, 16 * 80))
    w["w_xprojC"] = trunc_bf16(
        xp[80:96].T.reshape(16, 128, 16).transpose(1, 0, 2).reshape(128, 16 * 16))
    w["w_dtproj"] = trunc_bf16(np.asarray(inp["L0_attn_dt_proj_w"], np.float32).T)
    w["b_dtproj"] = np.ascontiguousarray(
        np.asarray(inp["L0_attn_dt_proj_b"], np.float32).reshape(16, 128).T)
    cw = np.asarray(inp["L0_attn_conv_w"], np.float32)
    w["conv_w4"] = np.ascontiguousarray(
        cw.reshape(16, 128, 4).transpose(1, 0, 2).reshape(128, 64))
    w["conv_bc"] = np.ascontiguousarray(
        np.asarray(inp["L0_attn_conv_b"], np.float32).reshape(16, 128).T)
    al = np.asarray(inp["L0_attn_A_log"], np.float32)
    w["A_logr"] = np.ascontiguousarray(
        al.T.reshape(16, 128, 16).transpose(1, 0, 2).reshape(128, 256))
    w["D_col"] = np.ascontiguousarray(
        np.asarray(inp["L0_attn_D"], np.float32).reshape(16, 128).T)
    opw = np.asarray(inp["L0_attn_out_proj_w"], np.float32)
    w["w_outproj"] = trunc_bf16(
        opw.T.reshape(16, 128, 8, 128).transpose(2, 1, 0, 3).reshape(8, 128, 16 * 128))
    w["b_outproj"] = np.ascontiguousarray(
        np.asarray(inp["L0_attn_out_proj_b"], np.float32).reshape(8, 128).T)
    gwt = np.asarray(inp["L0_ffn_gate_w"], np.float32).T
    w["gate_wT"] = np.ascontiguousarray(
        gwt.reshape(8, 128, 8).transpose(1, 0, 2).reshape(128, 64))
    w1 = np.asarray(inp["L0_ffn_w1"], np.float32)
    w["w1_dev"] = trunc_bf16(
        w1.reshape(8, 8, 128, 28, 128).transpose(0, 3, 2, 1, 4).reshape(8, 28, 128, 1024))
    w3 = np.asarray(inp["L0_ffn_w3"], np.float32)
    w["w3_dev"] = trunc_bf16(
        w3.reshape(8, 8, 128, 28, 128).transpose(0, 3, 2, 1, 4).reshape(8, 28, 128, 1024))
    w2 = np.asarray(inp["L0_ffn_w2"], np.float32)
    w["w2_dev"] = trunc_bf16(w2.reshape(8, 28, 128, 1024))
    w["ident_f"] = np.eye(128, dtype=np.float32)
    w["ident_b"] = trunc_bf16(np.eye(128, dtype=np.float32))
    lt = np.tril(np.ones((128, 128), np.float32), -1).T
    w["LT"] = np.ascontiguousarray(lt)
    bsel = np.zeros((16, 16 * 128), np.float32)
    for s in range(16):
        bsel[s, s * 128:(s + 1) * 128] = 1.0
    w["BSEL"] = trunc_bf16(bsel)
    return w


def kernel(**inputs):
    x = np.asarray(inputs["L0_attn_x"], np.float32)
    shared = _host_inputs(inputs)
    nc = build_program()
    in_maps = []
    for c in range(NCORES):
        m = dict(shared)
        m["x"] = np.ascontiguousarray(x[c])
        in_maps.append(m)
    res = run_bass_kernel_spmd(nc, in_maps, core_ids=list(range(NCORES)))
    out = np.stack([res.results[c]["out"] for c in range(NCORES)], axis=0)
    return out.astype(np.float32)


# version 11
# speedup vs baseline: 1.2277x; 1.0008x over previous
import numpy as np
import ml_dtypes
import concourse.bass as bass
import concourse.mybir as mybir
from concourse.bass_utils import run_bass_kernel_spmd
from concourse.bass_core_rust import IndirectOffsetOnAxis
from contextlib import ExitStack

F32 = mybir.dt.float32
BF16 = mybir.dt.bfloat16
I32 = mybir.dt.int32
ALU = mybir.AluOpType
AF = mybir.ActivationFunctionType
AX = mybir.AxisListType.X
BF = ml_dtypes.bfloat16

T, DM, DI, DS, DTR = 4096, 1024, 2048, 16, 64
NE, FF, CAP = 8, 3584, 1280
NJ = DI // 128
NKD = DM // 128
NF = FF // 128
TC = 1024
NCH = T // TC
NCT = 32
NCAP = CAP // 128
EPS = 1e-5

NCORES = 8
DEBUG_TAPS = False


def trunc_bf16(a):
    a = np.ascontiguousarray(a, dtype=np.float32)
    u = a.view(np.uint16)
    hi = np.ascontiguousarray(u.reshape(-1, 2)[:, 1])
    return hi.reshape(a.shape).view(BF)


ENGINES = ("pe", "dve", "act", "pool", "sync")
SEM_OF = {"pe": "sP", "dve": "sV", "act": "sA", "pool": "sGc", "sync": "sS"}
DMA_POOL = {"sync": ["sSd%d" % i for i in range(12)],
            "pool": ["sGd%d" % i for i in range(8)],
            "act": ["sAd0"]}
ALL_SEMS = sorted(set(SEM_OF.values()) | {s for v in DMA_POOL.values() for s in v})


class Buf:
    __slots__ = ("name", "w", "r")

    def __init__(self, name):
        self.name = name
        self.w = None
        self.r = []


class Sched:
    def __init__(self):
        self.streams = {e: [] for e in ENGINES}
        self.cnt = {s: 0 for s in ALL_SEMS}
        self.waited = {e: {} for e in ENGINES}
        self.rr = {q: 0 for q in DMA_POOL}

    def _wait(self, eng, waits, sem, val):
        seen = self.waited[eng]
        if seen.get(sem, 0) >= val:
            return
        seen[sem] = val
        waits.append((sem, val))

    def emit(self, eng, body, reads=(), writes=(), dma=False, selfwait=False, ndma=1):
        waits = []
        for b in reads:
            if b.w is not None and b.w[0] != eng:
                self._wait(eng, waits, b.w[1], b.w[2])
        for b in writes:
            if b.w is not None and b.w[0] != eng:
                self._wait(eng, waits, b.w[1], b.w[2])
            for (reng, rsem, rval) in b.r:
                if reng != eng:
                    self._wait(eng, waits, rsem, rval)
        if dma:
            pool = DMA_POOL[eng]
            sem = pool[self.rr[eng] % len(pool)]
            self.rr[eng] += 1
            if self.cnt[sem] > 0:
                self._wait(eng, waits, sem, self.cnt[sem])
            inc = 16 * ndma
        else:
            sem = SEM_OF[eng]
            inc = 1
        self.cnt[sem] += inc
        val = self.cnt[sem]
        self.streams[eng].append((waits, body, sem, inc, selfwait, val))
        ev = (eng, sem, val)
        for b in reads:
            b.r.append(ev)
        for b in writes:
            b.w = ev
            b.r = []
        return ev

    def barrier(self):
        snapshot = dict(self.cnt)
        for e in ENGINES:
            waits = []
            for sem, val in snapshot.items():
                if val > 0:
                    self._wait(e, waits, sem, val)
            if waits:
                self.streams[e].append((waits, None, None, 0, False, 0))


def build_program():
    nc = bass.Bass("TRN2", target_bir_lowering=False, debug=False,
                   detect_race_conditions=False)
    S = Sched()
    es = ExitStack()

    def din(name, shape, dt):
        return nc.dram_tensor(name, list(shape), dt, kind="ExternalInput").ap()

    def dscratch(name, shape, dt):
        kind = "ExternalOutput" if DEBUG_TAPS else "Internal"
        return nc.dram_tensor(name, list(shape), dt, kind=kind).ap()

    x_hbm = din("x", (T, DM), F32)
    w_inproj = din("w_inproj", (32, 128, 8 * 128), BF16)
    b_inproj = din("b_inproj", (128, 32), F32)
    w_xproj1 = din("w_xproj1", (128, 16 * 80), BF16)
    w_xprojC = din("w_xprojC", (128, 16 * 16), BF16)
    w_dtproj = din("w_dtproj", (64, DI), BF16)
    b_dtproj = din("b_dtproj", (128, NJ), F32)
    conv_w4 = din("conv_w4", (128, NJ * 4), F32)
    conv_bc = din("conv_bc", (128, NJ), F32)
    A_logr = din("A_logr", (128, NJ * DS), F32)
    D_col = din("D_col", (128, NJ), F32)
    w_outproj = din("w_outproj", (8, 128, 16 * 128), BF16)
    b_outproj = din("b_outproj", (128, 8), F32)
    gate_wT = din("gate_wT", (128, NKD * NE), F32)
    w1_dev = din("w1_dev", (NE, NF, 128, 8 * 128), BF16)
    w3_dev = din("w3_dev", (NE, NF, 128, 8 * 128), BF16)
    w2_dev = din("w2_dev", (NE, NF, 128, DM), BF16)
    ident_f_d = din("ident_f", (128, 128), F32)
    ident_b_d = din("ident_b", (128, 128), BF16)
    LT_d = din("LT", (128, 128), F32)
    BSEL_d = din("BSEL", (16, DS * 128), BF16)

    xn_tok_mir = dscratch("xn_tok_mir", (T, DM), BF16)
    xn_dmaj_mir = dscratch("xn_dmaj_mir", (DM, T), BF16)
    yg_mir = dscratch("yg_mir", (NCH, NJ, 128, TC), BF16)
    disp_dram = dscratch("disp_dram", (NE * CAP, DM), BF16)
    g_dram = dscratch("g_dram", (NE, NF, 128, CAP), BF16)
    eout_dram = dscratch("eout_dram", (NE * CAP, DM), BF16)
    out_hbm = nc.dram_tensor("out", [T, DM], F32, kind="ExternalOutput").ap()

    B_xn_tok_mir = Buf("xn_tok_mir")
    B_xn_dmaj_mir = Buf("xn_dmaj_mir")
    B_yg_mir = Buf("yg_mir")
    B_disp_dram = Buf("disp_dram")
    B_g_dram = Buf("g_dram")
    B_eout_dram = Buf("eout_dram")
    B_out_hbm = Buf("out_hbm")

    sems = {}
    for sname in ALL_SEMS:
        sems[sname] = es.enter_context(nc.semaphore(sname))

    def sb(name, shape, dt, stack=None):
        t = (stack or es).enter_context(nc.sbuf_tensor(name, list(shape), dt))
        return t, Buf(name)

    def psum(name, shape, dt, stack=None):
        t = (stack or es).enter_context(nc.psum_tensor(name, list(shape), dt))
        return t, Buf(name)

    identf, B_identf = sb("identf", (128, 128), F32)
    identb, B_identb = sb("identb", (128, 128), BF16)

    BSEL_sb, B_BSEL = sb("BSEL_sb", (16, DS * 128), BF16)
    A_sb, B_A = sb("A_sb", (128, NJ * DS), F32)
    invA_sb, B_invA = sb("invA_sb", (128, NJ * DS), F32)
    bin_sb, B_bin = sb("bin_sb", (128, 32), F32)
    bdt_sb, B_bdt = sb("bdt_sb", (128, NJ), F32)
    cw_sb, B_cw = sb("cw_sb", (128, NJ * 4), F32)
    cb_sb, B_cb = sb("cb_sb", (128, NJ), F32)
    Dc_sb, B_Dc = sb("Dc_sb", (128, NJ), F32)
    bout_sb, B_bout = sb("bout_sb", (128, 8), F32)
    wxp1_sb, B_wxp1 = sb("wxp1_sb", (128, 16 * 80), BF16)
    wxpC_sb, B_wxpC = sb("wxpC_sb", (128, 16 * 16), BF16)
    wdt_sb, B_wdt = sb("wdt_sb", (64, DI), BF16)
    dsl1_sb, B_dsl1 = sb("dsl1", (128, 32), I32)
    dsl2_sb, B_dsl2 = sb("dsl2", (128, 32), I32)
    pm1_sb, B_pm1 = sb("pm1", (128, 32), F32)
    pm2_sb, B_pm2 = sb("pm2", (128, 32), F32)
    gstate, B_gstate = sb("gstate", (128, NJ * DS), BF16)
    xinst, B_xinst = sb("xinst", (128, NJ * 4), BF16)

    cst_sb, B_cst = sb("cst_sb", (128, 8), F32)
    CONSTS = {EPS: 0, 1.0: 1, -0.5: 2, -1.0: 3, 0.0: 4}
    for v, i in CONSTS.items():
        nc.const_aps.register(F32, v, cst_sb[:, i:i + 1])

    def _fillconsts(eng):
        last = None
        for v, i in CONSTS.items():
            last = eng.memset(cst_sb[:, i:i + 1], float(v))
        return last
    S.emit("dve", _fillconsts, writes=[B_cst])

    BCREG = {}

    def _mkbc(eng):
        r = eng.alloc_register("bcreg")
        BCREG["r"] = r
        return eng.reg_mov(r, NE * CAP - 1)
    S.emit("pool", _mkbc)
    S.barrier()

    PS = []
    B_PS = []
    for i in range(7):
        t, b = psum("ps%d" % i, (128, 512), F32)
        PS.append(t)
        B_PS.append(b)
    PSB, B_PSB = psum("psb", (128, 512), BF16)

    p0 = ExitStack()
    LT_sb, B_LT = sb("LT_sb", (128, 128), F32, p0)
    gw_sb, B_gw = sb("gw_sb", (128, NKD * NE), F32, p0)
    scr0, B_scr0 = sb("scr0", (128, NJ * DS), F32, p0)

    def ldconst(dst, src, bufs):
        S.emit("sync", (lambda d=dst, s=src: (lambda eng: [eng.dma_start(out=d, in_=s)]))(),
               writes=bufs, dma=True)

    ldconst(identf[:], ident_f_d, [B_identf])
    ldconst(identb[:], ident_b_d, [B_identb])
    ldconst(LT_sb[:], LT_d, [B_LT])
    ldconst(BSEL_sb[:], BSEL_d, [B_BSEL])
    ldconst(A_sb[:], A_logr, [B_A])
    ldconst(bin_sb[:], b_inproj, [B_bin])
    ldconst(bdt_sb[:], b_dtproj, [B_bdt])
    ldconst(cw_sb[:], conv_w4, [B_cw])
    ldconst(cb_sb[:], conv_bc, [B_cb])
    ldconst(Dc_sb[:], D_col, [B_Dc])
    ldconst(bout_sb[:], b_outproj, [B_bout])
    ldconst(gw_sb[:], gate_wT, [B_gw])
    ldconst(wxp1_sb[:], w_xproj1, [B_wxp1])
    ldconst(wxpC_sb[:], w_xprojC, [B_wxpC])
    ldconst(wdt_sb[:], w_dtproj, [B_wdt])


    def _actwarm(eng):
        last = None
        for fn in (AF.Square, AF.Ln, AF.Exp, AF.Silu, AF.Sigmoid, AF.Ln, AF.Exp):
            last = eng.activation(out=cst_sb[:, 5:6], in_=cst_sb[:, 1:2], func=fn)
        last = eng.sign(cst_sb[:, 5:6], cst_sb[:, 1:2])
        return last
    S.emit("act", _actwarm, reads=[B_cst], writes=[B_cst])

    S.emit("act", lambda eng: [eng.activation(out=scr0[:], in_=A_sb[:], func=AF.Exp)],
           reads=[B_A], writes=[B_scr0], selfwait=True)
    S.emit("dve", lambda eng: [eng.tensor_scalar(out=A_sb[:], in0=scr0[:], scalar1=-1.0,
                                                 scalar2=None, op0=ALU.mult)],
           reads=[B_scr0], writes=[B_A], selfwait=True)
    S.emit("dve", lambda eng: [eng.tensor_single_scalar(invA_sb[:], scr0[:], 1e-6, ALU.is_lt)],
           reads=[B_scr0], writes=[B_invA], selfwait=True)
    S.emit("dve", lambda eng: [eng.tensor_scalar(out=scr0[:], in0=A_sb[:], scalar1=-1.0,
                                                 scalar2=1.0, op0=ALU.mult, op1=ALU.add)],
           reads=[B_A], writes=[B_scr0], selfwait=True)
    S.emit("dve", lambda eng: [eng.tensor_mul(scr0[:], scr0[:], invA_sb[:])],
           reads=[B_invA], writes=[B_scr0], selfwait=True)
    S.emit("dve", lambda eng: [eng.tensor_add(scr0[:], scr0[:], A_sb[:])],
           reads=[B_A], writes=[B_scr0], selfwait=True)
    S.emit("dve", lambda eng: [eng.reciprocal(invA_sb[:], scr0[:])],
           reads=[B_scr0], writes=[B_invA], selfwait=True)
    S.barrier()

    p1 = ExitStack()
    xnd, B_xnd = sb("xnd", (128, NKD * T), BF16, p1)
    xt, B_xt = sb("xt", (128, DM), F32, p1)
    sq, B_sq = sb("sq", (128, DM), F32, p1)
    mcol, B_mcol = sb("mcol", (128, 8), F32, p1)
    xnf, B_xnf = sb("xnf", (128, DM), F32, p1)
    xnb, B_xnb = sb("xnb", (128, DM), BF16, p1)
    xnTf, B_xnTf = sb("xnTf", (128, NKD * 128), F32, p1)
    LG, B_LG = sb("LG", (128, NCT * NE), F32, p1)
    if DEBUG_TAPS:
        mcol_tap = dscratch("mcol_tap", (128, 8), F32)
        xt_tap = dscratch("xt_tap", (128, DM), F32)
        B_mcol_tap, B_xt_tap = Buf("mcol_tap"), Buf("xt_tap")
        dsl_tap = dscratch("dsl_tap", (128, 64), I32)
        pm_tap = dscratch("pm_tap", (128, 64), F32)
        B_dsl_tap, B_pm_tap = Buf("dsl_tap"), Buf("pm_tap")

    for c in range(NCT):
        xrows = x_hbm.rearrange("(p c) d -> c p d", c=NCT)[c]
        S.emit("sync", (lambda xr=xrows: (lambda eng: [eng.dma_start(out=xt[:], in_=xr)]))(),
               reads=[], writes=[B_xt], dma=True, selfwait=True)
        S.emit("act", lambda eng: [eng.activation(out=sq[:], in_=xt[:], func=AF.Square)],
               reads=[B_xt], writes=[B_sq], selfwait=True)
        S.emit("dve", lambda eng: [eng.reduce_sum(mcol[:, 0:1], sq[:], axis=AX)],
               reads=[B_sq], writes=[B_mcol], selfwait=True)
        S.emit("dve", lambda eng: [eng.reduce_sum(mcol[:, 1:2], xt[:], axis=AX)],
               reads=[B_xt], writes=[B_mcol], selfwait=True)
        def _stats1(eng):
            eng.tensor_scalar(out=mcol[:, 1:2], in0=mcol[:, 1:2], scalar1=1.0 / DM,
                              scalar2=None, op0=ALU.mult)
            return eng.tensor_scalar(out=mcol[:, 2:3], in0=mcol[:, 0:1], scalar1=1.0 / DM,
                                     scalar2=None, op0=ALU.mult)
        S.emit("dve", lambda eng: [_stats1(eng)], reads=[], writes=[B_mcol], selfwait=True)
        S.emit("dve", lambda eng: [eng.tensor_mul(mcol[:, 3:4], mcol[:, 1:2], mcol[:, 1:2])],
               reads=[], writes=[B_mcol], selfwait=True)
        S.emit("dve", lambda eng: [eng.tensor_sub(mcol[:, 2:3], mcol[:, 2:3], mcol[:, 3:4])],
               reads=[], writes=[B_mcol], selfwait=True)
        S.emit("act", lambda eng: [eng.activation(out=mcol[:, 4:5], in_=mcol[:, 2:3],
                                                  func=AF.Ln, bias=EPS)],
               reads=[B_mcol], writes=[B_mcol], selfwait=True)
        S.emit("act", lambda eng: [eng.activation(out=mcol[:, 5:6], in_=mcol[:, 4:5],
                                                  func=AF.Exp, scale=-0.5)],
               reads=[B_mcol], writes=[B_mcol], selfwait=True)
        if DEBUG_TAPS and c == 0:
            S.emit("sync", lambda eng: [eng.dma_start(out=mcol_tap, in_=mcol[:])],
                   reads=[B_mcol], writes=[B_mcol_tap], dma=True)
            S.emit("sync", lambda eng: [eng.dma_start(out=xt_tap, in_=xt[:])],
                   reads=[B_xt], writes=[B_xt_tap], dma=True)
        S.emit("dve", lambda eng: [eng.tensor_scalar(out=xnf[:], in0=xt[:],
                                                     scalar1=mcol[:, 1:2], scalar2=mcol[:, 5:6],
                                                     op0=ALU.subtract, op1=ALU.mult)],
               reads=[B_xt, B_mcol], writes=[B_xnf], selfwait=True)
        S.emit("dve", lambda eng: [eng.tensor_copy(xnb[:], xnf[:])],
               reads=[B_xnf], writes=[B_xnb], selfwait=True)
        mrows = xn_tok_mir.rearrange("(p c) d -> c p d", c=NCT)[c]
        S.emit("sync", (lambda mr=mrows: (lambda eng: [eng.dma_start(out=mr, in_=xnb[:])]))(),
               reads=[B_xnb], writes=[B_xn_tok_mir], dma=True, selfwait=True)
        for kd in range(NKD):
            pk = kd % 2
            S.emit("pe", (lambda kd=kd, pk=pk: (lambda eng: [
                eng.transpose(PS[pk][:, 0:128], xnf[:, kd * 128:(kd + 1) * 128], identf[:])
            ]))(), reads=[B_xnf, B_identf], writes=[B_PS[pk]], selfwait=True)
            def _evac(eng, kd=kd, pk=pk, c=c):
                eng.tensor_copy(xnTf[:, kd * 128:(kd + 1) * 128], PS[pk][:, 0:128])
                dst = xnd[:].rearrange("p (k t) -> p k t", k=NKD)[:, kd]
                dst = bass.AP(dst.tensor, dst.offset + c, [dst.ap[0], [NCT, 128]])
                return eng.tensor_copy(dst, PS[pk][:, 0:128])
            S.emit("dve", (lambda f=_evac: (lambda eng: [f(eng)]))(),
                   reads=[B_PS[pk]], writes=[B_xnTf, B_xnd], selfwait=True)
        def _router(eng, c=c):
            last = None
            for kd in range(NKD):
                last = eng.matmul(PS[2][:, 0:8], xnTf[:, kd * 128:(kd + 1) * 128],
                                  gw_sb[:, kd * NE:(kd + 1) * NE],
                                  start=(kd == 0), stop=(kd == NKD - 1))
            return last
        S.emit("pe", (lambda f=_router: (lambda eng: [f(eng)]))(),
               reads=[B_xnTf, B_gw], writes=[B_PS[2]], selfwait=True)
        S.emit("dve", (lambda c=c: (lambda eng: [
            eng.tensor_copy(LG[:, c * NE:(c + 1) * NE], PS[2][:, 0:8])]))(),
               reads=[B_PS[2]], writes=[B_LG], selfwait=True)

    def _stmir(eng):
        ins = []
        for kd in range(NKD):
            ins.append(eng.dma_start(out=xn_dmaj_mir[kd * 128:(kd + 1) * 128, :],
                                     in_=xnd[:, kd * T:(kd + 1) * T]))
        return ins
    S.emit("sync", lambda eng: _stmir(eng), reads=[B_xnd], writes=[B_xn_dmaj_mir],
           dma=True, ndma=NKD)

    LG3 = LG[:].rearrange("p (c e) -> p c e", e=NE)
    rt = ExitStack()
    m1, B_m1 = sb("m1", (128, NCT), F32, rt)
    nm1, B_nm1 = sb("nm1", (128, NCT), F32, rt)
    ee, B_ee = sb("ee", (128, NCT * NE), F32, rt)
    ssum, B_ssum = sb("ssum", (128, NCT), F32, rt)
    p1c, B_p1c = sb("p1c", (128, NCT), F32, rt)
    d3, B_d3 = sb("d3", (128, NCT * NE), F32, rt)
    sg, B_sg = sb("sgn", (128, NCT * NE), F32, rt)
    oh, B_oh = sb("oh", (128, NCT * NE), F32, rt)
    cand, B_cand = sb("cand", (128, NCT * NE), F32, rt)
    idx1, B_idx1 = sb("idx1", (128, NCT), F32, rt)
    idx2, B_idx2 = sb("idx2", (128, NCT), F32, rt)
    ohx1, B_ohx1 = sb("ohx1", (128, NCT * NE), F32, rt)
    ohx2, B_ohx2 = sb("ohx2", (128, NCT * NE), F32, rt)
    l2t, B_l2t = sb("l2t", (128, NCT * NE), F32, rt)
    m2, B_m2 = sb("m2", (128, NCT), F32, rt)
    ohseq, B_ohseq = sb("ohseq", (128, NE * 64), F32, rt)
    zer64, B_zer64 = sb("zer64", (128, 64), F32, rt)
    Spe, B_Spe = sb("Spe", (128, NE), F32, rt)
    P0sb, B_P0sb = sb("P0sb", (128, NE), F32, rt)
    posi, B_posi = sb("posi", (128, NE * 64), F32, rt)
    qtmp, B_qtmp = sb("qtmp", (128, NCT * NE), F32, rt)
    pos1, B_pos1 = sb("pos1", (128, NCT), F32, rt)
    pos2, B_pos2 = sb("pos2", (128, NCT), F32, rt)
    keep, B_keep = sb("keep", (128, NCT), F32, rt)
    dslf, B_dslf = sb("dslf", (128, NCT), F32, rt)
    xnsb, B_xnsb = sb("xnsb", (128, DM), BF16, rt)

    d33 = d3[:].rearrange("p (c e) -> p c e", e=NE)
    oh3 = oh[:].rearrange("p (c e) -> p c e", e=NE)
    cand3 = cand[:].rearrange("p (c e) -> p c e", e=NE)
    ohx13 = ohx1[:].rearrange("p (c e) -> p c e", e=NE)
    ohx23 = ohx2[:].rearrange("p (c e) -> p c e", e=NE)
    l23 = l2t[:].rearrange("p (c e) -> p c e", e=NE)
    ee3 = ee[:].rearrange("p (c e) -> p c e", e=NE)
    qt3 = qtmp[:].rearrange("p (c e) -> p c e", e=NE)
    oq2 = ohseq[:].rearrange("p (e s) -> p e s", s=64)
    po2 = posi[:].rearrange("p (e s) -> p e s", s=64)

    S.emit("dve", lambda eng: [eng.reduce_max(m1[:], LG3, axis=AX)],
           reads=[B_LG], writes=[B_m1], selfwait=True)
    S.emit("dve", lambda eng: [eng.reduce_max(nm1[:], LG3, axis=AX, negate=True)],
           reads=[B_LG], writes=[B_nm1], selfwait=True)

    def _subm(eng, src3, mm, B=None):
        last = None
        for e in range(NE):
            last = eng.tensor_sub(d33[:, :, e], src3[:, :, e], mm[:])
        return last
    S.emit("dve", lambda eng: [_subm(eng, LG3, m1)], reads=[B_LG, B_m1], writes=[B_d3])
    S.emit("act", lambda eng: [eng.activation(out=ee[:], in_=d3[:], func=AF.Exp)],
           reads=[B_d3], writes=[B_ee])
    S.emit("dve", lambda eng: [eng.reduce_sum(ssum[:], ee3, axis=AX)],
           reads=[B_ee], writes=[B_ssum], selfwait=True)
    S.emit("dve", lambda eng: [eng.reciprocal(p1c[:], ssum[:])],
           reads=[B_ssum], writes=[B_p1c], selfwait=True)
    S.emit("act", lambda eng: [eng.sign(sg[:], d3[:])], reads=[B_d3], writes=[B_sg])
    S.emit("dve", lambda eng: [eng.tensor_scalar(out=oh[:], in0=sg[:], scalar1=1.0,
                                                 scalar2=None, op0=ALU.add)],
           reads=[B_sg], writes=[B_oh], selfwait=True)

    def _cand(eng):
        last = None
        for e in range(NE):
            last = eng.tensor_scalar(out=cand3[:, :, e], in0=oh3[:, :, e],
                                     scalar1=float(e - 99), scalar2=99.0,
                                     op0=ALU.mult, op1=ALU.add)
        return last
    S.emit("dve", lambda eng: [_cand(eng)], reads=[B_oh], writes=[B_cand], selfwait=True)
    S.emit("dve", lambda eng: [eng.tensor_reduce(idx1[:], cand3, axis=AX, op=ALU.min)],
           reads=[B_cand], writes=[B_idx1], selfwait=True)

    def _ohx(eng, idx, dst3):
        last = None
        for e in range(NE):
            last = eng.tensor_scalar(out=dst3[:, :, e], in0=idx[:], scalar1=float(-e),
                                     scalar2=None, op0=ALU.add)
        return last
    S.emit("dve", lambda eng: [_ohx(eng, idx1, ohx13)], reads=[B_idx1], writes=[B_ohx1], selfwait=True)
    S.emit("act", lambda eng: [eng.square(ohx1[:], ohx1[:])], reads=[B_ohx1], writes=[B_ohx1])
    S.emit("dve", lambda eng: [eng.tensor_single_scalar(ohx1[:], ohx1[:], 0.5, ALU.is_lt)],
           reads=[B_ohx1], writes=[B_ohx1], selfwait=True)
    S.emit("dve", lambda eng: [eng.scalar_tensor_tensor(out=l2t[:], in0=ohx1[:],
                                                        scalar=-1e30, in1=LG[:],
                                                        op0=ALU.mult, op1=ALU.add)],
           reads=[B_ohx1, B_LG], writes=[B_l2t], selfwait=True)
    S.emit("dve", lambda eng: [eng.reduce_max(m2[:], l23, axis=AX)],
           reads=[B_l2t], writes=[B_m2], selfwait=True)
    S.emit("dve", lambda eng: [_subm(eng, l23, m2)], reads=[B_l2t, B_m2], writes=[B_d3])
    S.emit("act", lambda eng: [eng.sign(sg[:], d3[:])], reads=[B_d3], writes=[B_sg])
    S.emit("dve", lambda eng: [eng.tensor_scalar(out=oh[:], in0=sg[:], scalar1=1.0,
                                                 scalar2=None, op0=ALU.add)],
           reads=[B_sg], writes=[B_oh], selfwait=True)
    S.emit("dve", lambda eng: [_cand(eng)], reads=[B_oh], writes=[B_cand], selfwait=True)
    S.emit("dve", lambda eng: [eng.tensor_reduce(idx2[:], cand3, axis=AX, op=ALU.min)],
           reads=[B_cand], writes=[B_idx2], selfwait=True)
    S.emit("dve", lambda eng: [_ohx(eng, idx2, ohx23)], reads=[B_idx2], writes=[B_ohx2], selfwait=True)
    S.emit("act", lambda eng: [eng.square(ohx2[:], ohx2[:])], reads=[B_ohx2], writes=[B_ohx2])
    S.emit("dve", lambda eng: [eng.tensor_single_scalar(ohx2[:], ohx2[:], 0.5, ALU.is_lt)],
           reads=[B_ohx2], writes=[B_ohx2], selfwait=True)
    S.emit("dve", lambda eng: [eng.tensor_sub(m2[:], m2[:], m1[:])],
           reads=[B_m1], writes=[B_m2], selfwait=True)
    S.emit("act", lambda eng: [eng.activation(out=m2[:], in_=m2[:], func=AF.Exp)],
           reads=[B_m2], writes=[B_m2])
    S.emit("dve", lambda eng: [eng.tensor_mul(m2[:], m2[:], p1c[:])],
           reads=[B_p1c], writes=[B_m2], selfwait=True)
    def _ilv(eng):
        last = None
        for e in range(NE):
            src1 = ohx13[:, :, e]
            src2 = ohx23[:, :, e]
            d1 = bass.AP(oq2.tensor, oq2.offset + e * 64, [oq2.ap[0], [2, 32]])
            d2 = bass.AP(oq2.tensor, oq2.offset + e * 64 + 1, [oq2.ap[0], [2, 32]])
            eng.tensor_copy(d1, src1)
            last = eng.tensor_copy(d2, src2)
        return last
    S.emit("dve", lambda eng: [_ilv(eng)], reads=[B_ohx1, B_ohx2], writes=[B_ohseq], selfwait=True)
    S.emit("dve", lambda eng: [eng.memset(zer64[:], 0.0)], writes=[B_zer64], selfwait=True)
    S.emit("dve", lambda eng: [eng.reduce_sum(Spe[:], oq2, axis=AX)],
           reads=[B_ohseq], writes=[B_Spe], selfwait=True)
    S.emit("pe", lambda eng: [eng.matmul(PS[2][:, 0:8], LT_sb[:], Spe[:],
                                         start=True, stop=True)],
           reads=[B_LT, B_Spe], writes=[B_PS[2]])
    S.emit("dve", lambda eng: [eng.tensor_copy(P0sb[:], PS[2][:, 0:8])],
           reads=[B_PS[2]], writes=[B_P0sb], selfwait=True)
    for e in range(NE):
        S.emit("dve", (lambda e=e: (lambda eng: [
            eng.tensor_tensor_scan(out=po2[:, e], data0=oq2[:, e], data1=zer64[:],
                                   initial=P0sb[:, e:e + 1], op0=ALU.add, op1=ALU.add)
        ]))(), reads=[B_ohseq, B_zer64, B_P0sb], writes=[B_posi], selfwait=True)

    def _posk(eng, which, dst):
        last = None
        for e in range(NE):
            srcpos = bass.AP(po2.tensor, po2.offset + e * 64 + which, [po2.ap[0], [2, 32]])
            src_oh = (ohx13 if which == 0 else ohx23)[:, :, e]
            last = eng.tensor_mul(qt3[:, :, e], src_oh, srcpos)
        return last
    S.emit("dve", lambda eng: [_posk(eng, 0, pos1)], reads=[B_posi, B_ohx1], writes=[B_qtmp], selfwait=True)
    S.emit("dve", lambda eng: [eng.reduce_sum(pos1[:], qt3, axis=AX)],
           reads=[B_qtmp], writes=[B_pos1], selfwait=True)
    S.emit("dve", lambda eng: [_posk(eng, 1, pos2)], reads=[B_posi, B_ohx2], writes=[B_qtmp], selfwait=True)
    S.emit("dve", lambda eng: [eng.reduce_sum(pos2[:], qt3, axis=AX)],
           reads=[B_qtmp], writes=[B_pos2], selfwait=True)

    def _mkdsl(idxk, posk, pk_src, B_idxk, B_posk, B_pk_src, dsl_i, pm_dst, B_dsl_i, B_pm):
        S.emit("dve", lambda eng: [eng.tensor_single_scalar(keep[:], posk[:],
                                                            float(CAP) + 0.5, ALU.is_lt)],
               reads=[B_posk], writes=[B_keep], selfwait=True)
        S.emit("dve", lambda eng: [eng.tensor_mul(pm_dst[:], pk_src[:], keep[:])],
               reads=[B_pk_src, B_keep], writes=[B_pm], selfwait=True)
        S.emit("dve", lambda eng: [eng.tensor_scalar(out=dslf[:], in0=idxk[:],
                                                     scalar1=float(CAP), scalar2=-1.0,
                                                     op0=ALU.mult, op1=ALU.add)],
               reads=[B_idxk], writes=[B_dslf], selfwait=True)
        S.emit("dve", lambda eng: [eng.tensor_add(dslf[:], dslf[:], posk[:])],
               reads=[B_posk], writes=[B_dslf], selfwait=True)
        S.emit("dve", lambda eng: [eng.tensor_mul(dslf[:], dslf[:], keep[:])],
               reads=[B_keep], writes=[B_dslf], selfwait=True)
        S.emit("dve", lambda eng: [eng.tensor_scalar(out=qtmp[:, 0:NCT], in0=keep[:],
                                                     scalar1=-2e6, scalar2=2e6,
                                                     op0=ALU.mult, op1=ALU.add)],
               reads=[B_keep], writes=[B_qtmp], selfwait=True)
        S.emit("dve", lambda eng: [eng.tensor_add(dslf[:], dslf[:], qtmp[:, 0:NCT])],
               reads=[B_qtmp], writes=[B_dslf], selfwait=True)
        S.emit("dve", lambda eng: [eng.tensor_copy(dsl_i[:], dslf[:])],
               reads=[B_dslf], writes=[B_dsl_i], selfwait=True)

    _mkdsl(idx1, pos1, p1c, B_idx1, B_pos1, B_p1c, dsl1_sb, pm1_sb, B_dsl1, B_pm1)
    _mkdsl(idx2, pos2, m2, B_idx2, B_pos2, B_m2, dsl2_sb, pm2_sb, B_dsl2, B_pm2)

    if DEBUG_TAPS:
        def _dsltap(eng):
            return [eng.dma_start(out=dsl_tap[:, 0:32], in_=dsl1_sb[:]),
                    eng.dma_start(out=dsl_tap[:, 32:64], in_=dsl2_sb[:]),
                    eng.dma_start(out=pm_tap[:, 0:32], in_=pm1_sb[:]),
                    eng.dma_start(out=pm_tap[:, 32:64], in_=pm2_sb[:])]
        S.emit("sync", _dsltap, reads=[B_dsl1, B_dsl2, B_pm1, B_pm2],
               writes=[B_dsl_tap, B_pm_tap], dma=True, ndma=4)

    for c in range(NCT):
        mrows = xn_tok_mir.rearrange("(p c) d -> c p d", c=NCT)[c]
        S.emit("sync", (lambda mr=mrows: (lambda eng: [eng.dma_start(out=xnsb[:], in_=mr)]))(),
               reads=[B_xn_tok_mir], writes=[B_xnsb], dma=True)
        for (dsl_i, B_dsl_i) in ((dsl1_sb, B_dsl1), (dsl2_sb, B_dsl2)):
            S.emit("pool", (lambda c=c, dsl_i=dsl_i: (lambda eng: [
                eng.indirect_dma_start(
                    out=disp_dram, out_offset=IndirectOffsetOnAxis(ap=dsl_i[:, c:c + 1], axis=0),
                    in_=xnsb[:], in_offset=None,
                    bounds_check=BCREG["r"], oob_is_err=False)
            ]))(), reads=[B_xnsb, B_dsl_i], writes=[B_disp_dram], dma=True)

    rt.close()
    p1.close()
    p0.close()
    S.barrier()

    p2 = ExitStack()
    xnch, B_xnch = sb("xnch", (128, NKD * TC), BF16, p2)
    wslab = []
    B_wslab = []
    for i in range(2):
        t, b = sb("wslab%d" % i, (128, 8 * 128), BF16, p2)
        wslab.append(t)
        B_wslab.append(b)
    xc, B_xc = sb("xc", (128, NJ * TC), BF16, p2)
    Ball, B_Ball = sb("Ball", (128, DS * TC), BF16, p2)
    Call, B_Call = sb("Call", (128, DS * TC), BF16, p2)
    dtlow, B_dtlow = sb("dtlow", (64, TC), BF16, p2)
    Brow, B_Brow = sb("Brow", (16, TC), BF16, p2)
    Crow, B_Crow = sb("Crow", (16, TC), BF16, p2)
    xin = []
    B_xin = []
    for i in range(2):
        t, b = sb("xin%d" % i, (128, TC + 3), BF16, p2)
        xin.append(t)
        B_xin.append(b)
    SCRB = []
    B_SCRB = []
    for i in range(2):
        t, b = sb("scrb%d" % i, (128, TC), BF16, p2)
        SCRB.append(t)
        B_SCRB.append(b)
    dtj = []
    B_dtj = []
    silz = []
    B_silz = []
    for i in range(2):
        t, b = sb("dtj%d" % i, (128, TC), BF16, p2)
        dtj.append(t)
        B_dtj.append(b)
        t, b = sb("silz%d" % i, (128, TC), BF16, p2)
        silz.append(t)
        B_silz.append(b)
    av = []
    B_av = []
    wv = []
    B_wv = []
    uv = []
    B_uv = []
    hv = []
    B_hv = []
    pv = []
    B_pv = []
    dg = []
    B_dg = []
    for i in range(1):
        t, b = sb("wv%d" % i, (128, TC), BF16, p2)
        wv.append(t)
        B_wv.append(b)
    for i in range(2):
        t, b = sb("hv%d" % i, (128, TC), BF16, p2)
        hv.append(t)
        B_hv.append(b)
    for i in range(2):
        t, b = sb("av%d" % i, (128, TC), F32, p2)
        av.append(t)
        B_av.append(b)
    for i in range(4):
        t, b = sb("pv%d" % i, (128, TC), BF16, p2)
        pv.append(t)
        B_pv.append(b)
        t, b = sb("dg%d" % i, (128, 128), BF16, p2)
        dg.append(t)
        B_dg.append(b)

    B_gscol = [Buf("gs%d" % i) for i in range(NJ * DS)]
    S.emit("dve", lambda eng: [eng.memset(gstate[:], 0.0)],
           writes=[B_gstate] + B_gscol)
    S.emit("dve", lambda eng: [eng.memset(xinst[:], 0.0)], writes=[B_xinst])

    dspr = []
    B_dspr = []
    for i in range(2):
        t, b = sb("dspr%d" % i, (128, DM), BF16, p2)
        dspr.append(t)
        B_dspr.append(b)
    dispTcc, B_dispTcc = sb("dispTcc", (128, NKD * 512), BF16, p2)
    w1r = []
    B_w1r = []
    w3r = []
    B_w3r = []
    silcc = []
    B_silcc = []
    h3c = []
    B_h3c = []
    gcc = []
    B_gcc = []
    for i in range(2):
        t, b = sb("w1r%d" % i, (128, 8 * 128), BF16, p2)
        w1r.append(t)
        B_w1r.append(b)
        t, b = sb("w3r%d" % i, (128, 8 * 128), BF16, p2)
        w3r.append(t)
        B_w3r.append(b)
    for i in range(3):
        t, b = sb("silcc%d" % i, (128, 512), BF16, p2)
        silcc.append(t)
        B_silcc.append(b)
        t, b = sb("h3c%d" % i, (128, 512), BF16, p2)
        h3c.append(t)
        B_h3c.append(b)
        t, b = sb("gcc%d" % i, (128, 512), BF16, p2)
        gcc.append(t)
        B_gcc.append(b)

    UH = [0]
    MOEA = []
    for e in range(NE):
        for cc in range(3):
            n0 = cc * 512
            ccw = min(CAP, n0 + 512) - n0
            qlist = list(range(cc * 4, min(NCAP, cc * 4 + 4)))
            for qi, q in enumerate(qlist):
                def trans_unit(e=e, q=q, qi=qi, ccw=ccw):
                    dq = q % 2
                    S.emit("sync", (lambda e=e, q=q, dq=dq: (lambda eng: [
                        eng.dma_start(out=dspr[dq][:],
                                      in_=disp_dram[e * CAP + q * 128: e * CAP + (q + 1) * 128, :])]))(),
                           reads=[B_disp_dram], writes=[B_dspr[dq]], dma=True)
                    for half in range(2):
                        def _tr(eng, dq=dq, half=half):
                            last = None
                            for k in range(4):
                                kd = 4 * half + k
                                last = eng.transpose(PSB[:, k * 128:(k + 1) * 128],
                                                     dspr[dq][:, kd * 128:(kd + 1) * 128],
                                                     identb[:])
                            return last
                        S.emit("pe", _tr, reads=[B_dspr[dq], B_identb], writes=[B_PSB])
                        def _trev(eng, half=half, qi=qi, ccw=ccw):
                            last = None
                            for k in range(4):
                                kd = 4 * half + k
                                last = eng.copy(
                                    dispTcc[:, kd * ccw + qi * 128: kd * ccw + qi * 128 + 128],
                                    PSB[:, k * 128:(k + 1) * 128])
                            return last
                        S.emit("act", _trev, reads=[B_PSB], writes=[B_dispTcc])
                    return None
                MOEA.append(trans_unit)
            for ft in range(NF):
                def h13_unit(e=e, cc=cc, ft=ft, n0=n0, ccw=ccw):
                    wr = UH[0] % 2
                    ur = UH[0] % 3
                    UH[0] += 1
                    S.emit("sync", (lambda e=e, ft=ft, wr=wr: (lambda eng: [
                        eng.dma_start(out=w1r[wr][:], in_=w1_dev[e, ft]),
                        eng.dma_start(out=w3r[wr][:], in_=w3_dev[e, ft])]))(),
                           writes=[B_w1r[wr], B_w3r[wr]], dma=True, ndma=2)
                    def _h1(eng, wr=wr, ccw=ccw):
                        last = None
                        for kd in range(NKD):
                            last = eng.matmul(PS[5][:, 0:ccw],
                                              w1r[wr][:, kd * 128:(kd + 1) * 128],
                                              dispTcc[:, kd * ccw: kd * ccw + ccw],
                                              start=(kd == 0), stop=(kd == NKD - 1))
                        return last
                    S.emit("pe", _h1, reads=[B_w1r[wr], B_dispTcc], writes=[B_PS[5]])
                    S.emit("act", (lambda ccw=ccw, ur=ur: (lambda eng: [
                        eng.activation(out=silcc[ur][:, 0:ccw], in_=PS[5][:, 0:ccw],
                                       func=AF.Silu)]))(),
                           reads=[B_PS[5]], writes=[B_silcc[ur]])
                    def _h3(eng, wr=wr, ccw=ccw):
                        last = None
                        for kd in range(NKD):
                            last = eng.matmul(PS[6][:, 0:ccw],
                                              w3r[wr][:, kd * 128:(kd + 1) * 128],
                                              dispTcc[:, kd * ccw: kd * ccw + ccw],
                                              start=(kd == 0), stop=(kd == NKD - 1))
                        return last
                    S.emit("pe", _h3, reads=[B_w3r[wr], B_dispTcc], writes=[B_PS[6]])
                    S.emit("act", (lambda ccw=ccw, ur=ur: (lambda eng: [
                        eng.copy(h3c[ur][:, 0:ccw], PS[6][:, 0:ccw])]))(),
                           reads=[B_PS[6]], writes=[B_h3c[ur]])
                    def tail(e=e, ft=ft, n0=n0, ccw=ccw, ur=ur):
                        S.emit("dve", (lambda ccw=ccw, ur=ur: (lambda eng: [
                            eng.tensor_mul(gcc[ur][:, 0:ccw], h3c[ur][:, 0:ccw],
                                           silcc[ur][:, 0:ccw])]))(),
                               reads=[B_h3c[ur], B_silcc[ur]], writes=[B_gcc[ur]])
                        S.emit("pool", (lambda e=e, ft=ft, n0=n0, ccw=ccw, ur=ur: (lambda eng: [
                            eng.dma_start(out=g_dram[e, ft, :, n0:n0 + ccw],
                                          in_=gcc[ur][:, 0:ccw])]))(),
                               reads=[B_gcc[ur]], writes=[B_g_dram], dma=True)
                    return tail
                MOEA.append(h13_unit)
    MOEA_POS = [0]
    PENDING_TAILS = []

    def drain_pe(k):
        while k > 0 and MOEA_POS[0] < len(MOEA):
            t = MOEA[MOEA_POS[0]]()
            if t is not None:
                PENDING_TAILS.append(t)
            MOEA_POS[0] += 1
            k -= 1

    def flush_tails():
        for t in PENDING_TAILS:
            t()
        PENDING_TAILS.clear()

    WCTR = [0]
    for ch in range(NCH):
        c0 = ch * TC
        def _ldxn(eng, c0=c0):
            ins = []
            for kd in range(NKD):
                ins.append(eng.dma_start(out=xnch[:, kd * TC:(kd + 1) * TC],
                                         in_=xn_dmaj_mir[kd * 128:(kd + 1) * 128, c0:c0 + TC]))
            return ins
        S.emit("sync", _ldxn, reads=[B_xn_dmaj_mir], writes=[B_xnch], dma=True, ndma=NKD)

        for J in range(NJ):
            slab = WCTR[0] % 2
            WCTR[0] += 1
            S.emit("sync", (lambda J=J, slab=slab: (lambda eng: [
                eng.dma_start(out=wslab[slab][:], in_=w_inproj[J])]))(),
                   writes=[B_wslab[slab]], dma=True)
            pj = J % 2
            def _mm(eng, J=J, slab=slab, pj=pj):
                last = None
                for kd in range(NKD):
                    for nn in range(TC // 512):
                        last = eng.matmul(PS[2 * pj + nn][:],
                                          wslab[slab][:, kd * 128:(kd + 1) * 128],
                                          xnch[:].rearrange("p (k t) -> p k t", k=NKD)[:, kd, nn * 512:(nn + 1) * 512],
                                          start=(kd == 0), stop=(kd == NKD - 1))
                return last
            S.emit("pe", _mm, reads=[B_wslab[slab], B_xnch],
                   writes=[B_PS[2 * pj], B_PS[2 * pj + 1]])
            xr = J % 2
            def _evx(eng, J=J, pj=pj, xr=xr):
                eng.add(xin[xr][:, 3:3 + 512], PS[2 * pj][:], bin_sb[:, J:J + 1])
                return eng.add(xin[xr][:, 3 + 512:3 + 1024], PS[2 * pj + 1][:], bin_sb[:, J:J + 1])
            S.emit("act", _evx, reads=[B_PS[2 * pj], B_PS[2 * pj + 1], B_bin],
                   writes=[B_xin[xr]])
            S.emit("dve", (lambda J=J, xr=xr: (lambda eng: [
                eng.tensor_copy(xin[xr][:, 0:3], xinst[:, J * 4:J * 4 + 3])]))(),
                   reads=[B_xinst], writes=[B_xin[xr]])
            def _cprod(eng, J=J, xr=xr):
                eng.mul(SCRB[0][:], xin[xr][:, 0:TC], cw_sb[:, J * 4:J * 4 + 1])
                return eng.mul(SCRB[1][:], xin[xr][:, 1:1 + TC], cw_sb[:, J * 4 + 1:J * 4 + 2])
            S.emit("act", _cprod, reads=[B_xin[xr], B_cw],
                   writes=[B_SCRB[0], B_SCRB[1]])
            S.emit("dve", lambda eng: [eng.tensor_add(SCRB[0][:], SCRB[0][:], SCRB[1][:])],
                   reads=[B_SCRB[1]], writes=[B_SCRB[0]])
            S.emit("act", (lambda J=J, xr=xr: (lambda eng: [
                eng.mul(SCRB[1][:], xin[xr][:, 2:2 + TC], cw_sb[:, J * 4 + 2:J * 4 + 3])]))(),
                   reads=[B_xin[xr], B_cw], writes=[B_SCRB[1]])
            S.emit("dve", lambda eng: [eng.tensor_add(SCRB[0][:], SCRB[0][:], SCRB[1][:])],
                   reads=[B_SCRB[1]], writes=[B_SCRB[0]])
            S.emit("act", (lambda J=J, xr=xr: (lambda eng: [
                eng.mul(SCRB[1][:], xin[xr][:, 3:3 + TC], cw_sb[:, J * 4 + 3:J * 4 + 4])]))(),
                   reads=[B_xin[xr], B_cw], writes=[B_SCRB[1]])
            S.emit("dve", lambda eng: [eng.tensor_add(SCRB[0][:], SCRB[0][:], SCRB[1][:])],
                   reads=[B_SCRB[1]], writes=[B_SCRB[0]])
            S.emit("act", (lambda J=J: (lambda eng: [
                eng.activation(out=xc[:, J * TC:(J + 1) * TC], in_=SCRB[0][:], func=AF.Silu,
                               bias=cb_sb[:, J:J + 1])]))(),
                   reads=[B_SCRB[0], B_cb], writes=[B_xc])
            S.emit("dve", (lambda J=J, xr=xr: (lambda eng: [
                eng.tensor_copy(xinst[:, J * 4:J * 4 + 3], xin[xr][:, TC:TC + 3])]))(),
                   reads=[B_xin[xr]], writes=[B_xinst])
            drain_pe(1)
            flush_tails()

        def _xproj(eng):
            last = None
            for kk in range(16):
                for nn in range(TC // 512):
                    last = eng.matmul(PS[nn][0:80, :],
                                      wxp1_sb[:, kk * 80:(kk + 1) * 80],
                                      xc[:].rearrange("p (j t) -> p j t", j=NJ)[:, kk, nn * 512:(nn + 1) * 512],
                                      start=(kk == 0), stop=(kk == 15))
            return last
        S.emit("pe", _xproj, reads=[B_wxp1, B_xc], writes=[B_PS[0], B_PS[1]])
        def _evxp(eng):
            eng.copy(dtlow[:, 0:512], PS[0][0:64, :])
            eng.copy(dtlow[:, 512:1024], PS[1][0:64, :])
            eng.copy(Brow[:, 0:512], PS[0][64:80, :])
            return eng.copy(Brow[:, 512:1024], PS[1][64:80, :])
        S.emit("act", _evxp, reads=[B_PS[0], B_PS[1]], writes=[B_dtlow, B_Brow])
        def _xprojC(eng):
            last = None
            for kk in range(16):
                for nn in range(TC // 512):
                    last = eng.matmul(PS[2 + nn][0:16, :],
                                      wxpC_sb[:, kk * 16:(kk + 1) * 16],
                                      xc[:].rearrange("p (j t) -> p j t", j=NJ)[:, kk, nn * 512:(nn + 1) * 512],
                                      start=(kk == 0), stop=(kk == 15))
            return last
        S.emit("pe", _xprojC, reads=[B_wxpC, B_xc], writes=[B_PS[2], B_PS[3]])
        def _evC(eng):
            eng.copy(Crow[:, 0:512], PS[2][0:16, :])
            return eng.copy(Crow[:, 512:1024], PS[3][0:16, :])
        S.emit("act", _evC, reads=[B_PS[2], B_PS[3]], writes=[B_Crow])

        for s in range(DS):
            for (row, Ba, dstall, B_dstall) in ((Brow, B_Brow, Ball, B_Ball),
                                                (Crow, B_Crow, Call, B_Call)):
                def _bc(eng, s=s, row=row):
                    last = None
                    for nn in range(TC // 512):
                        last = eng.matmul(PS[2 + nn][:], BSEL_sb[:, s * 128:(s + 1) * 128],
                                          row[:, nn * 512:(nn + 1) * 512],
                                          start=True, stop=True)
                    return last
                S.emit("pe", _bc, reads=[Ba, B_BSEL], writes=[B_PS[2], B_PS[3]])
                def _bce(eng, s=s, dstall=dstall):
                    eng.copy(dstall[:, s * TC:s * TC + 512], PS[2][:])
                    return eng.copy(dstall[:, s * TC + 512:(s + 1) * TC], PS[3][:])
                S.emit("act", _bce, reads=[B_PS[2], B_PS[3]], writes=[B_dstall])

        def prep_J(J):
            pj = J % 2
            for nn in range(TC // 512):
                S.emit("pe", (lambda J=J, nn=nn: (lambda eng: [
                    eng.matmul(PS[4][:], wdt_sb[:, J * 128:(J + 1) * 128],
                               dtlow[:, nn * 512:(nn + 1) * 512],
                               start=True, stop=True)]))(),
                       reads=[B_wdt, B_dtlow], writes=[B_PS[4]])
                S.emit("act", (lambda J=J, nn=nn: (lambda eng: [
                    eng.activation(out=SCRB[1][:, nn * 512:(nn + 1) * 512], in_=PS[4][:],
                                   func=AF.Exp, bias=bdt_sb[:, J:J + 1])]))(),
                       reads=[B_PS[4], B_bdt], writes=[B_SCRB[1]])
            S.emit("act", (lambda pj=pj: (lambda eng: [
                eng.activation(out=dtj[pj][:], in_=SCRB[1][:], func=AF.Ln, bias=1.0)]))(),
                   reads=[B_SCRB[1]], writes=[B_dtj[pj]])

            slab = WCTR[0] % 2
            WCTR[0] += 1
            S.emit("sync", (lambda J=J, slab=slab: (lambda eng: [
                eng.dma_start(out=wslab[slab][:], in_=w_inproj[16 + J])]))(),
                   writes=[B_wslab[slab]], dma=True)
            def _zmm(eng, J=J, slab=slab, pj=pj):
                last = None
                for kd in range(NKD):
                    for nn in range(TC // 512):
                        last = eng.matmul(PS[2 * pj + nn][:],
                                          wslab[slab][:, kd * 128:(kd + 1) * 128],
                                          xnch[:].rearrange("p (k t) -> p k t", k=NKD)[:, kd, nn * 512:(nn + 1) * 512],
                                          start=(kd == 0), stop=(kd == NKD - 1))
                return last
            S.emit("pe", _zmm, reads=[B_wslab[slab], B_xnch],
                   writes=[B_PS[2 * pj], B_PS[2 * pj + 1]])
            def _zev(eng, J=J, pj=pj):
                eng.activation(out=silz[pj][:, 0:512], in_=PS[2 * pj][:], func=AF.Silu,
                               bias=bin_sb[:, 16 + J:17 + J])
                return eng.activation(out=silz[pj][:, 512:1024], in_=PS[2 * pj + 1][:],
                                      func=AF.Silu, bias=bin_sb[:, 16 + J:17 + J])
            S.emit("act", _zev, reads=[B_PS[2 * pj], B_PS[2 * pj + 1], B_bin],
                   writes=[B_silz[pj]])

        def s_body(J, s):
            pj = J % 2
            sr = s % 2
            s4 = s % 4
            col = J * DS + s
            S.emit("dve", (lambda J=J, s=s: (lambda eng: [
                eng.tensor_mul(wv[0][:], xc[:, J * TC:(J + 1) * TC],
                               Ball[:, s * TC:(s + 1) * TC])]))(),
                   reads=[B_xc, B_Ball], writes=[B_wv[0]])
            s2 = s % 2
            S.emit("dve", (lambda s2=s2: (lambda eng: [
                eng.scalar_tensor_tensor(out=wv[0][:], in0=av[s2][:], scalar=-1.0,
                                         in1=wv[0][:], op0=ALU.add, op1=ALU.mult)]))(),
                   reads=[B_av[s2]], writes=[B_wv[0]])
            S.emit("dve", (lambda s2=s2, col=col: (lambda eng: [
                eng.tensor_tensor_scan(out=hv[0][:], data0=av[s2][:], data1=wv[0][:],
                                       initial=gstate[:, col:col + 1],
                                       op0=ALU.mult, op1=ALU.add)]))(),
                   reads=[B_av[s2], B_gscol[col]], writes=[B_hv[0]],
                   selfwait=True)
            S.emit("dve", (lambda col=col: (lambda eng: [
                eng.tensor_copy(gstate[:, col:col + 1], hv[0][:, TC - 1:TC])]))(),
                   reads=[B_hv[0]], writes=[B_gscol[col]])
            S.emit("dve", (lambda s=s, s4=s4: (lambda eng: [
                eng.tensor_mul(pv[s4][:], hv[0][:], Call[:, s * TC:(s + 1) * TC])]))(),
                   reads=[B_hv[0], B_Call], writes=[B_pv[s4]])
            def _yred(eng, s4=s4, s=s, pj=pj):
                last = None
                for nn in range(TC // 512):
                    last = eng.matmul(PS[2 * pj + nn][:], dg[s4][:],
                                      pv[s4][:, nn * 512:(nn + 1) * 512],
                                      start=(s == 0), stop=(s == DS - 1))
                return last
            S.emit("pe", _yred, reads=[B_dg[s4], B_pv[s4]],
                   writes=[B_PS[2 * pj], B_PS[2 * pj + 1]])

        prep_J(0)
        for J in range(NJ):
            pj = J % 2
            for g in range(4):
                for s in range(4 * g, 4 * g + 4):
                    s2 = s % 2
                    s4 = s % 4
                    col = J * DS + s
                    S.emit("act", (lambda pj=pj, s2=s2, col=col: (lambda eng: [
                        eng.activation(out=av[s2][:], in_=dtj[pj][:], func=AF.Exp,
                                       scale=A_sb[:, col:col + 1])]))(),
                           reads=[B_dtj[pj], B_A], writes=[B_av[s2]])
                    S.emit("act", (lambda s4=s4, col=col: (lambda eng: [
                        eng.mul(dg[s4][:], identb[:], invA_sb[:, col:col + 1])]))(),
                           reads=[B_identb, B_invA], writes=[B_dg[s4]])
                drain_pe(2 if g == 3 else 3)
                for s in range(4 * g, 4 * g + 4):
                    s_body(J, s)
                flush_tails()
                if g == 0 and J + 1 < NJ:
                    prep_J(J + 1)

            def _yev(eng, J=J, pj=pj):
                eng.scalar_tensor_tensor(out=SCRB[0][:, 0:512], in0=xc[:, J * TC:J * TC + 512],
                                         scalar=Dc_sb[:, J:J + 1], in1=PS[2 * pj][:],
                                         op0=ALU.mult, op1=ALU.add)
                eng.scalar_tensor_tensor(out=SCRB[0][:, 512:1024],
                                         in0=xc[:, J * TC + 512:(J + 1) * TC],
                                         scalar=Dc_sb[:, J:J + 1], in1=PS[2 * pj + 1][:],
                                         op0=ALU.mult, op1=ALU.add)
                return eng.tensor_mul(SCRB[0][:], SCRB[0][:], silz[pj][:])
            S.emit("dve", _yev, reads=[B_PS[2 * pj], B_PS[2 * pj + 1], B_xc, B_Dc,
                                       B_silz[pj]], writes=[B_SCRB[0]])
            S.emit("sync", (lambda ch=ch, J=J: (lambda eng: [
                eng.dma_start(out=yg_mir[ch, J], in_=SCRB[0][:])]))(),
                   reads=[B_SCRB[0]], writes=[B_yg_mir], dma=True)

    drain_pe(10 ** 9)
    flush_tails()
    p2.close()
    S.barrier()

    p2b = ExitStack()
    yacF, B_yacF = sb("yacF", (128, (TC // 128) * DM), BF16, p2b)
    wop, B_wop = sb("wop", (128, 8 * NJ * 128), BF16, p2b)
    hv2 = []
    B_hv2 = []
    for i in range(3):
        t, b = sb("hv2_%d" % i, (128, 512), BF16, p2b)
        hv2.append(t)
        B_hv2.append(b)
    mo2b, B_mo2b = sb("mo2b", (128, 512), BF16, p2b)
    NTB = TC // 128
    S.emit("sync", lambda eng: [eng.dma_start(out=wop[:, m * 2048:(m + 1) * 2048],
                                              in_=w_outproj[m])
                                for m in range(8)],
           writes=[B_wop], dma=True, ndma=8)
    HCT = [0]
    yac4 = yacF[:].rearrange("p (tb d) -> p tb d", tb=NTB)
    for ch in range(NCH):
        def _ldx(eng, ch=ch):
            return [eng.dma_start(out=yacF[:, tb * DM:(tb + 1) * DM],
                                  in_=x_hbm[ch * TC + tb * 128: ch * TC + (tb + 1) * 128, :])
                    for tb in range(NTB)]
        S.emit("pool", _ldx, writes=[B_yacF], dma=True, ndma=NTB)
        for hf in range(2):
            for mh in range(2):
                for J in range(NJ):
                    r = HCT[0] % 3
                    HCT[0] += 1
                    S.emit("sync", (lambda ch=ch, J=J, hf=hf, r=r: (lambda eng: [
                        eng.dma_start(out=hv2[r][:],
                                      in_=yg_mir[ch, J, :, hf * 512:(hf + 1) * 512])]))(),
                           reads=[B_yg_mir], writes=[B_hv2[r]], dma=True)
                    def _opmm(eng, J=J, r=r, mh=mh):
                        last = None
                        for m in range(4):
                            gm = mh * 4 + m
                            last = eng.matmul(PS[m][:], wop[:, gm * 2048 + J * 128:
                                                            gm * 2048 + (J + 1) * 128],
                                              hv2[r][:], start=(J == 0), stop=(J == NJ - 1))
                        return last
                    S.emit("pe", _opmm, reads=[B_wop, B_hv2[r]],
                           writes=[B_PS[0], B_PS[1], B_PS[2], B_PS[3]])
                for m in range(4):
                    gm = mh * 4 + m
                    S.emit("act", (lambda m=m, gm=gm: (lambda eng: [
                        eng.add(mo2b[:], PS[m][:], bout_sb[:, gm:gm + 1])]))(),
                           reads=[B_PS[m], B_bout], writes=[B_mo2b])
                    def _tr4(eng):
                        last = None
                        for tb4 in range(4):
                            last = eng.transpose(PSB[:, tb4 * 128:(tb4 + 1) * 128],
                                                 mo2b[:, tb4 * 128:(tb4 + 1) * 128],
                                                 identb[:])
                        return last
                    S.emit("pe", _tr4, reads=[B_mo2b, B_identb], writes=[B_PSB])
                    def _acc(eng, hf=hf, gm=gm):
                        dst = yac4[:, 4 * hf:4 * hf + 4, gm * 128:(gm + 1) * 128]
                        src_ = PSB[:].rearrange("p (tb c) -> p tb c", tb=4)
                        return eng.tensor_add(dst, dst, src_)
                    S.emit("dve", _acc, reads=[B_PSB], writes=[B_yacF])
        def _stout(eng, ch=ch):
            return [eng.dma_start(out=out_hbm[ch * TC + tb * 128: ch * TC + (tb + 1) * 128, :],
                                  in_=yacF[:, tb * DM:(tb + 1) * DM])
                    for tb in range(NTB)]
        S.emit("pool", _stout, reads=[B_yacF], writes=[B_out_hbm], dma=True, ndma=NTB)
    p2b.close()
    S.barrier()

    p3 = ExitStack()
    HW = CAP // 2
    gb2h = []
    B_gb2h = []
    for i in range(2):
        t, b = sb("gb2h%d" % i, (128, NF * HW), BF16, p3)
        gb2h.append(t)
        B_gb2h.append(b)
    w2s = []
    B_w2s = []
    for i in range(2):
        t, b = sb("w2s_%d" % i, (128, NF * DM), BF16, p3)
        w2s.append(t)
        B_w2s.append(b)
    eo = []
    B_eo = []
    for i in range(2):
        t, b = sb("eo%d" % i, (128, DM), BF16, p3)
        eo.append(t)
        B_eo.append(b)

    for e in range(NE):
        we = e % 2
        def _ldw2(eng, e=e, we=we):
            return [eng.dma_start(out=w2s[we][:, kf * DM:(kf + 1) * DM], in_=w2_dev[e, kf])
                    for kf in range(NF)]
        S.emit("sync", _ldw2, writes=[B_w2s[we]], dma=True, ndma=NF)
        for h in range(2):
            def _ldg(eng, e=e, h=h):
                return [eng.dma_start(out=gb2h[h][:, kf * HW:(kf + 1) * HW],
                                      in_=g_dram[e, kf, :, h * HW:(h + 1) * HW])
                        for kf in range(NF)]
            S.emit("sync", _ldg, reads=[B_g_dram], writes=[B_gb2h[h]], dma=True, ndma=NF)
        for q in range(NCAP):
            h = q // 5
            qh = q - 5 * h
            def _w2mm(eng, q=q, we=we, h=h, qh=qh):
                last = None
                for kf in range(NF):
                    gs = gb2h[h][:, kf * HW + qh * 128: kf * HW + (qh + 1) * 128]
                    for nn in range(2):
                        last = eng.matmul(PS[2 * (q % 2) + nn][:], gs,
                                          w2s[we][:, kf * DM + nn * 512: kf * DM + (nn + 1) * 512],
                                          start=(kf == 0), stop=(kf == NF - 1))
                return last
            S.emit("pe", _w2mm, reads=[B_gb2h[h], B_w2s[we]],
                   writes=[B_PS[2 * (q % 2)], B_PS[2 * (q % 2) + 1]])
            def _eoev(eng, q=q):
                eng.copy(eo[q % 2][:, 0:512], PS[2 * (q % 2)][:])
                return eng.copy(eo[q % 2][:, 512:1024], PS[2 * (q % 2) + 1][:])
            S.emit("act", _eoev, reads=[B_PS[2 * (q % 2)], B_PS[2 * (q % 2) + 1]],
                   writes=[B_eo[q % 2]])
            S.emit("sync", (lambda e=e, q=q: (lambda eng: [
                eng.dma_start(out=eout_dram[e * CAP + q * 128: e * CAP + (q + 1) * 128, :],
                              in_=eo[q % 2][:])]))(),
                   reads=[B_eo[q % 2]], writes=[B_eout_dram], dma=True)

    p3.close()
    S.barrier()

    p4 = ExitStack()
    cmbA, B_cmbA = sb("cmbA", (128, NCT * DM), BF16, p4)
    cmbB, B_cmbB = sb("cmbB", (128, NCT * DM), BF16, p4)
    acc = []
    B_acc = []
    for i in range(4):
        t, b = sb("acc%d" % i, (128, DM), F32, p4)
        acc.append(t)
        B_acc.append(b)
    S.emit("dve", lambda eng: [eng.memset(cmbA[:], 0.0)], writes=[B_cmbA])
    S.emit("pool", lambda eng: [eng.memset(cmbB[:], 0.0)], writes=[B_cmbB])

    for c in range(NCT):
        for (dsl_i, B_dsl_i, cb_t, B_cb_t) in ((dsl1_sb, B_dsl1, cmbA, B_cmbA),
                                               (dsl2_sb, B_dsl2, cmbB, B_cmbB)):
            S.emit("pool", (lambda c=c, dsl_i=dsl_i, cb_t=cb_t: (lambda eng: [
                eng.indirect_dma_start(
                    out=cb_t[:, c * DM:(c + 1) * DM], out_offset=None,
                    in_=eout_dram, in_offset=IndirectOffsetOnAxis(ap=dsl_i[:, c:c + 1], axis=0),
                    bounds_check=BCREG["r"], oob_is_err=False)
            ]))(), reads=[B_eout_dram, B_dsl_i], writes=[B_cb_t], dma=True)

    for c in range(NCT):
        ia = c % 4
        orows = out_hbm.rearrange("(p c) d -> c p d", c=NCT)[c]
        S.emit("sync", (lambda orows=orows, ia=ia: (lambda eng: [
            eng.dma_start(out=acc[ia][:], in_=orows)]))(),
               reads=[B_out_hbm], writes=[B_acc[ia]], dma=True)
        for (cb_t, B_cb_t, pmv, B_pmv) in ((cmbA, B_cmbA, pm1_sb, B_pm1),
                                           (cmbB, B_cmbB, pm2_sb, B_pm2)):
            S.emit("dve", (lambda c=c, ia=ia, pmv=pmv, cb_t=cb_t: (lambda eng: [
                eng.scalar_tensor_tensor(out=acc[ia][:], in0=cb_t[:, c * DM:(c + 1) * DM],
                                         scalar=pmv[:, c:c + 1], in1=acc[ia][:],
                                         op0=ALU.mult, op1=ALU.add)]))(),
                   reads=[B_cb_t, B_pmv], writes=[B_acc[ia]])
        S.emit("sync", (lambda orows=orows, ia=ia: (lambda eng: [
            eng.dma_start(out=orows, in_=acc[ia][:])]))(),
               reads=[B_acc[ia]], writes=[B_out_hbm], dma=True)

    S.barrier()
    p4.close()

    with nc.Block() as blk:
        def run_stream(eng, name):
            for (waits, body, sem, inc, selfwait, val) in S.streams[name]:
                for (sname, v) in waits:
                    eng.wait_ge(sems[sname], v)
                if body is None:
                    continue
                ins = body(eng)
                if not isinstance(ins, (list, tuple)):
                    ins = [ins]
                if inc >= 16:
                    assert inc == 16 * len(ins), (name, inc, len(ins))
                    for li in ins:
                        li.then_inc(sems[sem], 16)
                else:
                    ins[-1].then_inc(sems[sem], inc)
                if selfwait:
                    eng.wait_ge(sems[sem], val)

        @blk.tensor
        def _(eng):
            run_stream(eng, "pe")

        @blk.vector
        def _(eng):
            run_stream(eng, "dve")

        @blk.scalar
        def _(eng):
            run_stream(eng, "act")

        @blk.gpsimd
        def _(eng):
            run_stream(eng, "pool")

        @blk.sync
        def _(eng):
            run_stream(eng, "sync")

    es.close()
    return nc


def _host_inputs(inp):
    w = {}
    ipw = np.asarray(inp["L0_attn_in_proj_w"], np.float32)
    w["w_inproj"] = trunc_bf16(
        ipw.T.reshape(8, 128, 32, 128).transpose(2, 1, 0, 3).reshape(32, 128, 1024))
    w["b_inproj"] = np.ascontiguousarray(
        np.asarray(inp["L0_attn_in_proj_b"], np.float32).reshape(32, 128).T)
    xp = np.asarray(inp["L0_attn_x_proj_w"], np.float32)
    w["w_xproj1"] = trunc_bf16(
        xp[:80].T.reshape(16, 128, 80).transpose(1, 0, 2).reshape(128, 16 * 80))
    w["w_xprojC"] = trunc_bf16(
        xp[80:96].T.reshape(16, 128, 16).transpose(1, 0, 2).reshape(128, 16 * 16))
    w["w_dtproj"] = trunc_bf16(np.asarray(inp["L0_attn_dt_proj_w"], np.float32).T)
    w["b_dtproj"] = np.ascontiguousarray(
        np.asarray(inp["L0_attn_dt_proj_b"], np.float32).reshape(16, 128).T)
    cw = np.asarray(inp["L0_attn_conv_w"], np.float32)
    w["conv_w4"] = np.ascontiguousarray(
        cw.reshape(16, 128, 4).transpose(1, 0, 2).reshape(128, 64))
    w["conv_bc"] = np.ascontiguousarray(
        np.asarray(inp["L0_attn_conv_b"], np.float32).reshape(16, 128).T)
    al = np.asarray(inp["L0_attn_A_log"], np.float32)
    w["A_logr"] = np.ascontiguousarray(
        al.T.reshape(16, 128, 16).transpose(1, 0, 2).reshape(128, 256))
    w["D_col"] = np.ascontiguousarray(
        np.asarray(inp["L0_attn_D"], np.float32).reshape(16, 128).T)
    opw = np.asarray(inp["L0_attn_out_proj_w"], np.float32)
    w["w_outproj"] = trunc_bf16(
        opw.T.reshape(16, 128, 8, 128).transpose(2, 1, 0, 3).reshape(8, 128, 16 * 128))
    w["b_outproj"] = np.ascontiguousarray(
        np.asarray(inp["L0_attn_out_proj_b"], np.float32).reshape(8, 128).T)
    gwt = np.asarray(inp["L0_ffn_gate_w"], np.float32).T
    w["gate_wT"] = np.ascontiguousarray(
        gwt.reshape(8, 128, 8).transpose(1, 0, 2).reshape(128, 64))
    w1 = np.asarray(inp["L0_ffn_w1"], np.float32)
    w["w1_dev"] = trunc_bf16(
        w1.reshape(8, 8, 128, 28, 128).transpose(0, 3, 2, 1, 4).reshape(8, 28, 128, 1024))
    w3 = np.asarray(inp["L0_ffn_w3"], np.float32)
    w["w3_dev"] = trunc_bf16(
        w3.reshape(8, 8, 128, 28, 128).transpose(0, 3, 2, 1, 4).reshape(8, 28, 128, 1024))
    w2 = np.asarray(inp["L0_ffn_w2"], np.float32)
    w["w2_dev"] = trunc_bf16(w2.reshape(8, 28, 128, 1024))
    w["ident_f"] = np.eye(128, dtype=np.float32)
    w["ident_b"] = trunc_bf16(np.eye(128, dtype=np.float32))
    lt = np.tril(np.ones((128, 128), np.float32), -1).T
    w["LT"] = np.ascontiguousarray(lt)
    bsel = np.zeros((16, 16 * 128), np.float32)
    for s in range(16):
        bsel[s, s * 128:(s + 1) * 128] = 1.0
    w["BSEL"] = trunc_bf16(bsel)
    return w


def kernel(**inputs):
    x = np.asarray(inputs["L0_attn_x"], np.float32)
    shared = _host_inputs(inputs)
    nc = build_program()
    in_maps = []
    for c in range(NCORES):
        m = dict(shared)
        m["x"] = np.ascontiguousarray(x[c])
        in_maps.append(m)
    res = run_bass_kernel_spmd(nc, in_maps, core_ids=list(range(NCORES)))
    out = np.stack([res.results[c]["out"] for c in range(NCORES)], axis=0)
    return out.astype(np.float32)


# version 12
# speedup vs baseline: 1.2846x; 1.0463x over previous
import numpy as np
import ml_dtypes
import concourse.bass as bass
import concourse.mybir as mybir
from concourse.bass_utils import run_bass_kernel_spmd
from concourse.bass_core_rust import IndirectOffsetOnAxis
from contextlib import ExitStack

F32 = mybir.dt.float32
BF16 = mybir.dt.bfloat16
I32 = mybir.dt.int32
ALU = mybir.AluOpType
AF = mybir.ActivationFunctionType
AX = mybir.AxisListType.X
BF = ml_dtypes.bfloat16

T, DM, DI, DS, DTR = 4096, 1024, 2048, 16, 64
NE, FF, CAP = 8, 3584, 1280
NJ = DI // 128
NKD = DM // 128
NF = FF // 128
TC = 1024
NCH = T // TC
NCT = 32
NCAP = CAP // 128
EPS = 1e-5

NCORES = 8
DEBUG_TAPS = False


def trunc_bf16(a):
    a = np.ascontiguousarray(a, dtype=np.float32)
    u = a.view(np.uint16)
    hi = np.ascontiguousarray(u.reshape(-1, 2)[:, 1])
    return hi.reshape(a.shape).view(BF)


ENGINES = ("pe", "dve", "act", "pool", "sync")
SEM_OF = {"pe": "sP", "dve": "sV", "act": "sA", "pool": "sGc", "sync": "sS"}
DMA_POOL = {"sync": ["sSd%d" % i for i in range(12)],
            "pool": ["sGd%d" % i for i in range(8)],
            "act": ["sAd0"]}
ALL_SEMS = sorted(set(SEM_OF.values()) | {s for v in DMA_POOL.values() for s in v})


class Buf:
    __slots__ = ("name", "w", "r")

    def __init__(self, name):
        self.name = name
        self.w = None
        self.r = []


class Sched:
    def __init__(self):
        self.streams = {e: [] for e in ENGINES}
        self.cnt = {s: 0 for s in ALL_SEMS}
        self.waited = {e: {} for e in ENGINES}
        self.rr = {q: 0 for q in DMA_POOL}

    def _wait(self, eng, waits, sem, val):
        seen = self.waited[eng]
        if seen.get(sem, 0) >= val:
            return
        seen[sem] = val
        waits.append((sem, val))

    def emit(self, eng, body, reads=(), writes=(), dma=False, selfwait=False, ndma=1):
        waits = []
        for b in reads:
            if b.w is not None and b.w[0] != eng:
                self._wait(eng, waits, b.w[1], b.w[2])
        for b in writes:
            if b.w is not None and b.w[0] != eng:
                self._wait(eng, waits, b.w[1], b.w[2])
            for (reng, rsem, rval) in b.r:
                if reng != eng:
                    self._wait(eng, waits, rsem, rval)
        if dma:
            pool = DMA_POOL[eng]
            sem = pool[self.rr[eng] % len(pool)]
            self.rr[eng] += 1
            if self.cnt[sem] > 0:
                self._wait(eng, waits, sem, self.cnt[sem])
            inc = 16 * ndma
        else:
            sem = SEM_OF[eng]
            inc = 1
        self.cnt[sem] += inc
        val = self.cnt[sem]
        self.streams[eng].append((waits, body, sem, inc, selfwait, val))
        ev = (eng, sem, val)
        for b in reads:
            b.r.append(ev)
        for b in writes:
            b.w = ev
            b.r = []
        return ev

    def barrier(self):
        snapshot = dict(self.cnt)
        for e in ENGINES:
            waits = []
            for sem, val in snapshot.items():
                if val > 0:
                    self._wait(e, waits, sem, val)
            if waits:
                self.streams[e].append((waits, None, None, 0, False, 0))


def build_program():
    nc = bass.Bass("TRN2", target_bir_lowering=False, debug=False,
                   detect_race_conditions=False)
    S = Sched()
    es = ExitStack()

    def din(name, shape, dt):
        return nc.dram_tensor(name, list(shape), dt, kind="ExternalInput").ap()

    def dscratch(name, shape, dt):
        kind = "ExternalOutput" if DEBUG_TAPS else "Internal"
        return nc.dram_tensor(name, list(shape), dt, kind=kind).ap()

    x_hbm = din("x", (T, DM), F32)
    w_inproj = din("w_inproj", (32, 128, 8 * 128), BF16)
    b_inproj = din("b_inproj", (128, 32), F32)
    w_xproj1 = din("w_xproj1", (128, 16 * 80), BF16)
    w_xprojC = din("w_xprojC", (128, 16 * 16), BF16)
    w_dtproj = din("w_dtproj", (64, DI), BF16)
    b_dtproj = din("b_dtproj", (128, NJ), F32)
    conv_w4 = din("conv_w4", (128, NJ * 4), F32)
    conv_bc = din("conv_bc", (128, NJ), F32)
    A_logr = din("A_logr", (128, NJ * DS), F32)
    D_col = din("D_col", (128, NJ), F32)
    w_outproj = din("w_outproj", (8, 128, 16 * 128), BF16)
    b_outproj = din("b_outproj", (128, 8), F32)
    gate_wT = din("gate_wT", (128, NKD * NE), F32)
    w1_dev = din("w1_dev", (NE, NF, 128, 8 * 128), BF16)
    w3_dev = din("w3_dev", (NE, NF, 128, 8 * 128), BF16)
    w2_dev = din("w2_dev", (NE, NF, 128, DM), BF16)
    ident_f_d = din("ident_f", (128, 128), F32)
    ident_b_d = din("ident_b", (128, 128), BF16)
    LT_d = din("LT", (128, 128), F32)
    BSEL_d = din("BSEL", (16, DS * 128), BF16)

    xn_tok_mir = dscratch("xn_tok_mir", (T, DM), BF16)
    xn_dmaj_mir = dscratch("xn_dmaj_mir", (DM, T), BF16)
    yg_mir = dscratch("yg_mir", (NCH, NJ, 128, TC), BF16)
    disp_dram = dscratch("disp_dram", (NE * CAP, DM), BF16)
    g_dram = dscratch("g_dram", (NE, NF, 128, CAP), BF16)
    eout_dram = dscratch("eout_dram", (NE * CAP, DM), BF16)
    out_hbm = nc.dram_tensor("out", [T, DM], F32, kind="ExternalOutput").ap()

    B_xn_tok_mir = Buf("xn_tok_mir")
    B_xn_dmaj_mir = Buf("xn_dmaj_mir")
    B_yg_mir = Buf("yg_mir")
    B_disp_dram = Buf("disp_dram")
    B_g_dram = Buf("g_dram")
    B_eout_dram = Buf("eout_dram")
    B_out_hbm = Buf("out_hbm")

    sems = {}
    for sname in ALL_SEMS:
        sems[sname] = es.enter_context(nc.semaphore(sname))

    def sb(name, shape, dt, stack=None):
        t = (stack or es).enter_context(nc.sbuf_tensor(name, list(shape), dt))
        return t, Buf(name)

    def psum(name, shape, dt, stack=None):
        t = (stack or es).enter_context(nc.psum_tensor(name, list(shape), dt))
        return t, Buf(name)

    identf, B_identf = sb("identf", (128, 128), F32)
    identb, B_identb = sb("identb", (128, 128), BF16)

    BSEL_sb, B_BSEL = sb("BSEL_sb", (16, DS * 128), BF16)
    A_sb, B_A = sb("A_sb", (128, NJ * DS), F32)
    invA_sb, B_invA = sb("invA_sb", (128, NJ * DS), F32)
    bin_sb, B_bin = sb("bin_sb", (128, 32), F32)
    bdt_sb, B_bdt = sb("bdt_sb", (128, NJ), F32)
    cw_sb, B_cw = sb("cw_sb", (128, NJ * 4), F32)
    cb_sb, B_cb = sb("cb_sb", (128, NJ), F32)
    Dc_sb, B_Dc = sb("Dc_sb", (128, NJ), F32)
    bout_sb, B_bout = sb("bout_sb", (128, 8), F32)
    wxp1_sb, B_wxp1 = sb("wxp1_sb", (128, 16 * 80), BF16)
    wxpC_sb, B_wxpC = sb("wxpC_sb", (128, 16 * 16), BF16)
    wdt_sb, B_wdt = sb("wdt_sb", (64, DI), BF16)
    dsl1_sb, B_dsl1 = sb("dsl1", (128, 32), I32)
    dsl2_sb, B_dsl2 = sb("dsl2", (128, 32), I32)
    pm1_sb, B_pm1 = sb("pm1", (128, 32), F32)
    pm2_sb, B_pm2 = sb("pm2", (128, 32), F32)
    gstate, B_gstate = sb("gstate", (128, NJ * DS), BF16)
    xinst, B_xinst = sb("xinst", (128, NJ * 4), BF16)

    cst_sb, B_cst = sb("cst_sb", (128, 8), F32)
    CONSTS = {EPS: 0, 1.0: 1, -0.5: 2, -1.0: 3, 0.0: 4}
    for v, i in CONSTS.items():
        nc.const_aps.register(F32, v, cst_sb[:, i:i + 1])

    def _fillconsts(eng):
        last = None
        for v, i in CONSTS.items():
            last = eng.memset(cst_sb[:, i:i + 1], float(v))
        return last
    S.emit("dve", _fillconsts, writes=[B_cst])

    BCREG = {}

    def _mkbc(eng):
        r = eng.alloc_register("bcreg")
        BCREG["r"] = r
        return eng.reg_mov(r, NE * CAP - 1)
    S.emit("pool", _mkbc)
    S.barrier()

    PS = []
    B_PS = []
    for i in range(7):
        t, b = psum("ps%d" % i, (128, 512), F32)
        PS.append(t)
        B_PS.append(b)
    PSB, B_PSB = psum("psb", (128, 512), BF16)

    p0 = ExitStack()
    LT_sb, B_LT = sb("LT_sb", (128, 128), F32, p0)
    gw_sb, B_gw = sb("gw_sb", (128, NKD * NE), F32, p0)
    scr0, B_scr0 = sb("scr0", (128, NJ * DS), F32, p0)

    def ldconst(dst, src, bufs):
        S.emit("sync", (lambda d=dst, s=src: (lambda eng: [eng.dma_start(out=d, in_=s)]))(),
               writes=bufs, dma=True)

    ldconst(identf[:], ident_f_d, [B_identf])
    ldconst(identb[:], ident_b_d, [B_identb])
    ldconst(LT_sb[:], LT_d, [B_LT])
    ldconst(BSEL_sb[:], BSEL_d, [B_BSEL])
    ldconst(A_sb[:], A_logr, [B_A])
    ldconst(bin_sb[:], b_inproj, [B_bin])
    ldconst(bdt_sb[:], b_dtproj, [B_bdt])
    ldconst(cw_sb[:], conv_w4, [B_cw])
    ldconst(cb_sb[:], conv_bc, [B_cb])
    ldconst(Dc_sb[:], D_col, [B_Dc])
    ldconst(bout_sb[:], b_outproj, [B_bout])
    ldconst(gw_sb[:], gate_wT, [B_gw])
    ldconst(wxp1_sb[:], w_xproj1, [B_wxp1])
    ldconst(wxpC_sb[:], w_xprojC, [B_wxpC])
    ldconst(wdt_sb[:], w_dtproj, [B_wdt])


    def _actwarm(eng):
        last = None
        for fn in (AF.Square, AF.Ln, AF.Exp, AF.Silu, AF.Sigmoid, AF.Ln, AF.Exp):
            last = eng.activation(out=cst_sb[:, 5:6], in_=cst_sb[:, 1:2], func=fn)
        last = eng.sign(cst_sb[:, 5:6], cst_sb[:, 1:2])
        return last
    S.emit("act", _actwarm, reads=[B_cst], writes=[B_cst])

    S.emit("act", lambda eng: [eng.activation(out=scr0[:], in_=A_sb[:], func=AF.Exp)],
           reads=[B_A], writes=[B_scr0], selfwait=True)
    S.emit("dve", lambda eng: [eng.tensor_scalar(out=A_sb[:], in0=scr0[:], scalar1=-1.0,
                                                 scalar2=None, op0=ALU.mult)],
           reads=[B_scr0], writes=[B_A], selfwait=True)
    S.emit("dve", lambda eng: [eng.tensor_single_scalar(invA_sb[:], scr0[:], 1e-6, ALU.is_lt)],
           reads=[B_scr0], writes=[B_invA], selfwait=True)
    S.emit("dve", lambda eng: [eng.tensor_scalar(out=scr0[:], in0=A_sb[:], scalar1=-1.0,
                                                 scalar2=1.0, op0=ALU.mult, op1=ALU.add)],
           reads=[B_A], writes=[B_scr0], selfwait=True)
    S.emit("dve", lambda eng: [eng.tensor_mul(scr0[:], scr0[:], invA_sb[:])],
           reads=[B_invA], writes=[B_scr0], selfwait=True)
    S.emit("dve", lambda eng: [eng.tensor_add(scr0[:], scr0[:], A_sb[:])],
           reads=[B_A], writes=[B_scr0], selfwait=True)
    S.emit("dve", lambda eng: [eng.reciprocal(invA_sb[:], scr0[:])],
           reads=[B_scr0], writes=[B_invA], selfwait=True)
    S.barrier()

    p1 = ExitStack()
    xnd, B_xnd = sb("xnd", (128, NKD * T), BF16, p1)
    xt, B_xt = sb("xt", (128, DM), F32, p1)
    sq, B_sq = sb("sq", (128, DM), F32, p1)
    mcol, B_mcol = sb("mcol", (128, 8), F32, p1)
    xnf, B_xnf = sb("xnf", (128, DM), F32, p1)
    xnb, B_xnb = sb("xnb", (128, DM), BF16, p1)
    xnTf, B_xnTf = sb("xnTf", (128, NKD * 128), F32, p1)
    LG, B_LG = sb("LG", (128, NCT * NE), F32, p1)
    if DEBUG_TAPS:
        mcol_tap = dscratch("mcol_tap", (128, 8), F32)
        xt_tap = dscratch("xt_tap", (128, DM), F32)
        B_mcol_tap, B_xt_tap = Buf("mcol_tap"), Buf("xt_tap")
        dsl_tap = dscratch("dsl_tap", (128, 64), I32)
        pm_tap = dscratch("pm_tap", (128, 64), F32)
        B_dsl_tap, B_pm_tap = Buf("dsl_tap"), Buf("pm_tap")

    for c in range(NCT):
        xrows = x_hbm.rearrange("(p c) d -> c p d", c=NCT)[c]
        S.emit("sync", (lambda xr=xrows: (lambda eng: [eng.dma_start(out=xt[:], in_=xr)]))(),
               reads=[], writes=[B_xt], dma=True, selfwait=True)
        S.emit("act", lambda eng: [eng.activation(out=sq[:], in_=xt[:], func=AF.Square)],
               reads=[B_xt], writes=[B_sq], selfwait=True)
        S.emit("dve", lambda eng: [eng.reduce_sum(mcol[:, 0:1], sq[:], axis=AX)],
               reads=[B_sq], writes=[B_mcol], selfwait=True)
        S.emit("dve", lambda eng: [eng.reduce_sum(mcol[:, 1:2], xt[:], axis=AX)],
               reads=[B_xt], writes=[B_mcol], selfwait=True)
        def _stats1(eng):
            eng.tensor_scalar(out=mcol[:, 1:2], in0=mcol[:, 1:2], scalar1=1.0 / DM,
                              scalar2=None, op0=ALU.mult)
            return eng.tensor_scalar(out=mcol[:, 2:3], in0=mcol[:, 0:1], scalar1=1.0 / DM,
                                     scalar2=None, op0=ALU.mult)
        S.emit("dve", lambda eng: [_stats1(eng)], reads=[], writes=[B_mcol], selfwait=True)
        S.emit("dve", lambda eng: [eng.tensor_mul(mcol[:, 3:4], mcol[:, 1:2], mcol[:, 1:2])],
               reads=[], writes=[B_mcol], selfwait=True)
        S.emit("dve", lambda eng: [eng.tensor_sub(mcol[:, 2:3], mcol[:, 2:3], mcol[:, 3:4])],
               reads=[], writes=[B_mcol], selfwait=True)
        S.emit("act", lambda eng: [eng.activation(out=mcol[:, 4:5], in_=mcol[:, 2:3],
                                                  func=AF.Ln, bias=EPS)],
               reads=[B_mcol], writes=[B_mcol], selfwait=True)
        S.emit("act", lambda eng: [eng.activation(out=mcol[:, 5:6], in_=mcol[:, 4:5],
                                                  func=AF.Exp, scale=-0.5)],
               reads=[B_mcol], writes=[B_mcol], selfwait=True)
        if DEBUG_TAPS and c == 0:
            S.emit("sync", lambda eng: [eng.dma_start(out=mcol_tap, in_=mcol[:])],
                   reads=[B_mcol], writes=[B_mcol_tap], dma=True)
            S.emit("sync", lambda eng: [eng.dma_start(out=xt_tap, in_=xt[:])],
                   reads=[B_xt], writes=[B_xt_tap], dma=True)
        S.emit("dve", lambda eng: [eng.tensor_scalar(out=xnf[:], in0=xt[:],
                                                     scalar1=mcol[:, 1:2], scalar2=mcol[:, 5:6],
                                                     op0=ALU.subtract, op1=ALU.mult)],
               reads=[B_xt, B_mcol], writes=[B_xnf], selfwait=True)
        S.emit("dve", lambda eng: [eng.tensor_copy(xnb[:], xnf[:])],
               reads=[B_xnf], writes=[B_xnb], selfwait=True)
        mrows = xn_tok_mir.rearrange("(p c) d -> c p d", c=NCT)[c]
        S.emit("sync", (lambda mr=mrows: (lambda eng: [eng.dma_start(out=mr, in_=xnb[:])]))(),
               reads=[B_xnb], writes=[B_xn_tok_mir], dma=True, selfwait=True)
        for kd in range(NKD):
            pk = kd % 2
            S.emit("pe", (lambda kd=kd, pk=pk: (lambda eng: [
                eng.transpose(PS[pk][:, 0:128], xnf[:, kd * 128:(kd + 1) * 128], identf[:])
            ]))(), reads=[B_xnf, B_identf], writes=[B_PS[pk]], selfwait=True)
            def _evac(eng, kd=kd, pk=pk, c=c):
                eng.tensor_copy(xnTf[:, kd * 128:(kd + 1) * 128], PS[pk][:, 0:128])
                dst = xnd[:].rearrange("p (k t) -> p k t", k=NKD)[:, kd]
                dst = bass.AP(dst.tensor, dst.offset + c, [dst.ap[0], [NCT, 128]])
                return eng.tensor_copy(dst, PS[pk][:, 0:128])
            S.emit("dve", (lambda f=_evac: (lambda eng: [f(eng)]))(),
                   reads=[B_PS[pk]], writes=[B_xnTf, B_xnd], selfwait=True)
        def _router(eng, c=c):
            last = None
            for kd in range(NKD):
                last = eng.matmul(PS[2][:, 0:8], xnTf[:, kd * 128:(kd + 1) * 128],
                                  gw_sb[:, kd * NE:(kd + 1) * NE],
                                  start=(kd == 0), stop=(kd == NKD - 1))
            return last
        S.emit("pe", (lambda f=_router: (lambda eng: [f(eng)]))(),
               reads=[B_xnTf, B_gw], writes=[B_PS[2]], selfwait=True)
        S.emit("dve", (lambda c=c: (lambda eng: [
            eng.tensor_copy(LG[:, c * NE:(c + 1) * NE], PS[2][:, 0:8])]))(),
               reads=[B_PS[2]], writes=[B_LG], selfwait=True)

    def _stmir(eng):
        ins = []
        for kd in range(NKD):
            ins.append(eng.dma_start(out=xn_dmaj_mir[kd * 128:(kd + 1) * 128, :],
                                     in_=xnd[:, kd * T:(kd + 1) * T]))
        return ins
    S.emit("sync", lambda eng: _stmir(eng), reads=[B_xnd], writes=[B_xn_dmaj_mir],
           dma=True, ndma=NKD)

    LG3 = LG[:].rearrange("p (c e) -> p c e", e=NE)
    rt = ExitStack()
    m1, B_m1 = sb("m1", (128, NCT), F32, rt)
    nm1, B_nm1 = sb("nm1", (128, NCT), F32, rt)
    ee, B_ee = sb("ee", (128, NCT * NE), F32, rt)
    ssum, B_ssum = sb("ssum", (128, NCT), F32, rt)
    p1c, B_p1c = sb("p1c", (128, NCT), F32, rt)
    d3, B_d3 = sb("d3", (128, NCT * NE), F32, rt)
    sg, B_sg = sb("sgn", (128, NCT * NE), F32, rt)
    oh, B_oh = sb("oh", (128, NCT * NE), F32, rt)
    cand, B_cand = sb("cand", (128, NCT * NE), F32, rt)
    idx1, B_idx1 = sb("idx1", (128, NCT), F32, rt)
    idx2, B_idx2 = sb("idx2", (128, NCT), F32, rt)
    ohx1, B_ohx1 = sb("ohx1", (128, NCT * NE), F32, rt)
    ohx2, B_ohx2 = sb("ohx2", (128, NCT * NE), F32, rt)
    l2t, B_l2t = sb("l2t", (128, NCT * NE), F32, rt)
    m2, B_m2 = sb("m2", (128, NCT), F32, rt)
    ohseq, B_ohseq = sb("ohseq", (128, NE * 64), F32, rt)
    zer64, B_zer64 = sb("zer64", (128, 64), F32, rt)
    Spe, B_Spe = sb("Spe", (128, NE), F32, rt)
    P0sb, B_P0sb = sb("P0sb", (128, NE), F32, rt)
    posi, B_posi = sb("posi", (128, NE * 64), F32, rt)
    qtmp, B_qtmp = sb("qtmp", (128, NCT * NE), F32, rt)
    pos1, B_pos1 = sb("pos1", (128, NCT), F32, rt)
    pos2, B_pos2 = sb("pos2", (128, NCT), F32, rt)
    keep, B_keep = sb("keep", (128, NCT), F32, rt)
    dslf, B_dslf = sb("dslf", (128, NCT), F32, rt)
    xnsb, B_xnsb = sb("xnsb", (128, DM), BF16, rt)

    d33 = d3[:].rearrange("p (c e) -> p c e", e=NE)
    oh3 = oh[:].rearrange("p (c e) -> p c e", e=NE)
    cand3 = cand[:].rearrange("p (c e) -> p c e", e=NE)
    ohx13 = ohx1[:].rearrange("p (c e) -> p c e", e=NE)
    ohx23 = ohx2[:].rearrange("p (c e) -> p c e", e=NE)
    l23 = l2t[:].rearrange("p (c e) -> p c e", e=NE)
    ee3 = ee[:].rearrange("p (c e) -> p c e", e=NE)
    qt3 = qtmp[:].rearrange("p (c e) -> p c e", e=NE)
    oq2 = ohseq[:].rearrange("p (e s) -> p e s", s=64)
    po2 = posi[:].rearrange("p (e s) -> p e s", s=64)

    S.emit("dve", lambda eng: [eng.reduce_max(m1[:], LG3, axis=AX)],
           reads=[B_LG], writes=[B_m1], selfwait=True)
    S.emit("dve", lambda eng: [eng.reduce_max(nm1[:], LG3, axis=AX, negate=True)],
           reads=[B_LG], writes=[B_nm1], selfwait=True)

    def _subm(eng, src3, mm, B=None):
        last = None
        for e in range(NE):
            last = eng.tensor_sub(d33[:, :, e], src3[:, :, e], mm[:])
        return last
    S.emit("dve", lambda eng: [_subm(eng, LG3, m1)], reads=[B_LG, B_m1], writes=[B_d3])
    S.emit("act", lambda eng: [eng.activation(out=ee[:], in_=d3[:], func=AF.Exp)],
           reads=[B_d3], writes=[B_ee])
    S.emit("dve", lambda eng: [eng.reduce_sum(ssum[:], ee3, axis=AX)],
           reads=[B_ee], writes=[B_ssum], selfwait=True)
    S.emit("dve", lambda eng: [eng.reciprocal(p1c[:], ssum[:])],
           reads=[B_ssum], writes=[B_p1c], selfwait=True)
    S.emit("act", lambda eng: [eng.sign(sg[:], d3[:])], reads=[B_d3], writes=[B_sg])
    S.emit("dve", lambda eng: [eng.tensor_scalar(out=oh[:], in0=sg[:], scalar1=1.0,
                                                 scalar2=None, op0=ALU.add)],
           reads=[B_sg], writes=[B_oh], selfwait=True)

    def _cand(eng):
        last = None
        for e in range(NE):
            last = eng.tensor_scalar(out=cand3[:, :, e], in0=oh3[:, :, e],
                                     scalar1=float(e - 99), scalar2=99.0,
                                     op0=ALU.mult, op1=ALU.add)
        return last
    S.emit("dve", lambda eng: [_cand(eng)], reads=[B_oh], writes=[B_cand], selfwait=True)
    S.emit("dve", lambda eng: [eng.tensor_reduce(idx1[:], cand3, axis=AX, op=ALU.min)],
           reads=[B_cand], writes=[B_idx1], selfwait=True)

    def _ohx(eng, idx, dst3):
        last = None
        for e in range(NE):
            last = eng.tensor_scalar(out=dst3[:, :, e], in0=idx[:], scalar1=float(-e),
                                     scalar2=None, op0=ALU.add)
        return last
    S.emit("dve", lambda eng: [_ohx(eng, idx1, ohx13)], reads=[B_idx1], writes=[B_ohx1], selfwait=True)
    S.emit("act", lambda eng: [eng.square(ohx1[:], ohx1[:])], reads=[B_ohx1], writes=[B_ohx1])
    S.emit("dve", lambda eng: [eng.tensor_single_scalar(ohx1[:], ohx1[:], 0.5, ALU.is_lt)],
           reads=[B_ohx1], writes=[B_ohx1], selfwait=True)
    S.emit("dve", lambda eng: [eng.scalar_tensor_tensor(out=l2t[:], in0=ohx1[:],
                                                        scalar=-1e30, in1=LG[:],
                                                        op0=ALU.mult, op1=ALU.add)],
           reads=[B_ohx1, B_LG], writes=[B_l2t], selfwait=True)
    S.emit("dve", lambda eng: [eng.reduce_max(m2[:], l23, axis=AX)],
           reads=[B_l2t], writes=[B_m2], selfwait=True)
    S.emit("dve", lambda eng: [_subm(eng, l23, m2)], reads=[B_l2t, B_m2], writes=[B_d3])
    S.emit("act", lambda eng: [eng.sign(sg[:], d3[:])], reads=[B_d3], writes=[B_sg])
    S.emit("dve", lambda eng: [eng.tensor_scalar(out=oh[:], in0=sg[:], scalar1=1.0,
                                                 scalar2=None, op0=ALU.add)],
           reads=[B_sg], writes=[B_oh], selfwait=True)
    S.emit("dve", lambda eng: [_cand(eng)], reads=[B_oh], writes=[B_cand], selfwait=True)
    S.emit("dve", lambda eng: [eng.tensor_reduce(idx2[:], cand3, axis=AX, op=ALU.min)],
           reads=[B_cand], writes=[B_idx2], selfwait=True)
    S.emit("dve", lambda eng: [_ohx(eng, idx2, ohx23)], reads=[B_idx2], writes=[B_ohx2], selfwait=True)
    S.emit("act", lambda eng: [eng.square(ohx2[:], ohx2[:])], reads=[B_ohx2], writes=[B_ohx2])
    S.emit("dve", lambda eng: [eng.tensor_single_scalar(ohx2[:], ohx2[:], 0.5, ALU.is_lt)],
           reads=[B_ohx2], writes=[B_ohx2], selfwait=True)
    S.emit("dve", lambda eng: [eng.tensor_sub(m2[:], m2[:], m1[:])],
           reads=[B_m1], writes=[B_m2], selfwait=True)
    S.emit("act", lambda eng: [eng.activation(out=m2[:], in_=m2[:], func=AF.Exp)],
           reads=[B_m2], writes=[B_m2])
    S.emit("dve", lambda eng: [eng.tensor_mul(m2[:], m2[:], p1c[:])],
           reads=[B_p1c], writes=[B_m2], selfwait=True)
    def _ilv(eng):
        last = None
        for e in range(NE):
            src1 = ohx13[:, :, e]
            src2 = ohx23[:, :, e]
            d1 = bass.AP(oq2.tensor, oq2.offset + e * 64, [oq2.ap[0], [2, 32]])
            d2 = bass.AP(oq2.tensor, oq2.offset + e * 64 + 1, [oq2.ap[0], [2, 32]])
            eng.tensor_copy(d1, src1)
            last = eng.tensor_copy(d2, src2)
        return last
    S.emit("dve", lambda eng: [_ilv(eng)], reads=[B_ohx1, B_ohx2], writes=[B_ohseq], selfwait=True)
    S.emit("dve", lambda eng: [eng.memset(zer64[:], 0.0)], writes=[B_zer64], selfwait=True)
    S.emit("dve", lambda eng: [eng.reduce_sum(Spe[:], oq2, axis=AX)],
           reads=[B_ohseq], writes=[B_Spe], selfwait=True)
    S.emit("pe", lambda eng: [eng.matmul(PS[2][:, 0:8], LT_sb[:], Spe[:],
                                         start=True, stop=True)],
           reads=[B_LT, B_Spe], writes=[B_PS[2]])
    S.emit("dve", lambda eng: [eng.tensor_copy(P0sb[:], PS[2][:, 0:8])],
           reads=[B_PS[2]], writes=[B_P0sb], selfwait=True)
    for e in range(NE):
        S.emit("dve", (lambda e=e: (lambda eng: [
            eng.tensor_tensor_scan(out=po2[:, e], data0=oq2[:, e], data1=zer64[:],
                                   initial=P0sb[:, e:e + 1], op0=ALU.add, op1=ALU.add)
        ]))(), reads=[B_ohseq, B_zer64, B_P0sb], writes=[B_posi], selfwait=True)

    def _posk(eng, which, dst):
        last = None
        for e in range(NE):
            srcpos = bass.AP(po2.tensor, po2.offset + e * 64 + which, [po2.ap[0], [2, 32]])
            src_oh = (ohx13 if which == 0 else ohx23)[:, :, e]
            last = eng.tensor_mul(qt3[:, :, e], src_oh, srcpos)
        return last
    S.emit("dve", lambda eng: [_posk(eng, 0, pos1)], reads=[B_posi, B_ohx1], writes=[B_qtmp], selfwait=True)
    S.emit("dve", lambda eng: [eng.reduce_sum(pos1[:], qt3, axis=AX)],
           reads=[B_qtmp], writes=[B_pos1], selfwait=True)
    S.emit("dve", lambda eng: [_posk(eng, 1, pos2)], reads=[B_posi, B_ohx2], writes=[B_qtmp], selfwait=True)
    S.emit("dve", lambda eng: [eng.reduce_sum(pos2[:], qt3, axis=AX)],
           reads=[B_qtmp], writes=[B_pos2], selfwait=True)

    def _mkdsl(idxk, posk, pk_src, B_idxk, B_posk, B_pk_src, dsl_i, pm_dst, B_dsl_i, B_pm):
        S.emit("dve", lambda eng: [eng.tensor_single_scalar(keep[:], posk[:],
                                                            float(CAP) + 0.5, ALU.is_lt)],
               reads=[B_posk], writes=[B_keep], selfwait=True)
        S.emit("dve", lambda eng: [eng.tensor_mul(pm_dst[:], pk_src[:], keep[:])],
               reads=[B_pk_src, B_keep], writes=[B_pm], selfwait=True)
        S.emit("dve", lambda eng: [eng.tensor_scalar(out=dslf[:], in0=idxk[:],
                                                     scalar1=float(CAP), scalar2=-1.0,
                                                     op0=ALU.mult, op1=ALU.add)],
               reads=[B_idxk], writes=[B_dslf], selfwait=True)
        S.emit("dve", lambda eng: [eng.tensor_add(dslf[:], dslf[:], posk[:])],
               reads=[B_posk], writes=[B_dslf], selfwait=True)
        S.emit("dve", lambda eng: [eng.tensor_mul(dslf[:], dslf[:], keep[:])],
               reads=[B_keep], writes=[B_dslf], selfwait=True)
        S.emit("dve", lambda eng: [eng.tensor_scalar(out=qtmp[:, 0:NCT], in0=keep[:],
                                                     scalar1=-2e6, scalar2=2e6,
                                                     op0=ALU.mult, op1=ALU.add)],
               reads=[B_keep], writes=[B_qtmp], selfwait=True)
        S.emit("dve", lambda eng: [eng.tensor_add(dslf[:], dslf[:], qtmp[:, 0:NCT])],
               reads=[B_qtmp], writes=[B_dslf], selfwait=True)
        S.emit("dve", lambda eng: [eng.tensor_copy(dsl_i[:], dslf[:])],
               reads=[B_dslf], writes=[B_dsl_i], selfwait=True)

    _mkdsl(idx1, pos1, p1c, B_idx1, B_pos1, B_p1c, dsl1_sb, pm1_sb, B_dsl1, B_pm1)
    _mkdsl(idx2, pos2, m2, B_idx2, B_pos2, B_m2, dsl2_sb, pm2_sb, B_dsl2, B_pm2)

    if DEBUG_TAPS:
        def _dsltap(eng):
            return [eng.dma_start(out=dsl_tap[:, 0:32], in_=dsl1_sb[:]),
                    eng.dma_start(out=dsl_tap[:, 32:64], in_=dsl2_sb[:]),
                    eng.dma_start(out=pm_tap[:, 0:32], in_=pm1_sb[:]),
                    eng.dma_start(out=pm_tap[:, 32:64], in_=pm2_sb[:])]
        S.emit("sync", _dsltap, reads=[B_dsl1, B_dsl2, B_pm1, B_pm2],
               writes=[B_dsl_tap, B_pm_tap], dma=True, ndma=4)

    for c in range(NCT):
        mrows = xn_tok_mir.rearrange("(p c) d -> c p d", c=NCT)[c]
        S.emit("sync", (lambda mr=mrows: (lambda eng: [eng.dma_start(out=xnsb[:], in_=mr)]))(),
               reads=[B_xn_tok_mir], writes=[B_xnsb], dma=True)
        for (dsl_i, B_dsl_i) in ((dsl1_sb, B_dsl1), (dsl2_sb, B_dsl2)):
            S.emit("pool", (lambda c=c, dsl_i=dsl_i: (lambda eng: [
                eng.indirect_dma_start(
                    out=disp_dram, out_offset=IndirectOffsetOnAxis(ap=dsl_i[:, c:c + 1], axis=0),
                    in_=xnsb[:], in_offset=None,
                    bounds_check=BCREG["r"], oob_is_err=False)
            ]))(), reads=[B_xnsb, B_dsl_i], writes=[B_disp_dram], dma=True)

    rt.close()
    p1.close()
    p0.close()
    S.barrier()

    p2 = ExitStack()
    xnch, B_xnch = sb("xnch", (128, NKD * TC), BF16, p2)
    wslab = []
    B_wslab = []
    for i in range(2):
        t, b = sb("wslab%d" % i, (128, 8 * 128), BF16, p2)
        wslab.append(t)
        B_wslab.append(b)
    xc, B_xc = sb("xc", (128, NJ * TC), BF16, p2)
    Ball, B_Ball = sb("Ball", (128, DS * TC), BF16, p2)
    Call, B_Call = sb("Call", (128, DS * TC), BF16, p2)
    dtlow, B_dtlow = sb("dtlow", (64, TC), BF16, p2)
    Brow, B_Brow = sb("Brow", (16, TC), BF16, p2)
    Crow, B_Crow = sb("Crow", (16, TC), BF16, p2)
    xin = []
    B_xin = []
    for i in range(2):
        t, b = sb("xin%d" % i, (128, TC + 3), BF16, p2)
        xin.append(t)
        B_xin.append(b)
    SCRB = []
    B_SCRB = []
    for i in range(2):
        t, b = sb("scrb%d" % i, (128, TC), BF16, p2)
        SCRB.append(t)
        B_SCRB.append(b)
    dtj = []
    B_dtj = []
    silz = []
    B_silz = []
    for i in range(2):
        t, b = sb("dtj%d" % i, (128, TC), BF16, p2)
        dtj.append(t)
        B_dtj.append(b)
        t, b = sb("silz%d" % i, (128, TC), BF16, p2)
        silz.append(t)
        B_silz.append(b)
    av = []
    B_av = []
    wv = []
    B_wv = []
    uv = []
    B_uv = []
    hv = []
    B_hv = []
    pv = []
    B_pv = []
    dg = []
    B_dg = []
    for i in range(1):
        t, b = sb("wv%d" % i, (128, TC), BF16, p2)
        wv.append(t)
        B_wv.append(b)
    for i in range(2):
        t, b = sb("hv%d" % i, (128, TC), BF16, p2)
        hv.append(t)
        B_hv.append(b)
    for i in range(2):
        t, b = sb("av%d" % i, (128, TC), F32, p2)
        av.append(t)
        B_av.append(b)
    for i in range(4):
        t, b = sb("pv%d" % i, (128, TC), BF16, p2)
        pv.append(t)
        B_pv.append(b)
        t, b = sb("dg%d" % i, (128, 128), BF16, p2)
        dg.append(t)
        B_dg.append(b)

    B_gscol = [Buf("gs%d" % i) for i in range(NJ * DS)]
    S.emit("dve", lambda eng: [eng.memset(gstate[:], 0.0)],
           writes=[B_gstate] + B_gscol)
    S.emit("dve", lambda eng: [eng.memset(xinst[:], 0.0)], writes=[B_xinst])

    dspr = []
    B_dspr = []
    for i in range(2):
        t, b = sb("dspr%d" % i, (128, DM), BF16, p2)
        dspr.append(t)
        B_dspr.append(b)
    dispTcc, B_dispTcc = sb("dispTcc", (128, NKD * 512), BF16, p2)
    w1r = []
    B_w1r = []
    w3r = []
    B_w3r = []
    silcc = []
    B_silcc = []
    h3c = []
    B_h3c = []
    gcc = []
    B_gcc = []
    for i in range(2):
        t, b = sb("w1r%d" % i, (128, 8 * 128), BF16, p2)
        w1r.append(t)
        B_w1r.append(b)
        t, b = sb("w3r%d" % i, (128, 8 * 128), BF16, p2)
        w3r.append(t)
        B_w3r.append(b)
    for i in range(3):
        t, b = sb("silcc%d" % i, (128, 512), BF16, p2)
        silcc.append(t)
        B_silcc.append(b)
        t, b = sb("h3c%d" % i, (128, 512), BF16, p2)
        h3c.append(t)
        B_h3c.append(b)
        t, b = sb("gcc%d" % i, (128, 512), BF16, p2)
        gcc.append(t)
        B_gcc.append(b)

    UH = [0]
    MOEA = []
    for e in range(NE):
        for cc in range(3):
            n0 = cc * 512
            ccw = min(CAP, n0 + 512) - n0
            qlist = list(range(cc * 4, min(NCAP, cc * 4 + 4)))
            for qi, q in enumerate(qlist):
                def trans_unit(e=e, q=q, qi=qi, ccw=ccw):
                    dq = q % 2
                    S.emit("sync", (lambda e=e, q=q, dq=dq: (lambda eng: [
                        eng.dma_start(out=dspr[dq][:],
                                      in_=disp_dram[e * CAP + q * 128: e * CAP + (q + 1) * 128, :])]))(),
                           reads=[B_disp_dram], writes=[B_dspr[dq]], dma=True)
                    for half in range(2):
                        def _tr(eng, dq=dq, half=half):
                            last = None
                            for k in range(4):
                                kd = 4 * half + k
                                last = eng.transpose(PSB[:, k * 128:(k + 1) * 128],
                                                     dspr[dq][:, kd * 128:(kd + 1) * 128],
                                                     identb[:])
                            return last
                        S.emit("pe", _tr, reads=[B_dspr[dq], B_identb], writes=[B_PSB])
                        def _trev(eng, half=half, qi=qi, ccw=ccw):
                            last = None
                            for k in range(4):
                                kd = 4 * half + k
                                last = eng.copy(
                                    dispTcc[:, kd * ccw + qi * 128: kd * ccw + qi * 128 + 128],
                                    PSB[:, k * 128:(k + 1) * 128])
                            return last
                        S.emit("act", _trev, reads=[B_PSB], writes=[B_dispTcc])
                    return None
                MOEA.append(trans_unit)
            for ft in range(NF):
                def h13_unit(e=e, cc=cc, ft=ft, n0=n0, ccw=ccw):
                    wr = UH[0] % 2
                    ur = UH[0] % 3
                    UH[0] += 1
                    S.emit("sync", (lambda e=e, ft=ft, wr=wr: (lambda eng: [
                        eng.dma_start(out=w1r[wr][:], in_=w1_dev[e, ft]),
                        eng.dma_start(out=w3r[wr][:], in_=w3_dev[e, ft])]))(),
                           writes=[B_w1r[wr], B_w3r[wr]], dma=True, ndma=2)
                    def _h1(eng, wr=wr, ccw=ccw):
                        last = None
                        for kd in range(NKD):
                            last = eng.matmul(PS[5][:, 0:ccw],
                                              w1r[wr][:, kd * 128:(kd + 1) * 128],
                                              dispTcc[:, kd * ccw: kd * ccw + ccw],
                                              start=(kd == 0), stop=(kd == NKD - 1))
                        return last
                    S.emit("pe", _h1, reads=[B_w1r[wr], B_dispTcc], writes=[B_PS[5]])
                    S.emit("act", (lambda ccw=ccw, ur=ur: (lambda eng: [
                        eng.activation(out=silcc[ur][:, 0:ccw], in_=PS[5][:, 0:ccw],
                                       func=AF.Silu)]))(),
                           reads=[B_PS[5]], writes=[B_silcc[ur]])
                    def _h3(eng, wr=wr, ccw=ccw):
                        last = None
                        for kd in range(NKD):
                            last = eng.matmul(PS[6][:, 0:ccw],
                                              w3r[wr][:, kd * 128:(kd + 1) * 128],
                                              dispTcc[:, kd * ccw: kd * ccw + ccw],
                                              start=(kd == 0), stop=(kd == NKD - 1))
                        return last
                    S.emit("pe", _h3, reads=[B_w3r[wr], B_dispTcc], writes=[B_PS[6]])
                    S.emit("act", (lambda ccw=ccw, ur=ur: (lambda eng: [
                        eng.copy(h3c[ur][:, 0:ccw], PS[6][:, 0:ccw])]))(),
                           reads=[B_PS[6]], writes=[B_h3c[ur]])
                    def tail(e=e, ft=ft, n0=n0, ccw=ccw, ur=ur):
                        S.emit("dve", (lambda ccw=ccw, ur=ur: (lambda eng: [
                            eng.tensor_mul(gcc[ur][:, 0:ccw], h3c[ur][:, 0:ccw],
                                           silcc[ur][:, 0:ccw])]))(),
                               reads=[B_h3c[ur], B_silcc[ur]], writes=[B_gcc[ur]])
                        S.emit("pool", (lambda e=e, ft=ft, n0=n0, ccw=ccw, ur=ur: (lambda eng: [
                            eng.dma_start(out=g_dram[e, ft, :, n0:n0 + ccw],
                                          in_=gcc[ur][:, 0:ccw])]))(),
                               reads=[B_gcc[ur]], writes=[B_g_dram], dma=True)
                    return tail
                MOEA.append(h13_unit)
    MOEA_POS = [0]
    PENDING_TAILS = []

    def drain_pe(k):
        while k > 0 and MOEA_POS[0] < len(MOEA):
            t = MOEA[MOEA_POS[0]]()
            if t is not None:
                PENDING_TAILS.append(t)
            MOEA_POS[0] += 1
            k -= 1

    def flush_tails():
        for t in PENDING_TAILS:
            t()
        PENDING_TAILS.clear()

    WCTR = [0]
    for ch in range(NCH):
        c0 = ch * TC
        def _ldxn(eng, c0=c0):
            ins = []
            for kd in range(NKD):
                ins.append(eng.dma_start(out=xnch[:, kd * TC:(kd + 1) * TC],
                                         in_=xn_dmaj_mir[kd * 128:(kd + 1) * 128, c0:c0 + TC]))
            return ins
        S.emit("sync", _ldxn, reads=[B_xn_dmaj_mir], writes=[B_xnch], dma=True, ndma=NKD)

        for J in range(NJ):
            slab = WCTR[0] % 2
            WCTR[0] += 1
            S.emit("sync", (lambda J=J, slab=slab: (lambda eng: [
                eng.dma_start(out=wslab[slab][:], in_=w_inproj[J])]))(),
                   writes=[B_wslab[slab]], dma=True)
            pj = J % 2
            def _mm(eng, J=J, slab=slab, pj=pj):
                last = None
                for kd in range(NKD):
                    for nn in range(TC // 512):
                        last = eng.matmul(PS[2 * pj + nn][:],
                                          wslab[slab][:, kd * 128:(kd + 1) * 128],
                                          xnch[:].rearrange("p (k t) -> p k t", k=NKD)[:, kd, nn * 512:(nn + 1) * 512],
                                          start=(kd == 0), stop=(kd == NKD - 1))
                return last
            S.emit("pe", _mm, reads=[B_wslab[slab], B_xnch],
                   writes=[B_PS[2 * pj], B_PS[2 * pj + 1]])
            xr = J % 2
            def _evx(eng, J=J, pj=pj, xr=xr):
                eng.add(xin[xr][:, 3:3 + 512], PS[2 * pj][:], bin_sb[:, J:J + 1])
                return eng.add(xin[xr][:, 3 + 512:3 + 1024], PS[2 * pj + 1][:], bin_sb[:, J:J + 1])
            S.emit("act", _evx, reads=[B_PS[2 * pj], B_PS[2 * pj + 1], B_bin],
                   writes=[B_xin[xr]])
            S.emit("dve", (lambda J=J, xr=xr: (lambda eng: [
                eng.tensor_copy(xin[xr][:, 0:3], xinst[:, J * 4:J * 4 + 3])]))(),
                   reads=[B_xinst], writes=[B_xin[xr]])
            def _cprod(eng, J=J, xr=xr):
                eng.mul(SCRB[0][:], xin[xr][:, 0:TC], cw_sb[:, J * 4:J * 4 + 1])
                return eng.mul(SCRB[1][:], xin[xr][:, 1:1 + TC], cw_sb[:, J * 4 + 1:J * 4 + 2])
            S.emit("act", _cprod, reads=[B_xin[xr], B_cw],
                   writes=[B_SCRB[0], B_SCRB[1]])
            S.emit("dve", lambda eng: [eng.tensor_add(SCRB[0][:], SCRB[0][:], SCRB[1][:])],
                   reads=[B_SCRB[1]], writes=[B_SCRB[0]])
            S.emit("act", (lambda J=J, xr=xr: (lambda eng: [
                eng.mul(SCRB[1][:], xin[xr][:, 2:2 + TC], cw_sb[:, J * 4 + 2:J * 4 + 3])]))(),
                   reads=[B_xin[xr], B_cw], writes=[B_SCRB[1]])
            S.emit("dve", lambda eng: [eng.tensor_add(SCRB[0][:], SCRB[0][:], SCRB[1][:])],
                   reads=[B_SCRB[1]], writes=[B_SCRB[0]])
            S.emit("act", (lambda J=J, xr=xr: (lambda eng: [
                eng.mul(SCRB[1][:], xin[xr][:, 3:3 + TC], cw_sb[:, J * 4 + 3:J * 4 + 4])]))(),
                   reads=[B_xin[xr], B_cw], writes=[B_SCRB[1]])
            S.emit("dve", lambda eng: [eng.tensor_add(SCRB[0][:], SCRB[0][:], SCRB[1][:])],
                   reads=[B_SCRB[1]], writes=[B_SCRB[0]])
            S.emit("act", (lambda J=J: (lambda eng: [
                eng.activation(out=xc[:, J * TC:(J + 1) * TC], in_=SCRB[0][:], func=AF.Silu,
                               bias=cb_sb[:, J:J + 1])]))(),
                   reads=[B_SCRB[0], B_cb], writes=[B_xc])
            S.emit("dve", (lambda J=J, xr=xr: (lambda eng: [
                eng.tensor_copy(xinst[:, J * 4:J * 4 + 3], xin[xr][:, TC:TC + 3])]))(),
                   reads=[B_xin[xr]], writes=[B_xinst])
            drain_pe(1)
            flush_tails()

        def _xproj(eng):
            last = None
            for kk in range(16):
                for nn in range(TC // 512):
                    last = eng.matmul(PS[nn][0:80, :],
                                      wxp1_sb[:, kk * 80:(kk + 1) * 80],
                                      xc[:].rearrange("p (j t) -> p j t", j=NJ)[:, kk, nn * 512:(nn + 1) * 512],
                                      start=(kk == 0), stop=(kk == 15))
            return last
        S.emit("pe", _xproj, reads=[B_wxp1, B_xc], writes=[B_PS[0], B_PS[1]])
        def _evxp(eng):
            eng.copy(dtlow[:, 0:512], PS[0][0:64, :])
            eng.copy(dtlow[:, 512:1024], PS[1][0:64, :])
            eng.copy(Brow[:, 0:512], PS[0][64:80, :])
            return eng.copy(Brow[:, 512:1024], PS[1][64:80, :])
        S.emit("act", _evxp, reads=[B_PS[0], B_PS[1]], writes=[B_dtlow, B_Brow])
        def _xprojC(eng):
            last = None
            for kk in range(16):
                for nn in range(TC // 512):
                    last = eng.matmul(PS[2 + nn][0:16, :],
                                      wxpC_sb[:, kk * 16:(kk + 1) * 16],
                                      xc[:].rearrange("p (j t) -> p j t", j=NJ)[:, kk, nn * 512:(nn + 1) * 512],
                                      start=(kk == 0), stop=(kk == 15))
            return last
        S.emit("pe", _xprojC, reads=[B_wxpC, B_xc], writes=[B_PS[2], B_PS[3]])
        def _evC(eng):
            eng.copy(Crow[:, 0:512], PS[2][0:16, :])
            return eng.copy(Crow[:, 512:1024], PS[3][0:16, :])
        S.emit("act", _evC, reads=[B_PS[2], B_PS[3]], writes=[B_Crow])

        for s in range(DS):
            for (row, Ba, dstall, B_dstall) in ((Brow, B_Brow, Ball, B_Ball),
                                                (Crow, B_Crow, Call, B_Call)):
                def _bc(eng, s=s, row=row):
                    last = None
                    for nn in range(TC // 512):
                        last = eng.matmul(PS[2 + nn][:], BSEL_sb[:, s * 128:(s + 1) * 128],
                                          row[:, nn * 512:(nn + 1) * 512],
                                          start=True, stop=True)
                    return last
                S.emit("pe", _bc, reads=[Ba, B_BSEL], writes=[B_PS[2], B_PS[3]])
                def _bce(eng, s=s, dstall=dstall):
                    eng.copy(dstall[:, s * TC:s * TC + 512], PS[2][:])
                    return eng.copy(dstall[:, s * TC + 512:(s + 1) * TC], PS[3][:])
                S.emit("act", _bce, reads=[B_PS[2], B_PS[3]], writes=[B_dstall])

        def prep_J(J):
            pj = J % 2
            slab = WCTR[0] % 2
            WCTR[0] += 1
            S.emit("sync", (lambda J=J, slab=slab: (lambda eng: [
                eng.dma_start(out=wslab[slab][:], in_=w_inproj[16 + J])]))(),
                   writes=[B_wslab[slab]], dma=True)
            def _zmm(eng, J=J, slab=slab, pj=pj):
                last = None
                for kd in range(NKD):
                    for nn in range(TC // 512):
                        last = eng.matmul(PS[2 * pj + nn][:],
                                          wslab[slab][:, kd * 128:(kd + 1) * 128],
                                          xnch[:].rearrange("p (k t) -> p k t", k=NKD)[:, kd, nn * 512:(nn + 1) * 512],
                                          start=(kd == 0), stop=(kd == NKD - 1))
                return last
            S.emit("pe", _zmm, reads=[B_wslab[slab], B_xnch],
                   writes=[B_PS[2 * pj], B_PS[2 * pj + 1]])
            for nn in range(TC // 512):
                S.emit("pe", (lambda J=J, nn=nn: (lambda eng: [
                    eng.matmul(PS[4 + nn][:], wdt_sb[:, J * 128:(J + 1) * 128],
                               dtlow[:, nn * 512:(nn + 1) * 512],
                               start=True, stop=True)]))(),
                       reads=[B_wdt, B_dtlow], writes=[B_PS[4 + nn]])
            for nn in range(TC // 512):
                S.emit("act", (lambda J=J, nn=nn: (lambda eng: [
                    eng.activation(out=SCRB[1][:, nn * 512:(nn + 1) * 512], in_=PS[4 + nn][:],
                                   func=AF.Exp, bias=bdt_sb[:, J:J + 1])]))(),
                       reads=[B_PS[4 + nn], B_bdt], writes=[B_SCRB[1]])
            S.emit("act", (lambda pj=pj: (lambda eng: [
                eng.activation(out=dtj[pj][:], in_=SCRB[1][:], func=AF.Ln, bias=1.0)]))(),
                   reads=[B_SCRB[1]], writes=[B_dtj[pj]])
            def _zev(eng, J=J, pj=pj):
                eng.activation(out=silz[pj][:, 0:512], in_=PS[2 * pj][:], func=AF.Silu,
                               bias=bin_sb[:, 16 + J:17 + J])
                return eng.activation(out=silz[pj][:, 512:1024], in_=PS[2 * pj + 1][:],
                                      func=AF.Silu, bias=bin_sb[:, 16 + J:17 + J])
            S.emit("act", _zev, reads=[B_PS[2 * pj], B_PS[2 * pj + 1], B_bin],
                   writes=[B_silz[pj]])

        def s_body(J, s):
            pj = J % 2
            sr = s % 2
            s4 = s % 4
            col = J * DS + s
            S.emit("dve", (lambda J=J, s=s: (lambda eng: [
                eng.tensor_mul(wv[0][:], xc[:, J * TC:(J + 1) * TC],
                               Ball[:, s * TC:(s + 1) * TC])]))(),
                   reads=[B_xc, B_Ball], writes=[B_wv[0]])
            s2 = s % 2
            S.emit("dve", (lambda s2=s2: (lambda eng: [
                eng.scalar_tensor_tensor(out=wv[0][:], in0=av[s2][:], scalar=-1.0,
                                         in1=wv[0][:], op0=ALU.add, op1=ALU.mult)]))(),
                   reads=[B_av[s2]], writes=[B_wv[0]])
            S.emit("dve", (lambda s2=s2, col=col: (lambda eng: [
                eng.tensor_tensor_scan(out=hv[0][:], data0=av[s2][:], data1=wv[0][:],
                                       initial=gstate[:, col:col + 1],
                                       op0=ALU.mult, op1=ALU.add)]))(),
                   reads=[B_av[s2], B_gscol[col]], writes=[B_hv[0]],
                   selfwait=True)
            S.emit("dve", (lambda col=col: (lambda eng: [
                eng.tensor_copy(gstate[:, col:col + 1], hv[0][:, TC - 1:TC])]))(),
                   reads=[B_hv[0]], writes=[B_gscol[col]])
            S.emit("dve", (lambda s=s, s4=s4: (lambda eng: [
                eng.tensor_mul(pv[s4][:], hv[0][:], Call[:, s * TC:(s + 1) * TC])]))(),
                   reads=[B_hv[0], B_Call], writes=[B_pv[s4]])
            def _yred(eng, s4=s4, s=s, pj=pj):
                last = None
                for nn in range(TC // 512):
                    last = eng.matmul(PS[2 * pj + nn][:], dg[s4][:],
                                      pv[s4][:, nn * 512:(nn + 1) * 512],
                                      start=(s == 0), stop=(s == DS - 1))
                return last
            S.emit("pe", _yred, reads=[B_dg[s4], B_pv[s4]],
                   writes=[B_PS[2 * pj], B_PS[2 * pj + 1]])

        prep_J(0)
        for J in range(NJ):
            pj = J % 2
            for g in range(4):
                for s in range(4 * g, 4 * g + 4):
                    s2 = s % 2
                    s4 = s % 4
                    col = J * DS + s
                    S.emit("act", (lambda pj=pj, s2=s2, col=col: (lambda eng: [
                        eng.activation(out=av[s2][:], in_=dtj[pj][:], func=AF.Exp,
                                       scale=A_sb[:, col:col + 1])]))(),
                           reads=[B_dtj[pj], B_A], writes=[B_av[s2]])
                    S.emit("act", (lambda s4=s4, col=col: (lambda eng: [
                        eng.mul(dg[s4][:], identb[:], invA_sb[:, col:col + 1])]))(),
                           reads=[B_identb, B_invA], writes=[B_dg[s4]])
                if g == 0 and J + 1 < NJ:
                    prep_J(J + 1)
                drain_pe(2 if g == 3 else 3)
                for s in range(4 * g, 4 * g + 4):
                    s_body(J, s)
                flush_tails()

            def _yev(eng, J=J, pj=pj):
                eng.scalar_tensor_tensor(out=SCRB[0][:, 0:512], in0=xc[:, J * TC:J * TC + 512],
                                         scalar=Dc_sb[:, J:J + 1], in1=PS[2 * pj][:],
                                         op0=ALU.mult, op1=ALU.add)
                eng.scalar_tensor_tensor(out=SCRB[0][:, 512:1024],
                                         in0=xc[:, J * TC + 512:(J + 1) * TC],
                                         scalar=Dc_sb[:, J:J + 1], in1=PS[2 * pj + 1][:],
                                         op0=ALU.mult, op1=ALU.add)
                return eng.tensor_mul(SCRB[0][:], SCRB[0][:], silz[pj][:])
            S.emit("dve", _yev, reads=[B_PS[2 * pj], B_PS[2 * pj + 1], B_xc, B_Dc,
                                       B_silz[pj]], writes=[B_SCRB[0]])
            S.emit("sync", (lambda ch=ch, J=J: (lambda eng: [
                eng.dma_start(out=yg_mir[ch, J], in_=SCRB[0][:])]))(),
                   reads=[B_SCRB[0]], writes=[B_yg_mir], dma=True)

    drain_pe(10 ** 9)
    flush_tails()
    p2.close()
    S.barrier()

    p2b = ExitStack()
    yacF, B_yacF = sb("yacF", (128, (TC // 128) * DM), BF16, p2b)
    wop, B_wop = sb("wop", (128, 8 * NJ * 128), BF16, p2b)
    hv2 = []
    B_hv2 = []
    for i in range(3):
        t, b = sb("hv2_%d" % i, (128, 512), BF16, p2b)
        hv2.append(t)
        B_hv2.append(b)
    mo2b, B_mo2b = sb("mo2b", (128, 512), BF16, p2b)
    NTB = TC // 128
    S.emit("sync", lambda eng: [eng.dma_start(out=wop[:, m * 2048:(m + 1) * 2048],
                                              in_=w_outproj[m])
                                for m in range(8)],
           writes=[B_wop], dma=True, ndma=8)
    HCT = [0]
    yac4 = yacF[:].rearrange("p (tb d) -> p tb d", tb=NTB)
    for ch in range(NCH):
        def _ldx(eng, ch=ch):
            return [eng.dma_start(out=yacF[:, tb * DM:(tb + 1) * DM],
                                  in_=x_hbm[ch * TC + tb * 128: ch * TC + (tb + 1) * 128, :])
                    for tb in range(NTB)]
        S.emit("pool", _ldx, writes=[B_yacF], dma=True, ndma=NTB)
        for hf in range(2):
            for mh in range(2):
                for J in range(NJ):
                    r = HCT[0] % 3
                    HCT[0] += 1
                    S.emit("sync", (lambda ch=ch, J=J, hf=hf, r=r: (lambda eng: [
                        eng.dma_start(out=hv2[r][:],
                                      in_=yg_mir[ch, J, :, hf * 512:(hf + 1) * 512])]))(),
                           reads=[B_yg_mir], writes=[B_hv2[r]], dma=True)
                    def _opmm(eng, J=J, r=r, mh=mh):
                        last = None
                        for m in range(4):
                            gm = mh * 4 + m
                            last = eng.matmul(PS[m][:], wop[:, gm * 2048 + J * 128:
                                                            gm * 2048 + (J + 1) * 128],
                                              hv2[r][:], start=(J == 0), stop=(J == NJ - 1))
                        return last
                    S.emit("pe", _opmm, reads=[B_wop, B_hv2[r]],
                           writes=[B_PS[0], B_PS[1], B_PS[2], B_PS[3]])
                for m in range(4):
                    gm = mh * 4 + m
                    S.emit("act", (lambda m=m, gm=gm: (lambda eng: [
                        eng.add(mo2b[:], PS[m][:], bout_sb[:, gm:gm + 1])]))(),
                           reads=[B_PS[m], B_bout], writes=[B_mo2b])
                    def _tr4(eng):
                        last = None
                        for tb4 in range(4):
                            last = eng.transpose(PSB[:, tb4 * 128:(tb4 + 1) * 128],
                                                 mo2b[:, tb4 * 128:(tb4 + 1) * 128],
                                                 identb[:])
                        return last
                    S.emit("pe", _tr4, reads=[B_mo2b, B_identb], writes=[B_PSB])
                    def _acc(eng, hf=hf, gm=gm):
                        dst = yac4[:, 4 * hf:4 * hf + 4, gm * 128:(gm + 1) * 128]
                        src_ = PSB[:].rearrange("p (tb c) -> p tb c", tb=4)
                        return eng.tensor_add(dst, dst, src_)
                    S.emit("dve", _acc, reads=[B_PSB], writes=[B_yacF])
        def _stout(eng, ch=ch):
            return [eng.dma_start(out=out_hbm[ch * TC + tb * 128: ch * TC + (tb + 1) * 128, :],
                                  in_=yacF[:, tb * DM:(tb + 1) * DM])
                    for tb in range(NTB)]
        S.emit("pool", _stout, reads=[B_yacF], writes=[B_out_hbm], dma=True, ndma=NTB)
    p2b.close()
    S.barrier()

    p3 = ExitStack()
    HW = CAP // 2
    gb2h = []
    B_gb2h = []
    for i in range(2):
        t, b = sb("gb2h%d" % i, (128, NF * HW), BF16, p3)
        gb2h.append(t)
        B_gb2h.append(b)
    w2s = []
    B_w2s = []
    for i in range(2):
        t, b = sb("w2s_%d" % i, (128, NF * DM), BF16, p3)
        w2s.append(t)
        B_w2s.append(b)
    eo = []
    B_eo = []
    for i in range(2):
        t, b = sb("eo%d" % i, (128, DM), BF16, p3)
        eo.append(t)
        B_eo.append(b)

    for e in range(NE):
        we = e % 2
        def _ldw2(eng, e=e, we=we):
            return [eng.dma_start(out=w2s[we][:, kf * DM:(kf + 1) * DM], in_=w2_dev[e, kf])
                    for kf in range(NF)]
        S.emit("sync", _ldw2, writes=[B_w2s[we]], dma=True, ndma=NF)
        for h in range(2):
            def _ldg(eng, e=e, h=h):
                return [eng.dma_start(out=gb2h[h][:, kf * HW:(kf + 1) * HW],
                                      in_=g_dram[e, kf, :, h * HW:(h + 1) * HW])
                        for kf in range(NF)]
            S.emit("sync", _ldg, reads=[B_g_dram], writes=[B_gb2h[h]], dma=True, ndma=NF)
        for q in range(NCAP):
            h = q // 5
            qh = q - 5 * h
            def _w2mm(eng, q=q, we=we, h=h, qh=qh):
                last = None
                for kf in range(NF):
                    gs = gb2h[h][:, kf * HW + qh * 128: kf * HW + (qh + 1) * 128]
                    for nn in range(2):
                        last = eng.matmul(PS[2 * (q % 2) + nn][:], gs,
                                          w2s[we][:, kf * DM + nn * 512: kf * DM + (nn + 1) * 512],
                                          start=(kf == 0), stop=(kf == NF - 1))
                return last
            S.emit("pe", _w2mm, reads=[B_gb2h[h], B_w2s[we]],
                   writes=[B_PS[2 * (q % 2)], B_PS[2 * (q % 2) + 1]])
            def _eoev(eng, q=q):
                eng.copy(eo[q % 2][:, 0:512], PS[2 * (q % 2)][:])
                return eng.copy(eo[q % 2][:, 512:1024], PS[2 * (q % 2) + 1][:])
            S.emit("act", _eoev, reads=[B_PS[2 * (q % 2)], B_PS[2 * (q % 2) + 1]],
                   writes=[B_eo[q % 2]])
            S.emit("sync", (lambda e=e, q=q: (lambda eng: [
                eng.dma_start(out=eout_dram[e * CAP + q * 128: e * CAP + (q + 1) * 128, :],
                              in_=eo[q % 2][:])]))(),
                   reads=[B_eo[q % 2]], writes=[B_eout_dram], dma=True)

    p3.close()
    S.barrier()

    p4 = ExitStack()
    cmbA, B_cmbA = sb("cmbA", (128, NCT * DM), BF16, p4)
    cmbB, B_cmbB = sb("cmbB", (128, NCT * DM), BF16, p4)
    acc = []
    B_acc = []
    for i in range(4):
        t, b = sb("acc%d" % i, (128, DM), F32, p4)
        acc.append(t)
        B_acc.append(b)
    S.emit("dve", lambda eng: [eng.memset(cmbA[:], 0.0)], writes=[B_cmbA])
    S.emit("pool", lambda eng: [eng.memset(cmbB[:], 0.0)], writes=[B_cmbB])

    for c in range(NCT):
        for (dsl_i, B_dsl_i, cb_t, B_cb_t) in ((dsl1_sb, B_dsl1, cmbA, B_cmbA),
                                               (dsl2_sb, B_dsl2, cmbB, B_cmbB)):
            S.emit("pool", (lambda c=c, dsl_i=dsl_i, cb_t=cb_t: (lambda eng: [
                eng.indirect_dma_start(
                    out=cb_t[:, c * DM:(c + 1) * DM], out_offset=None,
                    in_=eout_dram, in_offset=IndirectOffsetOnAxis(ap=dsl_i[:, c:c + 1], axis=0),
                    bounds_check=BCREG["r"], oob_is_err=False)
            ]))(), reads=[B_eout_dram, B_dsl_i], writes=[B_cb_t], dma=True)

    for c in range(NCT):
        ia = c % 4
        orows = out_hbm.rearrange("(p c) d -> c p d", c=NCT)[c]
        S.emit("sync", (lambda orows=orows, ia=ia: (lambda eng: [
            eng.dma_start(out=acc[ia][:], in_=orows)]))(),
               reads=[B_out_hbm], writes=[B_acc[ia]], dma=True)
        for (cb_t, B_cb_t, pmv, B_pmv) in ((cmbA, B_cmbA, pm1_sb, B_pm1),
                                           (cmbB, B_cmbB, pm2_sb, B_pm2)):
            S.emit("dve", (lambda c=c, ia=ia, pmv=pmv, cb_t=cb_t: (lambda eng: [
                eng.scalar_tensor_tensor(out=acc[ia][:], in0=cb_t[:, c * DM:(c + 1) * DM],
                                         scalar=pmv[:, c:c + 1], in1=acc[ia][:],
                                         op0=ALU.mult, op1=ALU.add)]))(),
                   reads=[B_cb_t, B_pmv], writes=[B_acc[ia]])
        S.emit("sync", (lambda orows=orows, ia=ia: (lambda eng: [
            eng.dma_start(out=orows, in_=acc[ia][:])]))(),
               reads=[B_acc[ia]], writes=[B_out_hbm], dma=True)

    S.barrier()
    p4.close()

    with nc.Block() as blk:
        def run_stream(eng, name):
            for (waits, body, sem, inc, selfwait, val) in S.streams[name]:
                for (sname, v) in waits:
                    eng.wait_ge(sems[sname], v)
                if body is None:
                    continue
                ins = body(eng)
                if not isinstance(ins, (list, tuple)):
                    ins = [ins]
                if inc >= 16:
                    assert inc == 16 * len(ins), (name, inc, len(ins))
                    for li in ins:
                        li.then_inc(sems[sem], 16)
                else:
                    ins[-1].then_inc(sems[sem], inc)
                if selfwait:
                    eng.wait_ge(sems[sem], val)

        @blk.tensor
        def _(eng):
            run_stream(eng, "pe")

        @blk.vector
        def _(eng):
            run_stream(eng, "dve")

        @blk.scalar
        def _(eng):
            run_stream(eng, "act")

        @blk.gpsimd
        def _(eng):
            run_stream(eng, "pool")

        @blk.sync
        def _(eng):
            run_stream(eng, "sync")

    es.close()
    return nc


def _host_inputs(inp):
    w = {}
    ipw = np.asarray(inp["L0_attn_in_proj_w"], np.float32)
    w["w_inproj"] = trunc_bf16(
        ipw.T.reshape(8, 128, 32, 128).transpose(2, 1, 0, 3).reshape(32, 128, 1024))
    w["b_inproj"] = np.ascontiguousarray(
        np.asarray(inp["L0_attn_in_proj_b"], np.float32).reshape(32, 128).T)
    xp = np.asarray(inp["L0_attn_x_proj_w"], np.float32)
    w["w_xproj1"] = trunc_bf16(
        xp[:80].T.reshape(16, 128, 80).transpose(1, 0, 2).reshape(128, 16 * 80))
    w["w_xprojC"] = trunc_bf16(
        xp[80:96].T.reshape(16, 128, 16).transpose(1, 0, 2).reshape(128, 16 * 16))
    w["w_dtproj"] = trunc_bf16(np.asarray(inp["L0_attn_dt_proj_w"], np.float32).T)
    w["b_dtproj"] = np.ascontiguousarray(
        np.asarray(inp["L0_attn_dt_proj_b"], np.float32).reshape(16, 128).T)
    cw = np.asarray(inp["L0_attn_conv_w"], np.float32)
    w["conv_w4"] = np.ascontiguousarray(
        cw.reshape(16, 128, 4).transpose(1, 0, 2).reshape(128, 64))
    w["conv_bc"] = np.ascontiguousarray(
        np.asarray(inp["L0_attn_conv_b"], np.float32).reshape(16, 128).T)
    al = np.asarray(inp["L0_attn_A_log"], np.float32)
    w["A_logr"] = np.ascontiguousarray(
        al.T.reshape(16, 128, 16).transpose(1, 0, 2).reshape(128, 256))
    w["D_col"] = np.ascontiguousarray(
        np.asarray(inp["L0_attn_D"], np.float32).reshape(16, 128).T)
    opw = np.asarray(inp["L0_attn_out_proj_w"], np.float32)
    w["w_outproj"] = trunc_bf16(
        opw.T.reshape(16, 128, 8, 128).transpose(2, 1, 0, 3).reshape(8, 128, 16 * 128))
    w["b_outproj"] = np.ascontiguousarray(
        np.asarray(inp["L0_attn_out_proj_b"], np.float32).reshape(8, 128).T)
    gwt = np.asarray(inp["L0_ffn_gate_w"], np.float32).T
    w["gate_wT"] = np.ascontiguousarray(
        gwt.reshape(8, 128, 8).transpose(1, 0, 2).reshape(128, 64))
    w1 = np.asarray(inp["L0_ffn_w1"], np.float32)
    w["w1_dev"] = trunc_bf16(
        w1.reshape(8, 8, 128, 28, 128).transpose(0, 3, 2, 1, 4).reshape(8, 28, 128, 1024))
    w3 = np.asarray(inp["L0_ffn_w3"], np.float32)
    w["w3_dev"] = trunc_bf16(
        w3.reshape(8, 8, 128, 28, 128).transpose(0, 3, 2, 1, 4).reshape(8, 28, 128, 1024))
    w2 = np.asarray(inp["L0_ffn_w2"], np.float32)
    w["w2_dev"] = trunc_bf16(w2.reshape(8, 28, 128, 1024))
    w["ident_f"] = np.eye(128, dtype=np.float32)
    w["ident_b"] = trunc_bf16(np.eye(128, dtype=np.float32))
    lt = np.tril(np.ones((128, 128), np.float32), -1).T
    w["LT"] = np.ascontiguousarray(lt)
    bsel = np.zeros((16, 16 * 128), np.float32)
    for s in range(16):
        bsel[s, s * 128:(s + 1) * 128] = 1.0
    w["BSEL"] = trunc_bf16(bsel)
    return w


def kernel(**inputs):
    x = np.asarray(inputs["L0_attn_x"], np.float32)
    shared = _host_inputs(inputs)
    nc = build_program()
    in_maps = []
    for c in range(NCORES):
        m = dict(shared)
        m["x"] = np.ascontiguousarray(x[c])
        in_maps.append(m)
    res = run_bass_kernel_spmd(nc, in_maps, core_ids=list(range(NCORES)))
    out = np.stack([res.results[c]["out"] for c in range(NCORES)], axis=0)
    return out.astype(np.float32)
